# version 10
# baseline (speedup 1.0000x reference)
import numpy as np
import concourse.bass as bass
import concourse.mybir as mybir

F32 = mybir.dt.float32
BF16 = mybir.dt.bfloat16
AF = mybir.ActivationFunctionType
ALU = mybir.AluOpType
AX = mybir.AxisListType

PART = 128


class Sem:
    def __init__(self, handle, name):
        self.h = handle
        self.name = name
        self.n = 0

    def inc(self, k=1):
        self.n += k
        return self


class _OpRec:
    __slots__ = ["kind", "fname", "args", "kwargs", "sem", "inc_by", "fn",
                 "inc_default"]

    def __init__(self, kind, fname=None, args=None, kwargs=None, sem=None,
                 inc_by=1, fn=None, inc_default=False):
        self.kind = kind
        self.fname = fname
        self.args = args
        self.kwargs = kwargs
        self.sem = sem
        self.inc_by = inc_by
        self.fn = fn
        self.inc_default = inc_default


class EngQ:

    def __init__(self, mk, name):
        self.mk = mk
        self.name = name
        self.recs = []
        self._pending_waits = []
        self.self_sem = None
        self.self_dma_sem = None
        self._last_op = None

    def wait(self, sem, val=None):
        if val is None:
            val = sem.n
        if val <= 0:
            return
        self._pending_waits.append((sem, val))

    def _flush_waits(self):
        waits, self._pending_waits = self._pending_waits, []
        best = {}
        for sem, val in waits:
            if sem.name not in best or best[sem.name][1] < val:
                best[sem.name] = (sem, val)
        for sem, val in best.values():
            self.recs.append(_OpRec("wait", sem=sem, inc_by=val))

    def self_edge(self):
        lo = self._last_op
        if lo is None:
            return
        if lo.sem is None:
            if lo.fname == "dma_start":
                sem = self.self_dma_sem
                lo.inc_by = 16
            else:
                sem = self.self_sem
                lo.inc_by = 1
            lo.sem = sem
            sem.inc(lo.inc_by)
        self.wait(lo.sem, lo.sem.n)

    def op(self, fname, *args, inc=None, inc_by=1, chain=False,
           inc_default=False, **kwargs):
        if chain:
            self.self_edge()
        self._flush_waits()
        if inc is not None:
            inc.inc(inc_by)
        rec = _OpRec("op", fname=fname, args=args, kwargs=kwargs, sem=inc,
                     inc_by=inc_by, inc_default=inc_default)
        self.recs.append(rec)
        self._last_op = rec

    def dma(self, out, in_, inc=None, **kw):
        self.op("dma_start", inc=inc, inc_by=16, out=out, in_=in_, **kw)

    def raw(self, fn):
        self.recs.append(_OpRec("raw", fn=fn))

    def flush(self):
        self._flush_waits()

    def replay(self, e):
        for r in self.recs:
            if r.kind == "op":
                if r.sem is not None:
                    if r.inc_default:
                        getattr(e, r.fname)(*r.args, **r.kwargs).then_inc(
                            r.sem.h)
                    else:
                        getattr(e, r.fname)(*r.args, **r.kwargs).then_inc(
                            r.sem.h, r.inc_by)
                else:
                    getattr(e, r.fname)(*r.args, **r.kwargs)
            elif r.kind == "wait":
                e.wait_ge(r.sem.h, r.inc_by)
            else:
                r.fn(e)


SBUF_ARENA_MAX = 200 * 1024
SBUF_TOTAL = 212480


class MK:
    def __init__(self, name="mk", fused=False):
        self.nc = bass.Bass("TRN2", target_bir_lowering=False)
        self.name = name
        self.fused = fused
        self.prefix = ""
        self._ctxs = []
        self.sb = {}
        self.ps = {}
        self.dram = {}
        self.sems = {}
        self.pe = EngQ(self, "pe")
        self.act = EngQ(self, "act")
        self.dve = EngQ(self, "dve")
        self.pool = EngQ(self, "pool")
        self.sp = EngQ(self, "sp")
        for s in ["s_pe", "s_act", "s_dve", "s_pool", "s_wdma", "s_xdma", "s_odma", "s_cc"]:
            self.sems[s] = None
        self._sem_names = list(self.sems)
        self.sbuf_bytes = 0
        if fused:
            self.g_arena = self._real_sbuf("g_arena", [PART, SBUF_TOTAL],
                                           mybir.dt.uint8)
            self.g_psum = [self._real_psum(f"gpb{i}", [PART, 512], F32)
                           for i in range(8)]
            self._bump = 0
            self._psum_n = 0

    def new_block(self, prefix):
        self.prefix = prefix
        self._bump = 0
        self._psum_n = 0

    def din(self, name, shape, dtype=F32):
        name = self.prefix + name
        assert name not in self.dram, name
        ap = self.nc.dram_tensor(name, list(shape), dtype, kind="ExternalInput").ap()
        self.dram[name] = ap
        return ap

    def dout(self, name, shape, dtype=F32):
        name = self.prefix + name
        ap = self.nc.dram_tensor(name, list(shape), dtype, kind="ExternalOutput").ap()
        self.dram[name] = ap
        return ap

    def dtmp(self, name, shape, dtype=F32):
        name = self.prefix + name
        h = self.nc.dram_tensor(name, list(shape), dtype, kind="Internal")
        self.dram[name] = h.ap()
        return self.dram[name]

    def _real_sbuf(self, name, shape, dtype):
        g = self.nc.sbuf_tensor(name, list(shape), dtype)
        t = g.__enter__()
        self._ctxs.append(g)
        self.sb[name] = t
        return t

    def _real_psum(self, name, shape, dtype):
        g = self.nc.psum_tensor(name, list(shape), dtype)
        t = g.__enter__()
        self._ctxs.append(g)
        self.ps[name] = t
        return t

    def sbuf(self, name, shape, dtype=F32):
        if not self.fused:
            return self._real_sbuf(name, shape, dtype)
        if name == "arena":
            n = shape[1]
            assert shape[0] == PART and n <= SBUF_ARENA_MAX
            self._bump = n
            return self.g_arena[:, 0:n]
        esz = 4 if dtype == F32 else (2 if dtype == BF16 else 1)
        nb = 1
        for s_ in shape[1:]:
            nb *= s_
        nb *= esz
        off = (self._bump + 3) & ~3
        self._bump = off + nb
        assert self._bump <= SBUF_TOTAL, (name, self._bump)
        ap = self.g_arena[0:shape[0], off:off + nb].bitcast(dtype)
        if len(shape) == 3:
            ap = ap.rearrange("p (a b) -> p a b", a=shape[1])
        return ap

    def psum(self, name, shape, dtype=F32):
        if not self.fused:
            return self._real_psum(name, shape, dtype)
        fel = 1
        for s_ in shape[1:]:
            fel *= s_
        esz = 4 if dtype == F32 else 2
        assert fel * esz <= 2048, (name, shape, dtype)
        i = self._psum_n
        self._psum_n += 1
        assert i < 8, name
        if dtype == F32:
            ap = self.g_psum[i][0:shape[0], 0:fel]
        else:
            ap = self.g_psum[i][0:shape[0], 0:(fel * esz) // 4].bitcast(dtype)
        if len(shape) == 3:
            ap = ap.rearrange("p (a b) -> p a b", a=shape[1])
        return ap

    def sem(self, name):
        if self.fused and self.sems.get(name) is not None:
            return self.sems[name]
        g = self.nc.semaphore(name)
        h = g.__enter__()
        self._ctxs.append(g)
        s = Sem(h, name)
        self.sems[name] = s
        return s

    def std_sems(self, extra=()):
        for s in ["s_pe", "s_act", "s_dve", "s_pool", "s_sp",
                  "s_bpd", "s_bsd",
                  "s_wdma", "s_xdma", "s_odma", *extra]:
            if self.sems.get(s) is None:
                self.sem(s)
        self.pe.self_sem = self.sems["s_pe"]
        self.act.self_sem = self.sems["s_act"]
        self.dve.self_sem = self.sems["s_dve"]
        self.pool.self_sem = self.sems["s_pool"]
        self.sp.self_sem = self.sems["s_sp"]
        self.pool.self_dma_sem = self.sems["s_bpd"]
        self.sp.self_dma_sem = self.sems["s_bsd"]
        return self.sems

    def barrier(self):
        qs = [self.pe, self.act, self.dve, self.pool, self.sp]
        for q in qs:
            q.self_edge()
        sems = [s for s in self.sems.values() if isinstance(s, Sem)]
        for q in qs:
            for s in sems:
                q.wait(s, s.n)
            q.flush()

    def setup_sems(self):
        for n in self._sem_names:
            self.sem(n)

    def finalize(self):
        blk_g = self.nc.Block()
        block = blk_g.__enter__()
        self._ctxs.append(blk_g)
        pe_q, act_q, dve_q, pool_q, sp_q = (
            self.pe, self.act, self.dve, self.pool, self.sp)

        @block.tensor
        def _(e):
            pe_q.replay(e)

        @block.scalar
        def _(e):
            act_q.replay(e)

        @block.vector
        def _(e):
            dve_q.replay(e)

        @block.gpsimd
        def _(e):
            pool_q.replay(e)

        @block.sync
        def _(e):
            sp_q.replay(e)

        for g in reversed(self._ctxs):
            g.__exit__(None, None, None)
        return self.nc



def load_const_tiles(mk: MK):
    onesc = mk.sbuf("ones_col", [PART, 1], BF16)
    onesr = mk.sbuf("ones_row", [1, 512], BF16)
    negc = mk.sbuf("neg_col", [PART, 1], BF16)
    mk.din("c_ones", [PART, 1])
    mk.din("c_ones_row", [1, 512])
    mk.din("c_neg", [PART, 1])
    mk.sp.dma(onesc[:], mk.dram["c_ones"][:], inc=mk.sems["s_xdma"])
    mk.sp.dma(onesr[:], mk.dram["c_ones_row"][:], inc=mk.sems["s_xdma"])
    mk.sp.dma(negc[:], mk.dram["c_neg"][:], inc=mk.sems["s_xdma"])
    return onesc, onesr, negc


class PsumPool:

    def __init__(self, mk):
        self.mk = mk
        self.banks = [mk.psum(f"pb{i}", [PART, 512], F32) for i in range(8)]
        self.guard = [None] * 8

    def acquire(self, i):
        g = self.guard[i]
        if g is not None:
            self.mk.pe.wait(g[0], g[1])
            self.guard[i] = None
        return self.banks[i]

    def release(self, i, sem):
        self.guard[i] = (sem, sem.n)



class PsumPool8:
    def __init__(self, mk):
        self.mk = mk
        self.banks = [mk.psum(f"pb{i}", [PART, 512], F32) for i in range(8)]
        self.guard = [None] * 8
        self.rr = 0

    def acquire(self):
        i = self.rr % 8
        self.rr += 1
        g = self.guard[i]
        if g is not None:
            self.mk.pe.wait(g[0], g[1])
            self.guard[i] = None
        return i

    def release(self, i, sem):
        self.guard[i] = (sem, sem.n)


class LNTiles:

    def __init__(self, mu_f, tA, tB, r_f, mu_bf, r_bf, mubc, rbc):
        self.mu_f, self.tA, self.tB, self.r_f = mu_f, tA, tB, r_f
        self.mu_bf, self.r_bf, self.mubc, self.rbc = mu_bf, r_bf, mubc, rbc


def emit_ln_stats(mk, S, pp, lnt, src, src_b, src_sq, nch, W, ones_col,
                  ones_row, epsc, ready=(), d=1024):
    for sem, val in ready:
        mk.pool.wait(sem, val)
    if not isinstance(src_b, list):
        src_b = [src_b]
        src_sq = [src_sq]
    nslot = len(src_b)
    slot_rows = {}
    psum = pp.banks
    src_fn = src if callable(src) else (lambda c, sl: src[:, c, sl])
    for sub in range((W + 511) // 512):
        w0 = sub * 512
        w1 = min(w0 + 512, W)
        sl = slice(w0, w1)
        rsl = slice(0, w1 - w0)
        slot = sub % nslot
        sb_ = src_b[slot]
        ssq_ = src_sq[slot]
        g = slot_rows.get(slot)
        if g is not None:
            mk.pool.wait(S["s_pe"], g)
        for c in range(nch):
            mk.pool.op("tensor_copy", sb_[:, c, rsl], src_fn(c, sl),
                       inc=(S["s_pool"] if c == nch - 1 else None))
        pool_done = S["s_pool"].n
        mk.act.wait(S["s_pool"], pool_done)
        if g is not None:
            mk.act.wait(S["s_pe"], g)
        for c in range(nch):
            mk.act.op("activation", ssq_[:, c, rsl], sb_[:, c, rsl],
                      AF.Square, inc=(S["s_act"] if c == nch - 1 else None))
        sq_done = S["s_act"].n
        ba = pp.acquire()
        bb = pp.acquire()
        mk.pe.wait(S["s_pool"], pool_done)
        mk.pe.wait(S["s_act"], sq_done)
        rows_a = psum[ba][0:1, rsl]
        rows_b = psum[bb][0:1, rsl]
        for c in range(nch):
            mk.pe.op("matmul", rows_a, ones_col[:, :], sb_[:, c, rsl],
                     start=(c == 0), stop=(c == nch - 1))
        for c in range(nch):
            mk.pe.op("matmul", rows_b, ones_col[:, :], ssq_[:, c, rsl],
                     start=(c == 0), stop=(c == nch - 1),
                     inc=(S["s_pe"] if c == nch - 1 else None))
        slot_rows[slot] = S["s_pe"].n
        rows_done = S["s_pe"].n
        mk.dve.wait(S["s_pe"], rows_done)
        mk.dve.op("tensor_scalar", lnt.mu_f[:, rsl], rows_a, 1.0 / d, None,
                  ALU.mult)
        mk.dve.op("tensor_tensor", lnt.tA[:, rsl], lnt.mu_f[:, rsl],
                  lnt.mu_f[:, rsl], ALU.mult, chain=True)
        mk.dve.op("scalar_tensor_tensor", lnt.tB[:, rsl], rows_b, 1.0 / d,
                  lnt.tA[:, rsl], ALU.mult, ALU.subtract, chain=True,
                  inc=S["s_dve"])
        var_done = S["s_dve"].n
        mk.act.wait(S["s_dve"], var_done)
        mk.act.op("activation", lnt.tA[:, rsl], lnt.tB[:, rsl], AF.Sqrt,
                  bias=epsc[0:1, 0:1], inc=S["s_act"])
        sd_done = S["s_act"].n
        mk.dve.wait(S["s_act"], sd_done)
        mk.dve.op("reciprocal", lnt.r_f[:, rsl], lnt.tA[:, rsl])
        mk.dve.self_edge()
        mk.dve.op("tensor_copy", lnt.mu_bf[:, rsl], lnt.mu_f[:, rsl])
        mk.dve.op("tensor_copy", lnt.r_bf[:, rsl], lnt.r_f[:, rsl],
                  inc=S["s_dve"])
        rows_bf = S["s_dve"].n
        pp.release(ba, S["s_dve"])
        pp.release(bb, S["s_dve"])
        bm = pp.acquire()
        br = pp.acquire()
        mk.pe.wait(S["s_dve"], rows_bf)
        mk.pe.op("matmul", psum[bm][:, rsl], ones_row[:, :], lnt.mu_bf[:, rsl],
                 start=True, stop=True)
        mk.pe.op("matmul", psum[br][:, rsl], ones_row[:, :], lnt.r_bf[:, rsl],
                 start=True, stop=True, inc=S["s_pe"])
        bc_done = S["s_pe"].n
        mk.act.wait(S["s_pe"], bc_done)
        mk.act.op("activation", lnt.mubc[:, sl], psum[bm][:, rsl], AF.Copy)
        mk.act.op("activation", lnt.rbc[:, sl], psum[br][:, rsl], AF.Copy,
                  inc=S["s_act"])
        pp.release(bm, S["s_act"])
        pp.release(br, S["s_act"])
    return S["s_act"].n


def emit_ln_apply(mk, S, lnt, src, tmp, bc_ready, nch, dst_bf=None,
                  gamma=None, beta=None, residual=None, dst_f32=None,
                  extra_ready=(), W=None, tmpw=None):
    mk.pool.wait(S["s_act"], bc_ready)
    for sem, val in extra_ready:
        mk.pool.wait(sem, val)
    if W is None:
        W = src.shape[2]
    if tmpw is None:
        tmpw = tmp.shape[2]
    src_fn = src if callable(src) else (lambda c, sl: src[:, c, sl])
    res_fn = residual if (residual is None or callable(residual)) else (
        lambda c, sl: residual[:, c, sl])
    dstf_fn = dst_f32 if (dst_f32 is None or callable(dst_f32)) else (
        lambda c, sl: dst_f32[:, c, sl])
    dstb_fn = dst_bf if (dst_bf is None or callable(dst_bf)) else (
        lambda c, sl: dst_bf[:, c, sl])
    uses = []
    i = 0
    for c in range(nch):
        for s0 in range(0, W, tmpw):
            sl = slice(s0, min(s0 + tmpw, W))
            n = sl.stop - sl.start
            pp_ = i % 2
            if i >= 2:
                mk.pool.wait(S["s_dve"], uses[i - 2])
            mk.pool.op("tensor_tensor", tmp[:, pp_, 0:n], src_fn(c, sl),
                       lnt.mubc[:, sl], ALU.subtract, inc=S["s_pool"])
            u_done = S["s_pool"].n
            mk.dve.wait(S["s_pool"], u_done)
            if residual is not None:
                mk.dve.op("tensor_tensor", tmp[:, pp_, 0:n], tmp[:, pp_, 0:n],
                          lnt.rbc[:, sl], ALU.mult)
                mk.dve.op("tensor_tensor", dstf_fn(c, sl),
                          res_fn(c, sl), tmp[:, pp_, 0:n], ALU.add,
                          chain=True, inc=S["s_dve"])
            elif gamma is None:
                mk.dve.op("tensor_tensor", dstb_fn(c, sl), tmp[:, pp_, 0:n],
                          lnt.rbc[:, sl], ALU.mult, inc=S["s_dve"])
            else:
                mk.dve.op("scalar_tensor_tensor", tmp[:, pp_, 0:n],
                          tmp[:, pp_, 0:n], gamma[:, c:c + 1], lnt.rbc[:, sl],
                          ALU.mult, ALU.mult)
                mk.dve.op("tensor_scalar", dstb_fn(c, sl), tmp[:, pp_, 0:n],
                          beta[:, c:c + 1], None, ALU.add, chain=True,
                          inc=S["s_dve"])
            uses.append(S["s_dve"].n)
            i += 1
    return S["s_dve"].n


class WStream:

    def __init__(self, mk, S, name, bufs, sems):
        self.mk = mk
        self.S = S
        self.bufs = bufs
        self.sems = sems
        self.n = len(bufs)
        self.count = 0
        self.loaded = {}
        self.gates = {}

    def fetch(self, dram_ap, gate=None):
        seq = self.count
        self.count += 1
        b = seq % self.n
        if gate is not None:
            self.mk.pool.wait(self.S["s_pe"], gate)
        nel = dram_ap.shape[-1]
        self.mk.pool.dma(self.bufs[b][:, 0:nel], dram_ap, inc=self.sems[b])
        self.loaded[seq] = (self.sems[b], self.sems[b].n)
        return seq

    def buf(self, seq):
        return self.bufs[seq % self.n]

    def wait_pe(self, seq):
        self.mk.pe.wait(*self.loaded[seq])


"""L1_ffn sandwich stage as a standalone launch:
   y = x + LN( gelu(LN(x) @ w1 + b1) @ w2 + b2 )
Feature-major: x, y as [1024, 2048] fp32 per core (host pre-transposed).
"""

KB = 1024
D = 1024
NCH = 8
TCORE = 2048
TH = 1024
NH = TCORE // TH
HID = 4096
HCH = 32


def emit_l1ffn(mk, x_d, y_d):
    w1_d = mk.din("w1d", [8, PART, 8, 512])
    w2_d = mk.din("w2d", [8, 2, PART, 16, 128])
    b1_d = mk.din("b1cd", [PART, HCH])
    b2_d = mk.din("b2cd", [PART, NCH])
    oc_d = mk.din("c_ones_col", [PART, 1], BF16)
    or_d = mk.din("c_ones_row", [1, PART], BF16)
    eps_d = mk.din("c_eps", [PART, 1])

    S = mk.std_sems(extra=["wb0", "wb1", "wtk", "wtk2", "wtk3", "wtk4"])

    arena = mk.sbuf("arena", [PART, 196 * KB], mybir.dt.uint8)

    def view(off_kb, nbytes_kb, dtype, shape=None):
        ap = arena[:, int(off_kb * KB):int((off_kb + nbytes_kb) * KB)].bitcast(dtype)
        if shape is None:
            return ap
        return ap.rearrange("p (a b) -> p a b", a=shape[0])

    xt = view(0, 64, F32, [NCH, TCORE])
    xnb = view(64, 16, BF16, [NCH, TH])
    xb = view(80, 16, BF16, [NCH, TH])
    xsq = view(96, 16, BF16, [NCH, TH])
    h_t = view(80, 64, BF16, [HCH, TH])
    yb = view(80, 32, F32, [NCH, TH])
    osq = view(112, 16, BF16, [NCH, TH])
    ob = view(128, 16, BF16, [NCH, TH])
    u_tmp = view(112, 8, F32, [2, TH])
    u2_tmp = view(144 - 8, 8, F32, [2, TH])
    o_t = view(144, 32, F32, [NCH, TH])
    w1s = [view(176, 8, BF16, [8, 512]), view(184, 8, BF16, [8, 512])]
    w2s = [view(176, 4, BF16, [16, 128]), view(180, 4, BF16, [16, 128]),
           view(184, 4, BF16, [16, 128]), view(188, 4, BF16, [16, 128])]
    mubc = view(192, 2, BF16)
    rbc = view(194, 2, BF16)

    mu_f = mk.sbuf("mu_f", [1, 512], F32)
    tA = mk.sbuf("tA", [1, 512], F32)
    tB = mk.sbuf("tB", [1, 512], F32)
    r_f = mk.sbuf("r_f", [1, 512], F32)
    mu_bf = mk.sbuf("mu_bf", [1, 512], BF16)
    r_bf = mk.sbuf("r_bf", [1, 512], BF16)
    ones_col = mk.sbuf("ones_col", [PART, 1], BF16)
    ones_row = mk.sbuf("ones_row", [1, PART], BF16)
    b1c = mk.sbuf("b1c", [PART, HCH], F32)
    b2c = mk.sbuf("b2c", [PART, NCH], F32)
    epsc = mk.sbuf("epsc", [PART, 1], F32)

    psum = [mk.psum(f"pb{i}", [PART, 512], F32) for i in range(8)]
    ps_guard = [None] * 8
    ps_rr = [0]

    def ps_acquire():
        i = ps_rr[0] % 8
        ps_rr[0] += 1
        g = ps_guard[i]
        if g is not None:
            mk.pe.wait(g[0], g[1])
            ps_guard[i] = None
        return i

    def ps_release(i, sem):
        ps_guard[i] = (sem, sem.n)

    mk.sp.dma(ones_col[:, :], oc_d[:, :], inc=S["s_xdma"])
    mk.sp.dma(ones_row[:, :], or_d[:, :], inc=S["s_xdma"])
    mk.sp.dma(b1c[:, :], b1_d[:, :], inc=S["s_xdma"])
    mk.sp.dma(b2c[:, :], b2_d[:, :], inc=S["s_xdma"])
    mk.sp.dma(epsc[:, :], eps_d[:, :], inc=S["s_xdma"])
    for c in range(NCH):
        mk.sp.dma(xt[:, c, :], x_d[c * PART:(c + 1) * PART, :], inc=S["s_xdma"])
    x_loaded = S["s_xdma"].n

    w1_sem = {}
    w2_sem = {}
    slab_pe_done = {}
    w2_pe_done = {}
    y_store_done = {0: 0}

    def fetch_w1(th, s, gate=None):
        buf = w1s[s % 2]
        bsem = S[["wb0", "wb1"][s % 2]]
        if gate is not None:
            mk.pool.wait(S["s_pe"], gate)
        mk.pool.dma(buf[:, :, :], w1_d[s].rearrange("p c o -> p (c o)"),
                    inc=bsem)
        w1_sem[(th, s)] = (bsem, bsem.n)

    fetch_w1(0, 0)
    fetch_w1(0, 1)

    def ln_stats(src, src_b, src_sq, ready_sem, ready_val):
        mk.pool.wait(ready_sem, ready_val)
        for c in range(NCH):
            mk.pool.op("tensor_copy", src_b[:, c, :], src[:, c, :],
                       inc=(S["s_pool"] if c == NCH - 1 else None))
        pool_done = S["s_pool"].n
        mk.act.wait(S["s_pool"], pool_done)
        for c in range(NCH):
            mk.act.op("activation", src_sq[:, c, :], src_b[:, c, :], AF.Square,
                      inc=(S["s_act"] if c == NCH - 1 else None))
        sq_done = S["s_act"].n

        for sub in range(TH // 512):
            sl = slice(sub * 512, (sub + 1) * 512)
            ba = ps_acquire()
            bb = ps_acquire()
            mk.pe.wait(S["s_pool"], pool_done)
            mk.pe.wait(S["s_act"], sq_done)
            rows_a = psum[ba][0:1, :]
            rows_b = psum[bb][0:1, :]
            for c in range(NCH):
                mk.pe.op("matmul", rows_a, ones_col[:, :], src_b[:, c, sl],
                         start=(c == 0), stop=(c == NCH - 1))
            for c in range(NCH):
                mk.pe.op("matmul", rows_b, ones_col[:, :], src_sq[:, c, sl],
                         start=(c == 0), stop=(c == NCH - 1),
                         inc=(S["s_pe"] if c == NCH - 1 else None))
            rows_done = S["s_pe"].n
            mk.dve.wait(S["s_pe"], rows_done)
            mk.dve.op("tensor_scalar", mu_f[:, :], rows_a, 1.0 / D, None, ALU.mult)
            mk.dve.op("tensor_tensor", tA[:, :], mu_f[:, :], mu_f[:, :], ALU.mult,
                      chain=True)
            mk.dve.op("scalar_tensor_tensor", tB[:, :], rows_b, 1.0 / D, tA[:, :],
                      ALU.mult, ALU.subtract, chain=True, inc=S["s_dve"])
            var_done = S["s_dve"].n
            mk.act.wait(S["s_dve"], var_done)
            mk.act.op("activation", tA[:, :], tB[:, :], AF.Sqrt, bias=epsc[0:1, 0:1],
                      inc=S["s_act"])
            sd_done = S["s_act"].n
            mk.dve.wait(S["s_act"], sd_done)
            mk.dve.op("reciprocal", r_f[:, :], tA[:, :])
            mk.dve.self_edge()
            mk.dve.op("tensor_copy", mu_bf[:, :], mu_f[:, :])
            mk.dve.op("tensor_copy", r_bf[:, :], r_f[:, :], inc=S["s_dve"])
            rows_bf = S["s_dve"].n
            ps_release(ba, S["s_dve"])
            ps_release(bb, S["s_dve"])
            bm = ps_acquire()
            br = ps_acquire()
            mk.pe.wait(S["s_dve"], rows_bf)
            mk.pe.op("matmul", psum[bm][:, :], ones_row[:, :], mu_bf[:, :],
                     start=True, stop=True)
            mk.pe.op("matmul", psum[br][:, :], ones_row[:, :], r_bf[:, :],
                     start=True, stop=True, inc=S["s_pe"])
            bc_done = S["s_pe"].n
            mk.act.wait(S["s_pe"], bc_done)
            mk.act.op("activation", mubc[:, sl], psum[bm][:, :], AF.Copy)
            mk.act.op("activation", rbc[:, sl], psum[br][:, :], AF.Copy,
                      inc=S["s_act"])
            ps_release(bm, S["s_act"])
            ps_release(br, S["s_act"])
        return S["s_act"].n

    def ln_apply(src, tmp, bc_ready, dst_bf=None, residual=None, dst_f32=None):
        mk.pool.wait(S["s_act"], bc_ready)
        uses = []
        for c in range(NCH):
            pp = c % 2
            if c >= 2:
                mk.pool.wait(S["s_dve"], uses[c - 2])
            mk.pool.op("tensor_tensor", tmp[:, pp, :], src[:, c, :],
                       mubc[:, :], ALU.subtract, inc=S["s_pool"])
            u_done = S["s_pool"].n
            mk.dve.wait(S["s_pool"], u_done)
            if residual is None:
                mk.dve.op("tensor_tensor", dst_bf[:, c, :], tmp[:, pp, :],
                          rbc[:, :], ALU.mult, inc=S["s_dve"])
            else:
                mk.dve.op("tensor_tensor", tmp[:, pp, :], tmp[:, pp, :],
                          rbc[:, :], ALU.mult)
                mk.dve.op("tensor_tensor", dst_f32[:, c, :], residual[:, c, :],
                          tmp[:, pp, :], ALU.add, chain=True, inc=S["s_dve"])
            uses.append(S["s_dve"].n)
        return S["s_dve"].n

    for th in range(NH):
        src = xt[:, :, th * TH:(th + 1) * TH]
        if th > 0:
            mk.pool.wait(S["s_odma"], y_store_done[th])
            mk.act.wait(S["s_odma"], y_store_done[th])
            fetch_w1(th, 0, gate=w2_pe_done[(th - 1, 15)])
            fetch_w1(th, 1)

        bc_ready = ln_stats(src, xb, xsq, S["s_xdma"], x_loaded)
        xn_done = ln_apply(src, u_tmp, bc_ready, dst_bf=xnb)

        for s in range(8):
            buf = w1s[s % 2]
            mk.pe.wait(*w1_sem[(th, s)])
            mk.pe.wait(S["s_dve"], xn_done)
            for m in range(4):
                b0 = ps_acquire()
                b1_ = ps_acquire()
                for kc in range(NCH):
                    w_ap = buf[:, kc, m * 128:(m + 1) * 128]
                    last = kc == NCH - 1
                    mk.pe.op("matmul", psum[b0][:, :], w_ap,
                             xnb[:, kc, 0:512], start=(kc == 0), stop=last)
                    mk.pe.op("matmul", psum[b1_][:, :], w_ap,
                             xnb[:, kc, 512:1024], start=(kc == 0), stop=last,
                             inc=(S["s_pe"] if last else None))
                mm_done = S["s_pe"].n
                hc = s * 4 + m
                mk.act.wait(S["s_pe"], mm_done)
                mk.act.op("activation", h_t[:, hc, 0:512], psum[b0][:, :],
                          AF.Gelu, bias=b1c[:, hc:hc + 1])
                mk.act.op("activation", h_t[:, hc, 512:1024], psum[b1_][:, :],
                          AF.Gelu, bias=b1c[:, hc:hc + 1], inc=S["s_act"])
                ps_release(b0, S["s_act"])
                ps_release(b1_, S["s_act"])
            slab_pe_done[(th, s)] = S["s_pe"].n
            nxt = s + 2
            if nxt < 8:
                fetch_w1(th, nxt, gate=slab_pe_done[(th, s)])
        h_done = S["s_act"].n
        gemm1_all = slab_pe_done[(th, 7)]

        def fetch_w2(oc, hh, gate):
            gi = oc * 2 + hh
            buf = w2s[gi % 4]
            mk.pool.wait(S["s_pe"], gate)
            bsem = S[["wtk", "wtk2", "wtk3", "wtk4"][gi % 4]]
            mk.pool.dma(buf[:, :, :], w2_d[oc, hh].rearrange("p c o -> p (c o)"),
                        inc=bsem)
            w2_sem[(th, gi)] = (bsem, bsem.n)

        for gi0 in range(4):
            fetch_w2(gi0 // 2, gi0 % 2, gemm1_all)

        for oc in range(NCH):
            b0 = ps_acquire()
            b1_ = ps_acquire()
            mk.pe.wait(S["s_act"], h_done)
            for hh in range(2):
                gi = oc * 2 + hh
                mk.pe.wait(*w2_sem[(th, gi)])
                buf = w2s[gi % 4]
                for kc in range(16):
                    kglob = hh * 16 + kc
                    first = (hh == 0 and kc == 0)
                    last = (hh == 1 and kc == 15)
                    w_ap = buf[:, kc, :]
                    mk.pe.op("matmul", psum[b0][:, :], w_ap,
                             h_t[:, kglob, 0:512], start=first, stop=last)
                    mk.pe.op("matmul", psum[b1_][:, :], w_ap,
                             h_t[:, kglob, 512:1024], start=first, stop=last,
                             inc=(S["s_pe"] if kc == 15 else None))
                w2_pe_done[(th, gi)] = S["s_pe"].n
                ngi = gi + 4
                if ngi < 16:
                    fetch_w2(ngi // 2, ngi % 2, w2_pe_done[(th, ngi - 4)])
            mm_done = S["s_pe"].n
            mk.act.wait(S["s_pe"], mm_done)
            mk.act.op("activation", o_t[:, oc, 0:512], psum[b0][:, :],
                      AF.Identity, bias=b2c[:, oc:oc + 1])
            mk.act.op("activation", o_t[:, oc, 512:1024], psum[b1_][:, :],
                      AF.Identity, bias=b2c[:, oc:oc + 1], inc=S["s_act"])
            ps_release(b0, S["s_act"])
            ps_release(b1_, S["s_act"])
        o_done = S["s_act"].n

        bc2 = ln_stats(o_t, ob, osq, S["s_act"], o_done)
        y_done = ln_apply(o_t, u2_tmp, bc2, residual=src, dst_f32=yb)

        mk.sp.wait(S["s_dve"], y_done)
        for c in range(NCH):
            mk.sp.dma(y_d[c * PART:(c + 1) * PART, th * TH:(th + 1) * TH],
                      yb[:, c, :], inc=S["s_odma"])
        y_store_done[th + 1] = S["s_odma"].n

    mk.sp.wait(S["s_odma"], S["s_odma"].n)
    mk.sp.flush()


def host_prep_l1ffn(x_core, w1, b1, w2, b2):
    d = {}
    d["x"] = np.ascontiguousarray(x_core.T)
    d["w1d"] = np.ascontiguousarray(
        w1.reshape(8, 128, 8, 512).transpose(2, 1, 0, 3))
    d["w2d"] = np.ascontiguousarray(
        w2.reshape(2, 16, 128, 8, 128).transpose(3, 0, 2, 1, 4))
    d["b1cd"] = np.ascontiguousarray(b1.reshape(HCH, PART).T)
    d["b2cd"] = np.ascontiguousarray(b2.reshape(NCH, PART).T)
    import ml_dtypes
    d["c_ones_col"] = np.ones((PART, 1), ml_dtypes.bfloat16)
    d["c_ones_row"] = np.ones((1, PART), ml_dtypes.bfloat16)
    d["c_eps"] = np.full((PART, 1), 1e-5, np.float32)
    return d




"""L1_attn (RWKV block) sandwich stage as a standalone launch:
   y = x + LN( rwkv_block( LN(x) ) )
x, y: [1024, 2048] fp32 feature-major per core. Cores paired (h0|h1) per batch.
"""

KB = 1024
D = 1024
NCH = 8
T = 2048
L1_NTH = 4
L1_THW = 512
L1_CMT = 512
L1_NCM = T // L1_CMT


def emit_l1attn(mk, x_d, y_d, n_pairs=4):
    S = mk.std_sems(extra=["wb0", "wb1", "cc1", "cc2", "cc3",
                           "f0", "f1", "g0", "g1", "s_spill", "rcv",
                           "s_iota", "s_xs", "wtk2"])

    cols_d = mk.din("colsd", [PART, 11 * NCH])
    wk_d = mk.din("wkd", [2, PART, 8, 512])
    wv_d = mk.din("wvd", [2, PART, 8, 512])
    wr_d = mk.din("wrd", [2, PART, 8, 512])
    wo_d = mk.din("wod", [2, PART, 8, 512])
    wkey_d = mk.din("wkeyd", [8, PART, 8, 512])
    wval_d = mk.din("wvald", [8, PART, 32, 128])
    wrec_d = mk.din("wrecd", [2, PART, 8, 512])
    flag_d = mk.din("c_flag", [PART, 1])
    iota_d = mk.din("c_iota", [PART, T], BF16)
    eps_d = mk.din("c_eps", [PART, 1])
    onc_d = mk.din("c_ones_col", [PART, 1], BF16)
    onr_d = mk.din("c_ones_row", [1, PART], BF16)

    wkey_bf = mk.dtmp("wkey_bf", [8, PART, 8, 512], BF16)
    wval_bf = mk.dtmp("wval_bf", [8, PART, 32, 128], BF16)
    xi_spill = mk.dtmp("xi_spill", [PART, NCH * T])
    e1_src = mk.dtmp("e1_src", [PART, NCH])
    e1_dst = mk.dtmp("e1_dst", [2 * PART, NCH])
    e2_src = mk.dtmp("e2_src", [PART, 2 * NCH])
    e2_dst = mk.dtmp("e2_dst", [2 * PART, 2 * NCH])
    e3_src = mk.dtmp("e3_src", [PART, NCH])
    e3_dst = mk.dtmp("e3_dst", [2 * PART, NCH])
    groups = [[2 * i, 2 * i + 1] for i in range(n_pairs)]

    arena = mk.sbuf("arena", [PART, 195 * KB], mybir.dt.uint8)

    def view(off_kb, kb, dtype, shape=None):
        ap = arena[:, int(off_kb * KB):int((off_kb + kb) * KB)].bitcast(dtype)
        if shape is None:
            return ap
        return ap.rearrange("p (a b) -> p a b", a=shape[0])

    xi = view(0, 64, F32, [NCH, T])
    num_t = view(0, 32, BF16, [NCH, T])
    den_t = view(32, 32, BF16, [NCH, T])
    rwkv = num_t
    kk_t = view(0, 32, BF16, [32, L1_CMT])
    xmid_lo = view(32, 32, F32, [NCH, 1024])
    lnscrA_b = view(0, 8, BF16, [NCH, 512])
    lnscrA_sq = view(8, 8, BF16, [NCH, 512])
    lnscrA2_b = view(16, 8, BF16, [NCH, 512])
    lnscrA2_sq = view(24, 8, BF16, [NCH, 512])
    h_pad3 = arena[:, 64 * KB:64 * KB + NCH * 2049 * 2].bitcast(BF16).rearrange(
        "p (a b) -> p a b", a=NCH)
    x_stage = view(97, 16, F32, [NCH, 512])
    lnscrC_b = view(113, 8, BF16, [NCH, 512])
    lnscrC_sq = view(121, 8, BF16, [NCH, 512])
    mix_m = view(97, 8, BF16, [NCH, L1_THW])
    ek_t = view(105, 8, BF16, [NCH, L1_THW])
    ekv_t = view(113, 8, BF16, [NCH, L1_THW])
    apad = view(121, 9, BF16, [NCH, 513])
    bpad = view(130, 9, BF16, [NCH, 513])
    sr3 = view(139, 8, BF16, [NCH, L1_THW])
    iotab = view(97, 4, BF16)
    xmid_hi = view(97, 32, F32, [NCH, 1024])
    wbufs = [view(147, 8, BF16), view(155, 8, BF16)]
    pw_st = view(147, 2, F32)
    pw_bf = view(149, 1, BF16)
    rden_st = view(150, 4, F32)
    x2_rl = [view(147, 8, F32), view(155, 8, F32)]
    mubc = view(163, 4, BF16)
    rbc = view(167, 4, BF16)
    sr2 = view(163, 8, BF16, [NCH, L1_THW])
    sig_cm = view(163, 8, BF16, [NCH, L1_CMT])
    tmp2 = view(171, 8, F32, [2, 1024])
    xi_rl = [view(171, 8, F32), view(179, 8, F32)]
    tkv_st2 = [view(171, 1, BF16), view(172, 1, BF16)]
    sr0 = view(179, 8, BF16, [NCH, L1_THW])
    xr2_m = view(179, 8, BF16, [NCH, L1_CMT])
    sr1 = view(187, 8, BF16, [NCH, L1_THW])
    xk2_m = view(187, 8, BF16, [NCH, L1_CMT])
    tkv_st = None

    def sr_tile(th):
        return [sr0, sr1, sr2, sr3][th]

    def xmid_fn(c, sl):
        if sl.stop <= 1024:
            return xmid_lo[:, c, sl]
        assert sl.start >= 1024
        return xmid_hi[:, c, slice(sl.start - 1024, sl.stop - 1024)]

    mu_f = mk.sbuf("mu_f", [1, 512], F32)
    tA = mk.sbuf("tA", [1, 512], F32)
    tB = mk.sbuf("tB", [1, 512], F32)
    r_f = mk.sbuf("r_f", [1, 512], F32)
    mu_bf = mk.sbuf("mu_bf", [1, 512], BF16)
    r_bf = mk.sbuf("r_bf", [1, 512], BF16)
    ones_col = mk.sbuf("ones_col", [PART, 1], BF16)
    ones_row = mk.sbuf("ones_row", [1, PART], BF16)
    epsc = mk.sbuf("epsc", [PART, 1], F32)
    flagc = mk.sbuf("flagc", [PART, 1], F32)
    colsb = mk.sbuf("colsb", [PART, 11 * NCH], F32)
    lamc = mk.sbuf("lamc", [PART, NCH], F32)
    euc = mk.sbuf("euc", [PART, NCH], F32)
    tec = mk.sbuf("tec", [PART, NCH], F32)
    wcolc = mk.sbuf("wcolc", [PART, NCH], F32)
    carry_a = mk.sbuf("carry_a", [PART, NCH], F32)
    carry_b = mk.sbuf("carry_b", [PART, NCH], F32)
    e1cmp = mk.sbuf("e1cmp", [PART, NCH], F32)
    e1rcv = mk.sbuf("e1rcv", [PART, NCH], F32)
    e2cmp = mk.sbuf("e2cmp", [PART, 2 * NCH], F32)
    e2rcv = mk.sbuf("e2rcv", [PART, 2 * NCH], F32)
    amid = mk.sbuf("amid", [PART, NCH], F32)
    bmid = mk.sbuf("bmid", [PART, NCH], F32)
    e3cmp = mk.sbuf("e3cmp", [PART, NCH], F32)
    e3rcv = mk.sbuf("e3rcv", [PART, NCH], F32)
    tmk1 = mk.sbuf("tmk1", [PART, NCH], F32)
    tmv1 = mk.sbuf("tmv1", [PART, NCH], F32)
    tmr1 = mk.sbuf("tmr1", [PART, NCH], F32)
    ftmk1 = mk.sbuf("ftmk1", [PART, NCH], F32)
    ftmr1 = mk.sbuf("ftmr1", [PART, NCH], F32)

    def col(i):
        return colsb[:, i * NCH:(i + 1) * NCH]
    ln1w, ln1b, ln2w, ln2b = col(0), col(1), col(2), col(3)
    tmk, tmv, tmr, ftmk, ftmr = col(4), col(5), col(6), col(7), col(8)
    tdc, tfc = col(9), col(10)

    pp = PsumPool8(mk)
    psum = pp.banks
    lnt = LNTiles(mu_f, tA, tB, r_f, mu_bf, r_bf, mubc, rbc)
    ws = WStream(mk, S, "w", wbufs, [S["wb0"], S["wb1"]])

    mk.sp.dma(ones_col[:, :], onc_d[:, :], inc=S["s_xdma"])
    mk.sp.dma(ones_row[:, :], onr_d[:, :], inc=S["s_xdma"])
    mk.sp.dma(epsc[:, :], eps_d[:, :], inc=S["s_xdma"])
    mk.sp.dma(flagc[:, :], flag_d[:, :], inc=S["s_xdma"])
    mk.sp.dma(colsb[:, :], cols_d[:, :], inc=S["s_xdma"])
    consts_loaded = S["s_xdma"].n

    mk.dve.wait(S["s_xdma"], consts_loaded)
    for dst, src in [(tmk1, tmk), (tmv1, tmv), (tmr1, tmr),
                     (ftmk1, ftmk), (ftmr1, ftmr)]:
        mk.dve.op("tensor_scalar", dst[:, :], src, -1.0, 1.0, ALU.mult, ALU.add)
    mk.dve.op("memset", carry_a[:, :], 0.0)
    mk.dve.op("memset", carry_b[:, :], 0.0, inc=S["s_dve"])
    cols1_done = S["s_dve"].n
    mk.act.wait(S["s_xdma"], consts_loaded)
    mk.act.op("activation", tec[:, :], tdc, AF.Exp)
    mk.act.op("activation", lamc[:, :], tec[:, :], AF.Exp, scale=-1.0,
              chain=True)
    mk.act.op("activation", euc[:, :], tfc, AF.Exp, inc=S["s_act"])
    lam_done = S["s_act"].n
    mk.dve.wait(S["s_act"], lam_done)
    mk.dve.op("tensor_scalar", wcolc[:, :], tec[:, :], -1.0, None, ALU.mult,
              inc=S["s_dve"])
    wcol_done = S["s_dve"].n

    xstage_free = 0
    for q in range(4):
        sl = slice(q * 512, (q + 1) * 512)
        mk.sp.wait(S["s_pool"], xstage_free)
        for c in range(NCH):
            mk.sp.dma(x_stage[:, c, :], x_d[c * PART:(c + 1) * PART, sl],
                      inc=S["s_xs"])
        xl = S["s_xs"].n
        lnt_v = LNTiles(mu_f, tA, tB, r_f, mu_bf, r_bf,
                        mubc[:, 0:512], rbc[:, 0:512])
        bc = emit_ln_stats(mk, S, pp, lnt_v, x_stage, lnscrC_b, lnscrC_sq,
                           NCH, 512, ones_col, ones_row, epsc,
                           ready=[(S["s_xs"], xl)])
        mk.pool.wait(S["s_act"], bc)
        uses = []
        for c in range(NCH):
            ppg = c % 2
            if c >= 2:
                mk.pool.wait(S["s_dve"], uses[c - 2])
            mk.pool.op("tensor_tensor", tmp2[:, ppg, 0:512], x_stage[:, c, :],
                       mubc[:, 0:512], ALU.subtract, inc=S["s_pool"])
            mk.dve.wait(S["s_pool"], S["s_pool"].n)
            mk.dve.op("tensor_tensor", xi[:, c, sl], tmp2[:, ppg, 0:512],
                      rbc[:, 0:512], ALU.mult, inc=S["s_dve"])
            uses.append(S["s_dve"].n)
        xstage_free = S["s_pool"].n
    xi_done = S["s_dve"].n

    bc = emit_ln_stats(mk, S, pp, lnt, xi, lnscrC_b, lnscrC_sq, NCH, T,
                       ones_col, ones_row, epsc,
                       ready=[(S["s_dve"], xi_done)])
    h_done = emit_ln_apply(mk, S, lnt, xi, tmp2, bc, NCH,
                           dst_bf=h_pad3[:, :, 1:2049],
                           gamma=ln1w, beta=ln1b, W=T)

    mk.pool.wait(S["s_dve"], h_done)
    mk.pool.op("tensor_copy", e1cmp[:, :], h_pad3[:, :, 2048:2049].squeeze(2),
               inc=S["s_pool"])
    mk.pool.dma(e1_src[:, :], e1cmp[:, :], chain=True, inc=S["rcv"])
    mk.pool.wait(S["rcv"], S["rcv"].n)
    mk.pool.op("collective_compute", "AllGather", ALU.bypass,
               replica_groups=groups, ins=[e1_src[:, :].opt()],
               outs=[e1_dst[:, :].opt()], inc=S["cc1"], inc_by=1,
               inc_default=True)
    mk.pool.wait(S["cc1"], S["cc1"].n)
    mk.pool.dma(e1rcv[:, :], e1_dst[0:PART, :], inc=S["rcv"])
    mk.pool.wait(S["rcv"], S["rcv"].n)
    mk.pool.wait(S["s_xdma"], consts_loaded)
    mk.pool.op("tensor_scalar", h_pad3[:, :, 0:1].squeeze(2), e1rcv[:, :],
               flagc[:, 0:1], None, ALU.mult, inc=S["s_pool"])
    e1_done = S["s_pool"].n

    mk.sp.wait(S["s_dve"], h_done)
    for c in range(NCH):
        mk.sp.dma(xi_spill[:, c * T:(c + 1) * T], xi[:, c, :], inc=S["s_spill"])
    spill_done = S["s_spill"].n

    h_full = h_pad3[:, :, 1:2049]
    hh_full = h_pad3[:, :, 0:2048]

    def mixes(sl, which, hf, hhf, gates=()):
        for g, v in gates:
            mk.pool.wait(g, v)
        uses = []
        i = 0
        w = sl.stop - sl.start
        for (dst, tmc, tm1c) in which:
            for c in range(NCH):
                ppg = i % 2
                if i >= 2:
                    mk.pool.wait(S["s_dve"], uses[i - 2])
                mk.pool.op("tensor_scalar", tmp2[:, ppg, 0:w],
                           hhf[:, c, sl], tm1c[:, c:c + 1], None, ALU.mult,
                           inc=S["s_pool"])
                mk.dve.wait(S["s_pool"], S["s_pool"].n)
                mk.dve.op("scalar_tensor_tensor", dst[:, c, :],
                          hf[:, c, sl], tmc[:, c:c + 1], tmp2[:, ppg, 0:w],
                          ALU.mult, ALU.add, inc=S["s_dve"])
                uses.append(S["s_dve"].n)
                i += 1
        return S["s_dve"].n

    def gemm_1024(x_src, w_dram, evac, gates=(), tw=L1_THW):
        for slab in range(2):
            seq = ws.fetch(w_dram[slab].rearrange("p c o -> p (c o)"),
                           gate=ws.gates.get(ws.count - 2))
            ws.wait_pe(seq)
            for g, v in gates:
                mk.pe.wait(g, v)
            buf = ws.buf(seq).rearrange("p (c o) -> p c o", c=8)
            for m in range(4):
                oc = slab * 4 + m
                b0 = pp.acquire()
                for kc in range(NCH):
                    mk.pe.op("matmul", psum[b0][:, 0:tw],
                             buf[:, kc, m * 128:(m + 1) * 128],
                             x_src[:, kc, :], start=(kc == 0),
                             stop=(kc == NCH - 1),
                             inc=(S["s_pe"] if kc == NCH - 1 else None))
                evac(oc, b0, S["s_pe"].n)
            ws.gates[seq] = S["s_pe"].n

    stage_jobs = ([(wkey_bf[s_], wkey_d[s_]) for s_ in range(8)] +
                  [(wval_bf[s_], wval_d[s_]) for s_ in range(8)])
    stage_state = {"i": 0}

    def emit_stage(n):
        for _ in range(n):
            i_ = stage_state["i"]
            if i_ >= len(stage_jobs):
                return
            dst_, src_ = stage_jobs[i_]
            mk.pool.dma(dst_.rearrange("p c o -> p (c o)"),
                        src_.rearrange("p c o -> p (c o)"), inc=S["wtk2"])
            stage_state["i"] = i_ + 1

    prev_numden = 0
    prev_r_mms = 0
    for th in range(L1_NTH):
        sl = slice(th * L1_THW, (th + 1) * L1_THW)
        srt = sr_tile(th)
        gates = [(S["s_pool"], e1_done), (S["s_dve"], cols1_done)]
        if th > 0:
            gates.append((S["s_dve"], prev_numden))
            gates.append((S["s_pe"], prev_r_mms))

        mixk_done = mixes(sl, [(mix_m, tmk, tmk1)], h_full, hh_full,
                          gates=gates)

        def evac_k(oc, b0, mmd):
            mk.act.wait(S["s_pe"], mmd)
            mk.act.op("activation", ek_t[:, oc, :], psum[b0][:, 0:L1_THW], AF.Exp,
                      inc=S["s_act"])
            pp.release(b0, S["s_act"])
        gemm_1024(mix_m, wk_d, evac_k, gates=[(S["s_dve"], mixk_done)])
        k_mms = S["s_pe"].n
        ek_done = S["s_act"].n

        mixv_done = mixes(sl, [(mix_m, tmv, tmv1)], h_full, hh_full,
                          gates=[(S["s_pe"], k_mms)])

        def evac_v(oc, b0, mmd):
            mk.dve.wait(S["s_pe"], mmd)
            mk.dve.wait(S["s_act"], ek_done)
            mk.dve.op("tensor_tensor", ekv_t[:, oc, :], psum[b0][:, 0:L1_THW],
                      ek_t[:, oc, :], ALU.mult, inc=S["s_dve"])
            pp.release(b0, S["s_dve"])
        gemm_1024(mix_m, wv_d, evac_v, gates=[(S["s_dve"], mixv_done)])
        v_mms = S["s_pe"].n
        ekv_done = S["s_dve"].n

        mixr_done = mixes(sl, [(mix_m, tmr, tmr1)], h_full, hh_full,
                          gates=[(S["s_pe"], v_mms)])

        def evac_r(oc, b0, mmd):
            mk.act.wait(S["s_pe"], mmd)
            mk.act.op("activation", srt[:, oc, :], psum[b0][:, 0:L1_THW],
                      AF.Sigmoid, inc=S["s_act"])
            pp.release(b0, S["s_act"])
        gemm_1024(mix_m, wr_d, evac_r, gates=[(S["s_dve"], mixr_done)])
        prev_r_mms = S["s_pe"].n
        emit_stage(4)

        mk.pool.wait(S["s_dve"], ekv_done)
        mk.pool.wait(S["s_pool"], S["s_pool"].n)
        mk.pool.op("tensor_copy", apad[:, :, 0:1].squeeze(2), carry_a[:, :])
        mk.pool.op("tensor_copy", bpad[:, :, 0:1].squeeze(2), carry_b[:, :],
                   chain=True, inc=S["s_pool"])
        pads_done = S["s_pool"].n
        mk.dve.wait(S["s_pool"], pads_done)
        mk.dve.wait(S["s_act"], lam_done)
        for c in range(NCH):
            lam_b = lamc[:, c:c + 1].broadcast_to([PART, L1_THW])
            mk.dve.op("tensor_tensor_scan", apad[:, c, 1:513], lam_b,
                      ekv_t[:, c, :], apad[:, c, 0:1], ALU.mult, ALU.add)
            mk.dve.op("tensor_tensor_scan", bpad[:, c, 1:513], lam_b,
                      ek_t[:, c, :], bpad[:, c, 0:1], ALU.mult, ALU.add,
                      inc=(S["s_dve"] if c == NCH - 1 else None))
        scans_done = S["s_dve"].n
        mk.pool.wait(S["s_dve"], scans_done)
        mk.pool.op("tensor_copy", carry_a[:, :], apad[:, :, 512:513].squeeze(2),
                   inc=S["s_pool"])
        mk.pool.op("tensor_copy", carry_b[:, :], bpad[:, :, 512:513].squeeze(2),
                   inc=S["s_pool"])

        mk.dve.wait(S["s_spill"], spill_done)
        mk.dve.wait(S["s_dve"], scans_done)
        for c in range(NCH):
            mk.dve.op("scalar_tensor_tensor", num_t[:, c, sl], ekv_t[:, c, :],
                      euc[:, c:c + 1], apad[:, c, 0:512], ALU.mult, ALU.add)
            mk.dve.op("scalar_tensor_tensor", den_t[:, c, sl], ek_t[:, c, :],
                      euc[:, c:c + 1], bpad[:, c, 0:512], ALU.mult, ALU.add,
                      inc=(S["s_dve"] if c == NCH - 1 else None))
        prev_numden = S["s_dve"].n

    mk.pool.wait(S["s_pool"], S["s_pool"].n)
    mk.pool.op("tensor_copy", e2cmp[:, 0:NCH], carry_a[:, :])
    mk.pool.op("tensor_copy", e2cmp[:, NCH:2 * NCH], carry_b[:, :],
               chain=True, inc=S["s_pool"])
    mk.pool.dma(e2_src[:, :], e2cmp[:, :], chain=True, inc=S["rcv"])
    mk.pool.wait(S["rcv"], S["rcv"].n)
    mk.pool.op("collective_compute", "AllGather", ALU.bypass,
               replica_groups=groups, ins=[e2_src[:, :].opt()],
               outs=[e2_dst[:, :].opt()], inc=S["cc2"], inc_by=1,
               inc_default=True)
    mk.pool.wait(S["cc2"], S["cc2"].n)
    mk.pool.dma(e2rcv[:, :], e2_dst[0:PART, :], inc=S["rcv"])
    mk.pool.wait(S["rcv"], S["rcv"].n)
    mk.pool.op("tensor_scalar", amid[:, :], e2rcv[:, 0:NCH], flagc[:, 0:1],
               None, ALU.mult)
    mk.pool.op("tensor_scalar", bmid[:, :], e2rcv[:, NCH:2 * NCH],
               flagc[:, 0:1], None, ALU.mult, chain=True, inc=S["s_pool"])
    mid_done = S["s_pool"].n

    mk.sp.wait(S["s_dve"], prev_numden)
    mk.sp.wait(S["s_pe"], prev_r_mms)
    mk.sp.dma(iotab[:, :], iota_d[:, :], inc=S["s_iota"])
    iota_loaded = S["s_iota"].n
    last_w_gemm = ws.gates.get(ws.count - 1, 0)
    mk.dve.wait(S["s_pe"], last_w_gemm)
    mk.dve.wait(S["s_iota"], iota_loaded)
    mk.dve.wait(S["s_pool"], mid_done)
    mk.dve.wait(S["s_dve"], wcol_done)
    for c in range(NCH):
        for th in range(L1_NTH):
            sl = slice(th * L1_THW, (th + 1) * L1_THW)
            mk.dve.op("tensor_scalar", pw_st[:, 0:L1_THW], iotab[:, sl],
                      wcolc[:, c:c + 1], None, ALU.mult, chain=True,
                      inc=S["s_dve"])
            mk.act.wait(S["s_dve"], S["s_dve"].n)
            mk.act.op("activation", pw_bf[:, 0:L1_THW], pw_st[:, 0:L1_THW], AF.Exp,
                      inc=S["s_act"])
            mk.dve.wait(S["s_act"], S["s_act"].n)
            mk.dve.op("scalar_tensor_tensor", num_t[:, c, sl], pw_bf[:, 0:L1_THW],
                      amid[:, c:c + 1], num_t[:, c, sl], ALU.mult, ALU.add)
            mk.dve.op("scalar_tensor_tensor", den_t[:, c, sl], pw_bf[:, 0:L1_THW],
                      bmid[:, c:c + 1], den_t[:, c, sl], ALU.mult, ALU.add,
                      inc=S["s_dve"])
    corr_done = S["s_dve"].n

    mk.act.wait(S["s_dve"], corr_done)
    for c in range(NCH):
        for hf in range(2):
            hsl = slice(hf * 1024, (hf + 1) * 1024)
            mk.act.wait(S["s_dve"], S["s_dve"].n)
            mk.act.op("activation", rden_st[:, 0:1024], den_t[:, c, hsl],
                      AF.Ln, chain=True, inc=S["s_act"])
            mk.act.op("activation", den_t[:, c, hsl], rden_st[:, 0:1024],
                      AF.Exp, scale=-1.0, chain=True, inc=S["s_act"])
            mk.dve.wait(S["s_act"], S["s_act"].n)
            mk.dve.op("tensor_tensor", num_t[:, c, hsl], num_t[:, c, hsl],
                      den_t[:, c, hsl], ALU.mult, inc=S["s_dve"])
        for th in range(L1_NTH):
            sl = slice(th * L1_THW, (th + 1) * L1_THW)
            mk.dve.op("tensor_tensor", num_t[:, c, sl], num_t[:, c, sl],
                      sr_tile(th)[:, c, :], ALU.mult, chain=True,
                      inc=S["s_dve"])
    rwkv_done = S["s_dve"].n

    xi_sems = [S["f0"], S["f1"]]
    xi_loaded = {}
    xi_freed = {}

    def xi_reload(oc, gate=None):
        b = oc % 2
        mk.sp.wait(S["s_dve"], rwkv_done)
        if gate is not None:
            mk.sp.wait(S["s_dve"], gate)
        mk.sp.dma(xi_rl[b][:, :], xi_spill[:, oc * T:(oc + 1) * T],
                  inc=xi_sems[b])
        xi_loaded[oc] = (xi_sems[b], xi_sems[b].n)

    xi_reload(0)
    xi_reload(1)
    mk.pool.wait(S["s_dve"], rwkv_done)
    for slab in range(2):
        seq = ws.fetch(wo_d[slab].rearrange("p c o -> p (c o)"),
                       gate=ws.gates.get(ws.count - 2))
        ws.wait_pe(seq)
        mk.pe.wait(S["s_dve"], rwkv_done)
        buf = ws.buf(seq).rearrange("p (c o) -> p c o", c=8)
        for m in range(4):
            oc = slab * 4 + m
            for tt in range(4):
                tsl = slice(tt * 512, (tt + 1) * 512)
                b0 = pp.acquire()
                for kc in range(NCH):
                    mk.pe.op("matmul", psum[b0][:, :],
                             buf[:, kc, m * 128:(m + 1) * 128],
                             rwkv[:, kc, tsl], start=(kc == 0),
                             stop=(kc == NCH - 1),
                             inc=(S["s_pe"] if kc == NCH - 1 else None))
                mmd = S["s_pe"].n
                mk.dve.wait(S["s_pe"], mmd)
                mk.dve.wait(*xi_loaded[oc])
                mk.dve.op("tensor_tensor", xmid_fn(oc, tsl), psum[b0][:, :],
                          xi_rl[oc % 2][:, tsl], ALU.add, inc=S["s_dve"])
                pp.release(b0, S["s_dve"])
            xi_freed[oc] = S["s_dve"].n
            if oc + 2 < NCH:
                xi_reload(oc + 2, gate=xi_freed[oc])
        ws.gates[seq] = S["s_pe"].n
    xmid_done = S["s_dve"].n

    bc = emit_ln_stats(mk, S, pp, lnt, xmid_fn, [lnscrA_b, lnscrA2_b],
                       [lnscrA_sq, lnscrA2_sq], NCH, T,
                       ones_col, ones_row, epsc,
                       ready=[(S["s_dve"], xmid_done)])
    h2_done = emit_ln_apply(mk, S, lnt, xmid_fn, tmp2, bc, NCH,
                            dst_bf=h_pad3[:, :, 1:2049],
                            gamma=ln2w, beta=ln2b, W=T)

    mk.pool.wait(S["s_dve"], h2_done)
    mk.pool.op("tensor_copy", e3cmp[:, :], h_pad3[:, :, 2048:2049].squeeze(2),
               inc=S["s_pool"])
    mk.pool.dma(e3_src[:, :], e3cmp[:, :], chain=True, inc=S["rcv"])
    mk.pool.wait(S["rcv"], S["rcv"].n)
    mk.pool.op("collective_compute", "AllGather", ALU.bypass,
               replica_groups=groups, ins=[e3_src[:, :].opt()],
               outs=[e3_dst[:, :].opt()], inc=S["cc3"], inc_by=1,
               inc_default=True)
    mk.pool.wait(S["cc3"], S["cc3"].n)
    mk.pool.dma(e3rcv[:, :], e3_dst[0:PART, :], inc=S["rcv"])
    mk.pool.wait(S["rcv"], S["rcv"].n)
    mk.pool.op("tensor_scalar", h_pad3[:, :, 0:1].squeeze(2), e3rcv[:, :],
               flagc[:, 0:1], None, ALU.mult, inc=S["s_pool"])
    e3_done = S["s_pool"].n

    wbufs2 = [view(130, 8, BF16), view(138, 8, BF16)]
    w2sems = [S["f0"], S["f1"]]
    w2st = {"count": 0, "loaded": {}, "gates": {}}
    cm_free_dve = S["s_dve"].n
    cm_free_pool = S["s_pool"].n

    def fetch2(dram_flat):
        seq = w2st["count"]
        w2st["count"] = seq + 1
        b = seq % 2
        g = w2st["gates"].get(seq - 2)
        if g is not None:
            mk.sp.wait(S["s_pe"], g)
        if seq < 2:
            mk.sp.wait(S["s_dve"], cm_free_dve)
            mk.sp.wait(S["s_pool"], cm_free_pool)
            mk.sp.wait(S["wtk2"], S["wtk2"].n)
        mk.sp.dma(wbufs2[b][:, :], dram_flat, inc=w2sems[b])
        w2st["loaded"][seq] = (w2sems[b], w2sems[b].n)
        return seq

    tkv_freed = {}
    prev_fin = 0
    for tt in range(L1_NCM):
        sl = slice(tt * L1_CMT, (tt + 1) * L1_CMT)
        gates = [(S["s_pool"], e3_done), (S["s_dve"], cols1_done)]
        if tt > 0:
            gates.append((S["s_pool"], prev_fin))
        cm_done = mixes(sl, [(xk2_m, ftmk, ftmk1), (xr2_m, ftmr, ftmr1)],
                        h_pad3[:, :, 1:2049], h_pad3[:, :, 0:2048],
                        gates=gates)

        def evac_sig(oc, b0, mmd):
            mk.act.wait(S["s_pe"], mmd)
            mk.act.op("activation", sig_cm[:, oc, :], psum[b0][:, 0:L1_CMT],
                      AF.Sigmoid, inc=S["s_act"])
            pp.release(b0, S["s_act"])
        gemm_1024(xr2_m, wrec_d, evac_sig, gates=[(S["s_dve"], cm_done)],
                  tw=L1_CMT)
        sig_done = S["s_act"].n

        for slab in range(8):
            seq = fetch2(wkey_bf[slab].rearrange("p c o -> p (c o)"))
            mk.pe.wait(*w2st["loaded"][seq])
            buf = wbufs2[seq % 2].rearrange("p (c o) -> p c o", c=8)
            for m in range(4):
                oc = slab * 4 + m
                b0 = pp.acquire()
                for kc in range(NCH):
                    mk.pe.op("matmul", psum[b0][:, :],
                             buf[:, kc, m * 128:(m + 1) * 128],
                             xk2_m[:, kc, :], start=(kc == 0),
                             stop=(kc == NCH - 1),
                             inc=(S["s_pe"] if kc == NCH - 1 else None))
                mmd = S["s_pe"].n
                mk.dve.wait(S["s_pe"], mmd)
                mk.dve.op("tensor_scalar", kk_t[:, oc, :], psum[b0][:, :],
                          0.0, None, ALU.max, inc=S["s_dve"])
                pp.release(b0, S["s_dve"])
            w2st["gates"][seq] = S["s_pe"].n
        kkr_done = S["s_dve"].n
        mk.act.wait(S["s_dve"], kkr_done)
        for c in range(32):
            mk.act.op("activation", kk_t[:, c, :], kk_t[:, c, :], AF.Square,
                      inc=(S["s_act"] if c == 31 else None))
        kksq_done = S["s_act"].n

        for oc in range(NCH):
            seq = fetch2(wval_bf[oc].rearrange("p c o -> p (c o)"))
            mk.pe.wait(*w2st["loaded"][seq])
            mk.pe.wait(S["s_act"], kksq_done)
            buf = wbufs2[seq % 2].rearrange("p (c o) -> p c o", c=32)
            b0 = pp.acquire()
            for kc in range(32):
                mk.pe.op("matmul", psum[b0][:, 0:L1_CMT], buf[:, kc, :],
                         kk_t[:, kc, :], start=(kc == 0), stop=(kc == 31),
                         inc=(S["s_pe"] if kc == 31 else None))
            mmd = S["s_pe"].n
            w2st["gates"][seq] = S["s_pe"].n
            tb = tkv_st2[oc % 2]
            mk.dve.wait(S["s_pe"], mmd)
            mk.dve.wait(S["s_act"], sig_done)
            if oc >= 2:
                mk.dve.wait(S["s_pool"], tkv_freed[oc - 2])
            mk.dve.op("tensor_tensor", tb[:, 0:L1_CMT], psum[b0][:, 0:L1_CMT],
                      sig_cm[:, oc, :], ALU.mult, inc=S["s_dve"])
            pp.release(b0, S["s_dve"])
            mk.pool.wait(S["s_dve"], S["s_dve"].n)
            mk.pool.op("tensor_tensor", xmid_fn(oc, sl), xmid_fn(oc, sl),
                       tb[:, 0:L1_CMT], ALU.add, inc=S["s_pool"])
            tkv_freed[oc] = S["s_pool"].n
        prev_fin = S["s_pool"].n
    xfin_done = S["s_pool"].n

    bc = emit_ln_stats(mk, S, pp, lnt, xmid_fn, [lnscrA_b, lnscrA2_b],
                       [lnscrA_sq, lnscrA2_sq], NCH, T,
                       ones_col, ones_row, epsc,
                       ready=[(S["s_pool"], xfin_done)])
    x2_sems = [S["g0"], S["g1"]]
    x2_loaded = {}
    x2_freed = {}

    def x2_reload(c, gate=None):
        b = c % 2
        mk.sp.wait(S["s_pool"], xfin_done)
        if gate is not None:
            mk.sp.wait(S["s_dve"], gate)
        mk.sp.dma(x2_rl[b][:, :], x_d[c * PART:(c + 1) * PART, :],
                  inc=x2_sems[b])
        x2_loaded[c] = (x2_sems[b], x2_sems[b].n)

    x2_reload(0)
    x2_reload(1)
    mk.pool.wait(S["s_act"], bc)
    store_cnt = {}
    tmp_use = {0: 0, 1: 0}
    for c in range(NCH):
        b = c % 2
        for hf in range(2):
            hsl = slice(hf * 1024, (hf + 1) * 1024)
            mk.pool.wait(S["s_dve"], tmp_use[hf])
            mk.pool.op("tensor_tensor", tmp2[:, hf, :],
                       xmid_fn(c, hsl), mubc[:, hsl], ALU.subtract,
                       inc=S["s_pool"])
            mk.dve.wait(S["s_pool"], S["s_pool"].n)
            mk.dve.wait(*x2_loaded[c])
            mk.dve.op("tensor_tensor", tmp2[:, hf, :], tmp2[:, hf, :],
                      rbc[:, hsl], ALU.mult)
            mk.dve.op("tensor_tensor", xmid_fn(c, hsl), x2_rl[b][:, hsl],
                      tmp2[:, hf, :], ALU.add, chain=True, inc=S["s_dve"])
            tmp_use[hf] = S["s_dve"].n
        yc_done = S["s_dve"].n
        x2_freed[c] = yc_done
        mk.sp.wait(S["s_dve"], yc_done)
        mk.sp.dma(y_d[c * PART:(c + 1) * PART, 0:1024],
                  xmid_fn(c, slice(0, 1024)), inc=S["s_odma"])
        mk.sp.dma(y_d[c * PART:(c + 1) * PART, 1024:2048],
                  xmid_fn(c, slice(1024, 2048)), inc=S["s_odma"])
        store_cnt[c] = S["s_odma"].n
        if c + 2 < NCH:
            x2_reload(c + 2, gate=x2_freed[c])

    mk.sp.wait(S["s_odma"], S["s_odma"].n)
    mk.sp.flush()
    mk.pe.flush(); mk.act.flush(); mk.dve.flush(); mk.pool.flush()


def make_slabs(w, kcs, ow):
    K, N = w.shape
    n_slab = N // ow
    return np.ascontiguousarray(
        w.reshape(kcs, 128, n_slab, ow).transpose(2, 1, 0, 3))


def host_prep_l1attn(x_core, inp, h):
    d = {}
    d["x"] = np.ascontiguousarray(x_core.T)
    cols = []
    for k in ["ln1_w", "ln1_b", "ln2_w", "ln2_b"]:
        cols.append(np.asarray(inp[f"L1_attn_{k}"]))
    for k in ["att_tmk", "att_tmv", "att_tmr", "ffn_tmk", "ffn_tmr"]:
        cols.append(np.asarray(inp[f"L1_attn_{k}"]).reshape(-1))
    cols.append(np.asarray(inp["L1_attn_time_decay"]))
    cols.append(np.asarray(inp["L1_attn_time_first"]))
    d["colsd"] = np.ascontiguousarray(
        np.stack([c.reshape(8, 128).T for c in cols], 1).reshape(128, 88))
    d["wkd"] = make_slabs(np.ascontiguousarray(inp["L1_attn_Wk"].T), 8, 512)
    d["wvd"] = make_slabs(np.ascontiguousarray(inp["L1_attn_Wv"].T), 8, 512)
    d["wrd"] = make_slabs(np.ascontiguousarray(inp["L1_attn_Wr"].T), 8, 512)
    d["wod"] = make_slabs(np.ascontiguousarray(inp["L1_attn_Wo"].T), 8, 512)
    d["wkeyd"] = make_slabs(np.ascontiguousarray(inp["L1_attn_Wkey"].T), 8, 512)
    d["wvald"] = make_slabs(np.ascontiguousarray(inp["L1_attn_Wval"].T), 32, 128)
    d["wrecd"] = make_slabs(np.ascontiguousarray(inp["L1_attn_Wrec"].T), 8, 512)
    import ml_dtypes
    d["c_flag"] = np.full((128, 1), float(h), np.float32)
    d["c_iota"] = np.broadcast_to(
        np.arange(T, dtype=np.float32)[None, :], (128, T)).astype(
            ml_dtypes.bfloat16)
    d["c_eps"] = np.full((128, 1), 1e-5, np.float32)
    d["c_ones_col"] = np.ones((128, 1), ml_dtypes.bfloat16)
    d["c_ones_row"] = np.ones((1, 128), ml_dtypes.bfloat16)
    return d




"""L0_attn (gated linear attention) sandwich stage:
   y = x + LN( L0a( LN(x) ) )
x, y: [1024, 2048] fp32 feature-major per core; cores paired (h0|h1).
"""

KB = 1024
D = 1024
NCH = 8
T = 2048
L0A_NC = 16
H = 16
HD = 64
L0A_EPS = 1e-5


def emit_l0attn(mk, x_d, y_d, n_pairs=4):
    sm_used.clear(); delta_used.clear(); prefix_done.clear()
    S = mk.std_sems(extra=["wb0", "wb1", "cc1", "g0", "g1", "rcv", "s_xs",
                           "wtk"])

    wq_d = mk.din("wqd", [8, PART, 8, 128])
    wk_d = mk.din("wkd", [8, PART, 8, 128])
    wg_d = mk.din("wgd", [8, PART, 8, 128])
    wproj_d = mk.din("wprojd", [8, PART, 8, 128])
    wv_tok_d = mk.din("wvtokd", [NCH, PART, 1024])
    bqkv_d = mk.din("bqkvd", [PART, 16])
    bv_row_d = mk.din("bvrowd", [1, 1024])
    bg_d = mk.din("bgd", [PART, 8])
    bproj_d = mk.din("bprojd", [PART, 8])
    lng_d = mk.din("lngd", [PART, 8])
    lnb_d = mk.din("lnbd", [PART, 8])
    mask_d = mk.din("c_mask", [PART, PART], BF16)
    eye_d = mk.din("c_eye", [PART, PART], BF16)
    flag_d = mk.din("c_flag", [PART, 1])
    eps_d = mk.din("c_eps", [PART, 1])
    onc_d = mk.din("c_ones_col", [PART, 1], BF16)
    onr_d = mk.din("c_ones_row", [1, PART], BF16)

    st_src = mk.dtmp("st_src", [HD, H * 65], BF16)
    st_dst = mk.dtmp("st_dst", [2 * HD, H * 65], BF16)
    groups = [[2 * i, 2 * i + 1] for i in range(n_pairs)]

    arena = mk.sbuf("arena", [PART, 196 * KB], mybir.dt.uint8)

    def view(off_kb, kb, dtype, shape=None):
        ap = arena[:, int(off_kb * KB):int((off_kb + kb) * KB)].bitcast(dtype)
        if shape is None:
            return ap
        return ap.rearrange("p (a b) -> p a b", a=shape[0])

    xi = view(0, 64, F32, [NCH, T])
    q2 = view(0, 32, BF16, [NCH, T])
    v_t4 = arena[:, 32 * KB:32 * KB + L0A_NC * H * 64 * 2].bitcast(BF16).rearrange(
        "p (c h a) -> p c h a", c=L0A_NC, h=H)
    o_t = view(0, 64, F32, [NCH, T])
    xn = view(64, 32, BF16, [NCH, T])
    k2t = view(64, 32, BF16).rearrange("p (c g d) -> p c g d", c=L0A_NC, g=8)
    lnscrF_b = view(64, 8, BF16, [NCH, 512])
    lnscrF_sq = view(72, 8, BF16, [NCH, 512])
    x2_rl = [view(80, 8, F32), view(88, 8, F32)]
    xib = view(96, 32, BF16, [NCH, T])
    k2f = view(96, 32, BF16, [NCH, T])
    tmp2 = view(128, 8, F32, [2, 1024])
    mubc = view(136, 4, BF16)
    rbc = view(140, 4, BF16)
    gate_f = view(128, 32, BF16, [NCH, T])
    attn = view(128, 32, BF16, [NCH, T])
    lnscr_b = view(160, 8, BF16, [NCH, 512])
    lnscr_sq = view(168, 8, BF16, [NCH, 512])
    wtokb = view(160, 16, BF16, [NCH, 1024])
    rgbc = view(176, 4, BF16)
    sm_stage = [arena[:, 176 * KB:176 * KB + 256].bitcast(BF16),
                arena[:, 176 * KB + 256:176 * KB + 512].bitcast(BF16)]
    rb_dv = view(177, 1, BF16)
    delta_st = [arena[:, 178 * KB:178 * KB + 130].bitcast(BF16),
                arena[:, 178 * KB + 256:178 * KB + 386].bitcast(BF16)]
    bvrow = view(180, 2, BF16)[0:1, 0:1024]
    mu_f = view(182, 2, F32)[0:1, 0:512]
    tA = view(184, 2, F32)[0:1, 0:512]
    tB = view(186, 2, F32)[0:1, 0:512]
    r_f = view(188, 2, F32)[0:1, 0:512]
    mu_bf = view(190, 1, BF16)[0:1, 0:512]
    r_bf = view(191, 1, BF16)[0:1, 0:512]
    wbufs = [view(192, 2, BF16), view(194, 2, BF16)]

    state_h = mk.sbuf("state_h", [PART, L0A_NC, 65], BF16)
    ends = mk.sbuf("ends", [HD, H * 65], BF16)
    init_st = mk.sbuf("init_st", [PART, H, 65], BF16)
    rcv_sb = mk.sbuf("rcv_sb", [HD, H * 65], BF16)
    ones_col = mk.sbuf("ones_col", [PART, 1], BF16)
    ones_row = mk.sbuf("ones_row", [1, PART], BF16)
    epsc = mk.sbuf("epsc", [PART, 1], F32)
    flagc = mk.sbuf("flagc", [PART, 1], F32)
    maskt = mk.sbuf("maskt", [PART, PART], BF16)
    eyet = mk.sbuf("eyet", [PART, PART], BF16)
    bqkv = mk.sbuf("bqkv", [PART, 16], F32)
    bg_c = mk.sbuf("bg_c", [PART, 8], F32)
    bproj_c = mk.sbuf("bproj_c", [PART, 8], F32)
    lng_c = mk.sbuf("lng_c", [PART, 8], F32)
    lnb_c = mk.sbuf("lnb_c", [PART, 8], F32)

    psum = [mk.psum(f"pb{i}", [PART, 512], F32) for i in range(6)]
    pt = [mk.psum("pt0", [PART, 128], BF16), mk.psum("pt1", [PART, 128], BF16)]
    ps_guard = [None] * 6
    ps_rr = [0]

    def ps_acquire():
        i = ps_rr[0] % 6
        ps_rr[0] += 1
        g = ps_guard[i]
        if g is not None:
            mk.pe.wait(g[0], g[1])
            ps_guard[i] = None
        return i

    def ps_release(i, sem):
        ps_guard[i] = (sem, sem.n)

    class PPShim:
        acquire = staticmethod(ps_acquire)
        release = staticmethod(ps_release)
        banks = psum
    pp = PPShim()

    lnt = LNTiles(mu_f, tA, tB, r_f, mu_bf, r_bf, mubc, rbc)
    ws = WStream(mk, S, "w", wbufs, [S["wb0"], S["wb1"]])

    for t_, d_ in [(ones_col, onc_d), (ones_row, onr_d), (epsc, eps_d),
                   (flagc, flag_d), (maskt, mask_d), (eyet, eye_d),
                   (bqkv, bqkv_d), (bg_c, bg_d),
                   (bproj_c, bproj_d), (lng_c, lng_d), (lnb_c, lnb_d)]:
        mk.sp.dma(t_[:, :], d_[:, :], inc=S["s_xdma"])
    consts_loaded = S["s_xdma"].n
    mk.pool.dma(bvrow, bv_row_d[:, :], inc=S["wtk"])
    bvrow_loaded = S["wtk"].n
    for c in range(NCH):
        mk.sp.dma(xi[:, c, :], x_d[c * PART:(c + 1) * PART, :], inc=S["s_xs"])
    x_loaded = S["s_xs"].n

    mk.pool.wait(S["s_xdma"], consts_loaded)
    bc = emit_ln_stats(mk, S, pp, lnt, xi, [lnscr_b, lnscrF_b],
                       [lnscr_sq, lnscrF_sq], NCH, T,
                       ones_col, ones_row, epsc, ready=[(S["s_xs"], x_loaded)])
    mk.pool.wait(S["s_act"], bc)
    uses = []
    i = 0
    for c in range(NCH):
        for hf in range(2):
            hsl = slice(hf * 1024, (hf + 1) * 1024)
            ppg = i % 2
            if i >= 2:
                mk.pool.wait(S["s_dve"], uses[i - 2])
            mk.pool.op("tensor_tensor", tmp2[:, ppg, :], xi[:, c, hsl],
                       mubc[:, hsl], ALU.subtract, inc=S["s_pool"])
            mk.dve.wait(S["s_pool"], S["s_pool"].n)
            mk.dve.op("tensor_tensor", xi[:, c, hsl], tmp2[:, ppg, :],
                      rbc[:, hsl], ALU.mult, inc=S["s_dve"])
            uses.append(S["s_dve"].n)
            i += 1
    xi_done = S["s_dve"].n

    bc = emit_ln_stats(mk, S, pp, lnt, xi, [lnscr_b, lnscrF_b],
                       [lnscr_sq, lnscrF_sq], NCH, T,
                       ones_col, ones_row, epsc, ready=[(S["s_dve"], xi_done)])
    xn_done = emit_ln_apply(mk, S, lnt, xi, tmp2, bc, NCH, dst_bf=xn,
                            gamma=lng_c, beta=lnb_c, W=T)
    mk.pool.wait(S["s_dve"], xn_done)
    for c in range(NCH):
        mk.pool.op("tensor_copy", xib[:, c, :], xi[:, c, :],
                   inc=(S["s_pool"] if c == NCH - 1 else None))
    xib_done = S["s_pool"].n

    def gemm_fm(x_src, w_dram, evac, gates=()):
        for slab in range(8):
            seq = ws.fetch(w_dram[slab].rearrange("p c o -> p (c o)"),
                           gate=ws.gates.get(ws.count - 2))
            ws.wait_pe(seq)
            for g, v in gates:
                mk.pe.wait(g, v)
            buf = ws.buf(seq).rearrange("p (c o) -> p c o", c=8)
            oc = slab
            for tt in range(4):
                tsl = slice(tt * 512, (tt + 1) * 512)
                b0 = ps_acquire()
                for kc in range(NCH):
                    mk.pe.op("matmul", psum[b0][:, :],
                             buf[:, kc, :].opt(),
                             x_src[:, kc, tsl], start=(kc == 0),
                             stop=(kc == NCH - 1),
                             inc=(S["s_pe"] if kc == NCH - 1 else None))
                evac(oc, tt, b0, S["s_pe"].n)
            ws.gates[seq] = S["s_pe"].n

    def evac_gate(oc, tt, b0, mmd):
        mk.act.wait(S["s_pe"], mmd)
        mk.act.op("activation", gate_f[:, oc, tt * 512:(tt + 1) * 512],
                  psum[b0][:, :], AF.Sigmoid, bias=bg_c[:, oc:oc + 1],
                  inc=S["s_act"])
        ps_release(b0, S["s_act"])
    gemm_fm(xib, wg_d, evac_gate,
            gates=[(S["s_pool"], xib_done), (S["s_dve"], xn_done)])
    gate_pe_end = S["s_pe"].n
    gatef_done = S["s_act"].n

    for tt in range(4):
        tsl = slice(tt * 512, (tt + 1) * 512)
        ba = ps_acquire()
        mk.pe.wait(S["s_act"], gatef_done)
        rows = psum[ba][0:1, :]
        for c in range(NCH):
            mk.pe.op("matmul", rows, ones_col[:, :], gate_f[:, c, tsl],
                     start=(c == 0), stop=(c == NCH - 1),
                     inc=(S["s_pe"] if c == NCH - 1 else None))
        mk.dve.wait(S["s_pe"], S["s_pe"].n)
        mk.dve.op("tensor_scalar", mu_f[:, :], rows, 1.0 / D, L0A_EPS,
                  ALU.mult, ALU.add, chain=True)
        mk.dve.op("reciprocal", mu_f[:, :], mu_f[:, :], chain=True)
        mk.dve.op("tensor_copy", mu_bf[:, :], mu_f[:, :], chain=True,
                  inc=S["s_dve"])
        ps_release(ba, S["s_dve"])
        bb = ps_acquire()
        mk.pe.wait(S["s_dve"], S["s_dve"].n)
        mk.pe.op("matmul", psum[bb][:, :], ones_row[:, :], mu_bf[:, :],
                 start=True, stop=True, inc=S["s_pe"])
        mk.act.wait(S["s_pe"], S["s_pe"].n)
        mk.act.op("activation", rgbc[:, tsl], psum[bb][:, :], AF.Copy,
                  inc=S["s_act"])
        ps_release(bb, S["s_act"])
    rgbc_done = S["s_act"].n

    def mk_evac(dst, bias_c, boff):
        def evac(oc, tt, b0, mmd):
            mk.act.wait(S["s_pe"], mmd)
            mk.act.op("activation", dst[:, oc, tt * 512:(tt + 1) * 512],
                      psum[b0][:, :], AF.Identity,
                      bias=bias_c[:, boff + oc:boff + oc + 1], inc=S["s_act"])
            ps_release(b0, S["s_act"])
        return evac
    gemm_fm(xn, wq_d, mk_evac(q2, bqkv, 0), gates=[(S["s_dve"], xn_done)])
    q_done = S["s_act"].n

    def relu2_fm(tile, gdone):
        mk.pool.wait(S["s_act"], gdone)
        mk.pool.wait(S["s_act"], rgbc_done)
        for c in range(NCH):
            mk.pool.op("tensor_tensor", tile[:, c, :], tile[:, c, :],
                       gate_f[:, c, :], ALU.mult, chain=True,
                       inc=S["s_pool"])
            mk.pool.op("tensor_tensor", tile[:, c, :], tile[:, c, :],
                       rgbc[:, :], ALU.mult, chain=True, inc=S["s_pool"])
            mk.dve.wait(S["s_pool"], S["s_pool"].n)
            mk.dve.op("scalar_tensor_tensor", tile[:, c, :], tile[:, c, :],
                      0.0, tile[:, c, :], ALU.max, ALU.mult, chain=True)
            mk.dve.op("tensor_scalar", tile[:, c, :], tile[:, c, :],
                      L0A_EPS, None, ALU.add, chain=True, inc=S["s_dve"])
        return S["s_dve"].n

    q2_done = relu2_fm(q2, q_done)

    for kc in range(NCH):
        mk.pool.dma(wtokb[:, kc, :], wv_tok_d[kc], inc=S["wtk"])
    wv_loaded = S["wtk"].n

    for tc in range(L0A_NC):
        csl = slice(tc * 128, (tc + 1) * 128)
        b0 = ps_acquire()
        b1 = ps_acquire()
        mk.pe.wait(S["wtk"], wv_loaded)
        mk.pe.wait(S["wtk"], bvrow_loaded)
        mk.pe.wait(S["s_dve"], xn_done)
        for kc in range(NCH):
            lhs = xn[:, kc, csl].opt()
            mk.pe.op("matmul", psum[b0][:, :], lhs,
                     wtokb[:, kc, 0:512].opt(), start=(kc == 0), stop=False)
            mk.pe.op("matmul", psum[b1][:, :], lhs,
                     wtokb[:, kc, 512:1024].opt(), start=(kc == 0), stop=False)
        mk.pe.op("matmul", psum[b0][:, :], ones_row[:, 0:128],
                 bvrow[:, 0:512], start=False, stop=True)
        mk.pe.op("matmul", psum[b1][:, :], ones_row[:, 0:128],
                 bvrow[:, 512:1024], start=False, stop=True, inc=S["s_pe"])
        mmd = S["s_pe"].n
        mk.act.wait(S["s_pe"], mmd)
        mk.act.wait(S["s_pe"], gate_pe_end)
        mk.act.op("activation", v_t4[:, tc, 0:8, :],
                  psum[b0][:, :].rearrange("p (a b) -> p a b", a=8), AF.Copy)
        mk.act.op("activation", v_t4[:, tc, 8:16, :],
                  psum[b1][:, :].rearrange("p (a b) -> p a b", a=8), AF.Copy,
                  inc=S["s_act"])
        ps_release(b0, S["s_act"])
        ps_release(b1, S["s_act"])
    v_done = S["s_act"].n
    vtok_pe_end = S["s_pe"].n

    gemm_fm(xn, wk_d, mk_evac(k2f, bqkv, 8),
            gates=[(S["s_dve"], xn_done), (S["s_pe"], vtok_pe_end)])
    k_done = S["s_act"].n
    k2f_done = relu2_fm(k2f, k_done)

    mk.pe.wait(S["s_dve"], k2f_done)
    mk.pe.wait(S["s_dve"], q2_done)
    tp_hist = {}
    for idx in range(L0A_NC * NCH):
        tc = idx // NCH
        hp = idx % NCH
        b = idx % 2
        g = tp_hist.get(idx - 2)
        if g is not None:
            mk.pe.wait(S["s_act"], g)
        mk.pe.op("transpose", pt[b][:, :],
                 k2f[:, hp, tc * 128:(tc + 1) * 128].opt(), eyet[:, :],
                 inc=S["s_pe"])
        mk.act.wait(S["s_pe"], S["s_pe"].n)
        mk.act.op("activation", k2t[:, tc, hp, :], pt[b][:, :], AF.Copy,
                  inc=S["s_act"])
        tp_hist[idx] = S["s_act"].n
    k2t_done = S["s_act"].n

    bank_guard = {}

    def use_bank(i, eng=None):
        g = bank_guard.get(i)
        if g is not None:
            (eng or mk.pe).wait(g[0], g[1])
            bank_guard[i] = None

    for h in range(H):
        hp = h // 2
        hr = (h % 2) * 64
        bU = h % 4
        use_bank(bU)
        stp = psum[bU]
        mk.pe.wait(S["s_act"], k2t_done)
        mk.pe.wait(S["s_act"], v_done)
        for tc in range(L0A_NC):
            lhs = k2t[:, tc, hp, hr:hr + 64].opt()
            mk.pe.op("matmul", stp[0:64, 0:64], lhs, v_t4[:, tc, h, :].opt(),
                     start=(tc == 0), stop=False)
            mk.pe.op("matmul", stp[0:64, 64:65], lhs, ones_col[:, :],
                     start=False, stop=(tc == L0A_NC - 1),
                     inc=(S["s_pe"] if tc == L0A_NC - 1 else None))
        mk.act.wait(S["s_pe"], S["s_pe"].n)
        mk.act.op("activation", ends[:, h * 65:(h + 1) * 65], stp[0:64, 0:65],
                  AF.Copy, inc=S["s_act"])
        bank_guard[bU] = (S["s_act"], S["s_act"].n)
    ends_done = S["s_act"].n

    mk.pool.wait(S["s_act"], ends_done)
    mk.pool.dma(st_src[:, :], ends[:, :], inc=S["rcv"])
    mk.pool.wait(S["rcv"], S["rcv"].n)
    mk.pool.op("collective_compute", "AllGather", ALU.bypass,
               replica_groups=groups, ins=[st_src[:, :].opt()],
               outs=[st_dst[:, :].opt()], inc=S["cc1"], inc_by=1,
               inc_default=True)
    mk.pool.wait(S["cc1"], S["cc1"].n)
    mk.pool.dma(rcv_sb[:, :], st_dst[0:HD, :], inc=S["rcv"])
    mk.pool.wait(S["rcv"], S["rcv"].n)
    mk.pool.wait(S["s_xdma"], consts_loaded)
    mk.pool.op("tensor_scalar",
               init_st[0:HD, :, :].rearrange("p a b -> p (a b)"),
               rcv_sb[:, :], flagc[0:HD, 0:1], None, ALU.mult,
               inc=S["s_pool"])
    mk.pool.op("tensor_scalar",
               init_st[HD:PART, :, :].rearrange("p a b -> p (a b)"),
               rcv_sb[:, :], flagc[0:HD, 0:1], None, ALU.mult,
               inc=S["s_pool"])
    init_done = S["s_pool"].n

    mk.pe.wait(S["s_pool"], init_done)
    for h in range(H):
        hp = h // 2
        hr = (h % 2) * 64
        bU = 0
        use_bank(bU)
        stp = psum[bU]
        last_state_evac = 0
        mk.dve.wait(S["s_pe"], S["s_pe"].n)
        for cg in range(4):
            bAV = 1 + (cg % 2)
            use_bank(bAV)
            av = psum[bAV]
            gsl = slice(cg * 512, (cg + 1) * 512)
            for ci in range(4):
                c = cg * 4 + ci
                csl = slice(c * 128, (c + 1) * 128)
                fo = slice(ci * 128, (ci + 1) * 128)
                kfs = k2f[hr:hr + 64, hp, csl].opt()
                qfs = q2[hr:hr + 64, hp, csl].opt()
                ktk = k2t[:, c, hp, hr:hr + 64].opt()
                vtk = v_t4[:, c, h, :].opt()
                smb = sm_stage[c % 2]
                first = (ci == 0)
                bS = 3 + (ci % 2)
                use_bank(bS)
                mk.pe.op("matmul", psum[bS][0:128, 0:128], kfs, qfs,
                         start=True, stop=True, inc=S["s_pe"])
                s_mmd = S["s_pe"].n
                mk.dve.wait(S["s_pe"], s_mmd)
                if h * L0A_NC + c >= 2:
                    mk.dve.wait(S["s_pe"], sm_used.get((h * L0A_NC + c) - 2, 0))
                mk.dve.op("tensor_tensor", smb[:, :],
                          psum[bS][0:128, 0:128], maskt[:, :], ALU.mult,
                          inc=S["s_dve"])
                sm_done = S["s_dve"].n
                bank_guard[bS] = (S["s_dve"], sm_done)
                mk.pe.wait(S["s_dve"], sm_done)
                if c > 0:
                    mk.pe.wait(S["s_dve"], last_state_evac)
                mk.pe.op("matmul", av[0:65, fo], init_st[hr:hr + 64, h, :].opt(),
                         qfs, start=first, stop=False)
                mk.pe.op("matmul", av[0:64, fo], vtk, smb[:, :],
                         start=False, stop=False)
                if c > 0:
                    mk.pe.op("matmul", av[64:65, fo], ones_col[:, :],
                             smb[:, :], start=False, stop=False)
                    mk.pe.op("matmul", av[0:65, fo],
                             state_h[hr:hr + 64, c - 1, :].opt(), qfs,
                             start=False, stop=(ci == 3), inc=S["s_pe"])
                else:
                    mk.pe.op("matmul", av[64:65, fo], ones_col[:, :],
                             smb[:, :], start=False, stop=(ci == 3),
                             inc=S["s_pe"])
                sm_used[h * L0A_NC + c] = S["s_pe"].n
                if c < L0A_NC - 1:
                    use_bank(bU)
                    mk.pe.op("matmul", stp[0:64, 0:64], ktk, vtk,
                             start=True, stop=False)
                    mk.pe.op("matmul", stp[64:128, 0:64], ktk, vtk,
                             start=True, stop=False, tile_position=(0, 64),
                             skip_group_check=True)
                    mk.pe.op("matmul", stp[0:64, 64:65], ktk,
                             ones_col[:, :], start=False, stop=True)
                    mk.pe.op("matmul", stp[64:128, 64:65], ktk,
                             ones_col[:, :], start=False, stop=True,
                             tile_position=(0, 64), skip_group_check=True,
                             inc=S["s_pe"])
                    dsb = delta_st[c % 2]
                    mk.act.wait(S["s_pe"], S["s_pe"].n)
                    if h * L0A_NC + c >= 2:
                        mk.act.wait(S["s_dve"], delta_used.get(
                            (h * L0A_NC + c) - 2, 0))
                    mk.act.op("activation", dsb[:, 0:65],
                              stp[0:128, 0:65], AF.Copy, inc=S["s_act"])
                    bank_guard[bU] = (S["s_act"], S["s_act"].n)
                    mk.dve.wait(S["s_act"], S["s_act"].n)
                    if c == 0:
                        mk.dve.op("tensor_copy", state_h[:, c, :],
                                  dsb[:, 0:65], inc=S["s_dve"])
                    else:
                        mk.dve.wait(S["s_dve"], prefix_done.get(
                            (h, c - 1), 0))
                        mk.dve.op("tensor_tensor", state_h[:, c, :],
                                  state_h[:, c - 1, :], dsb[:, 0:65],
                                  ALU.add, inc=S["s_dve"])
                    delta_used[h * L0A_NC + c] = S["s_dve"].n
                    prefix_done[(h, c)] = S["s_dve"].n
                    last_state_evac = S["s_dve"].n
            av_mmd = S["s_pe"].n
            mk.dve.wait(S["s_pe"], av_mmd)
            mk.dve.op("reciprocal", r_f[:, :], av[64:65, :], chain=True)
            mk.dve.op("tensor_copy", r_bf[:, :], r_f[:, :], chain=True,
                      inc=S["s_dve"])
            bdv = 5
            use_bank(bdv)
            mk.pe.wait(S["s_dve"], S["s_dve"].n)
            mk.pe.op("matmul", psum[bdv][:, :], ones_row[:, :], r_bf[:, :],
                     start=True, stop=True, inc=S["s_pe"])
            mk.act.wait(S["s_pe"], S["s_pe"].n)
            mk.act.op("activation", rb_dv[:, :], psum[bdv][:, :], AF.Copy,
                      inc=S["s_act"])
            bank_guard[bdv] = (S["s_act"], S["s_act"].n)
            mk.dve.wait(S["s_act"], S["s_act"].n)
            mk.dve.op("tensor_tensor", attn[hr:hr + 64, hp, gsl],
                      av[0:64, :], rb_dv[0:64, :], ALU.mult, inc=S["s_dve"])
            bank_guard[bAV] = (S["s_dve"], S["s_dve"].n)
        bank_guard[bU] = (S["s_pe"], S["s_pe"].n)
    attn_done = S["s_dve"].n

    def evac_o(oc, tt, b0, mmd):
        mk.act.wait(S["s_pe"], mmd)
        mk.act.op("activation", o_t[:, oc, tt * 512:(tt + 1) * 512],
                  psum[b0][:, :], AF.Identity, bias=bproj_c[:, oc:oc + 1],
                  inc=S["s_act"])
        ps_release(b0, S["s_act"])
    for i_ in range(6):
        bank_guard[i_] = (S["s_dve"], attn_done)
    ps_rr[0] = 0
    for i_ in range(6):
        ps_guard[i_] = bank_guard.get(i_)
    gemm_fm(attn, wproj_d, evac_o, gates=[(S["s_dve"], attn_done)])
    o_done = S["s_act"].n

    bc = emit_ln_stats(mk, S, pp, lnt, o_t, [lnscrF_b, lnscr_b],
                       [lnscrF_sq, lnscr_sq], NCH, T,
                       ones_col, ones_row, epsc, ready=[(S["s_act"], o_done)])
    x2_sems = [S["g0"], S["g1"]]
    x2_loaded = {}
    x2_freed = {}

    def x2_reload(c, gate=None):
        b = c % 2
        mk.sp.wait(S["s_act"], o_done)
        if gate is not None:
            mk.sp.wait(S["s_dve"], gate)
        mk.sp.dma(x2_rl[b][:, :], x_d[c * PART:(c + 1) * PART, :],
                  inc=x2_sems[b])
        x2_loaded[c] = (x2_sems[b], x2_sems[b].n)

    x2_reload(0)
    x2_reload(1)
    mk.pool.wait(S["s_act"], bc)
    tmp_use = {0: 0, 1: 0}
    for c in range(NCH):
        b = c % 2
        for hf in range(2):
            hsl = slice(hf * 1024, (hf + 1) * 1024)
            mk.pool.wait(S["s_dve"], tmp_use[hf])
            mk.pool.op("tensor_tensor", tmp2[:, hf, :], o_t[:, c, hsl],
                       mubc[:, hsl], ALU.subtract, inc=S["s_pool"])
            mk.dve.wait(S["s_pool"], S["s_pool"].n)
            mk.dve.wait(*x2_loaded[c])
            mk.dve.op("tensor_tensor", tmp2[:, hf, :], tmp2[:, hf, :],
                      rbc[:, hsl], ALU.mult)
            mk.dve.op("tensor_tensor", o_t[:, c, hsl], x2_rl[b][:, hsl],
                      tmp2[:, hf, :], ALU.add, chain=True, inc=S["s_dve"])
            tmp_use[hf] = S["s_dve"].n
        yc_done = S["s_dve"].n
        x2_freed[c] = yc_done
        mk.sp.wait(S["s_dve"], yc_done)
        mk.sp.dma(y_d[c * PART:(c + 1) * PART, :], o_t[:, c, :],
                  inc=S["s_odma"])
        if c + 2 < NCH:
            x2_reload(c + 2, gate=x2_freed[c])

    mk.sp.wait(S["s_odma"], S["s_odma"].n)
    mk.sp.flush()
    mk.pe.flush(); mk.act.flush(); mk.dve.flush(); mk.pool.flush()


sm_used = {}
delta_used = {}
prefix_done = {}


def make_slabs(w, kcs, ow):
    K, N = w.shape
    n_slab = N // ow
    return np.ascontiguousarray(
        w.reshape(kcs, 128, n_slab, ow).transpose(2, 1, 0, 3))


def host_prep_l0attn(x_core, inp, h):
    d = {}
    d["x"] = np.ascontiguousarray(x_core.T)
    Wqkv = np.asarray(inp["L0_attn_W_qkv"])
    bqkv = np.asarray(inp["L0_attn_b_qkv"])
    d["wqd"] = make_slabs(np.ascontiguousarray(Wqkv[:, 0:1024]), 8, 128)
    d["wkd"] = make_slabs(np.ascontiguousarray(Wqkv[:, 1024:2048]), 8, 128)
    d["wgd"] = make_slabs(np.asarray(inp["L0_attn_W_gate"]), 8, 128)
    d["wprojd"] = make_slabs(np.asarray(inp["L0_attn_W_proj"]), 8, 128)
    d["wvtokd"] = np.ascontiguousarray(Wqkv[:, 2048:3072].reshape(8, 128, 1024))
    d["bqkvd"] = np.ascontiguousarray(
        np.concatenate([bqkv[0:1024].reshape(8, 128).T,
                        bqkv[1024:2048].reshape(8, 128).T], axis=1))
    d["bvrowd"] = bqkv[2048:3072].reshape(1, 1024).astype(np.float32)
    d["bgd"] = np.ascontiguousarray(
        np.asarray(inp["L0_attn_b_gate"]).reshape(8, 128).T)
    d["bprojd"] = np.ascontiguousarray(
        np.asarray(inp["L0_attn_b_proj"]).reshape(8, 128).T)
    d["lngd"] = np.ascontiguousarray(
        np.asarray(inp["L0_attn_ln_g"]).reshape(8, 128).T)
    d["lnbd"] = np.ascontiguousarray(
        np.asarray(inp["L0_attn_ln_b"]).reshape(8, 128).T)
    import ml_dtypes
    d["c_mask"] = np.triu(np.ones((128, 128))).astype(ml_dtypes.bfloat16)
    d["c_eye"] = np.eye(128).astype(ml_dtypes.bfloat16)
    d["c_flag"] = np.full((128, 1), float(h), np.float32)
    d["c_eps"] = np.full((128, 1), 1e-5, np.float32)
    d["c_ones_col"] = np.ones((128, 1), ml_dtypes.bfloat16)
    d["c_ones_row"] = np.ones((1, 128), ml_dtypes.bfloat16)
    return d




"""L0_ffn per-layer launch builders (host-sliced halos, no collectives)."""

KB = 1024
D = 1024
NCH = 8
T = 2048
L0F_TE = 2304
L0F_NTC = 18
H = 16
HD = 64
L0F_NQT = 4
OFFS = [-256, -128, 0, 128, 256, 384]


def emit_l0f_attn(mk, xh_d, y_d, tail=None):
    S = mk.std_sems(extra=["wb0", "wb1", "wtk", "g0", "g1", "s_xs", "s_tb",
                           "s_xa"])

    wq_d = mk.din("wqd", [4, PART, 8, 256])
    wk_d = mk.din("wkd", [4, PART, 8, 256])
    wv_tok_d = mk.din("wvtokd", [NCH, PART, 1024])
    wo_d = mk.din("wod", [4, PART, 8, 256])
    csq_d = mk.din("csqd", [PART, L0F_TE], BF16)
    ssq_d = mk.din("ssqd", [PART, L0F_TE], BF16)
    csk_d = mk.din("cskd", [PART, L0F_TE], BF16)
    ssk_d = mk.din("sskd", [PART, L0F_TE], BF16)
    qs_d = mk.din("qsd", [PART, 4])
    rt_d = mk.din("c_rt", [PART, PART], BF16)
    e16_d = mk.din("c_e16", [PART, NCH * 16], BF16)
    sel_d = mk.din("c_sel", [16, NCH * PART], BF16)
    mpre_d = mk.din("c_mpre", [8, PART, 512], BF16)
    eye_d = mk.din("c_eye", [PART, PART], BF16)
    onc_d = mk.din("c_ones_col", [PART, 1], BF16)
    onr_d = mk.din("c_ones_row", [1, PART], BF16)

    arena = mk.sbuf("arena", [PART, 196 * KB], mybir.dt.uint8)

    def view(off_kb, kb, dtype, shape=None):
        ap = arena[:, int(off_kb * KB):int((off_kb + kb) * KB)].bitcast(dtype)
        if shape is None:
            return ap
        return ap.rearrange("p (a b) -> p a b", a=shape[0])

    q_t = view(0, 32, BF16, [NCH, T])
    v_t4 = arena[:, 32 * KB:32 * KB + L0F_NTC * H * 65 * 2].bitcast(BF16).rearrange(
        "p (c h a) -> p c h a", c=L0F_NTC, h=H)
    exp_st = view(69, 2, F32)
    smB = view(71, 6, BF16).rearrange("p (a b) -> p a b", a=6)
    rnb_t = view(77, 5, BF16)
    t2_st = view(82, 1, BF16)
    rb_dv = view(83, 1, BF16)
    xrl = [view(84, 8, F32), view(92, 8, F32)]
    sx = view(100, 36, BF16, [NCH, L0F_TE])
    attn = view(100, 32, BF16, [NCH, T])
    k_t = view(136, 36, BF16, [NCH, L0F_TE])
    wtokb = view(172, 16, BF16, [NCH, 1024])
    tblslc = [view(172, 1, BF16), view(173, 1, BF16),
              view(174, 1, BF16), view(175, 1, BF16)]
    wbufs = [view(188, 4, BF16), view(192, 4, BF16)]
    mpre_t = view(84, 8, BF16).rearrange("p (a b) -> p a b", a=8)
    rb_dvs = [view(83, 1, BF16), view(79, 1, BF16)]
    csq_r = view(119, 4, BF16)
    ssq_r = view(123, 4, BF16)
    csk_r = view(127, 4.5, BF16)
    ssk_r = view(131.5, 4.5, BF16)

    rows16 = view(100, 10, F32)[0:16, :]
    rnq_bf = view(110, 4, BF16)[0:16, :]
    rnk_bf = view(114, 5, BF16)[0:16, :]
    e16 = mk.sbuf("e16", [PART, NCH, 16], BF16)
    sel = mk.sbuf("sel", [16, NCH, PART], BF16)
    rt_t = mk.sbuf("rt_t", [PART, PART], BF16)
    eyet = mk.sbuf("eyet", [PART, PART], BF16)
    qs_c = mk.sbuf("qs_c", [PART, 4], F32)
    ones_col = mk.sbuf("ones_col", [PART, 1], BF16)
    ones_row = mk.sbuf("ones_row", [1, PART], BF16)
    r_f = mk.sbuf("r_f", [1, 512], F32)
    r_bf = mk.sbuf("r_bf", [1, 512], BF16)
    r_bf2 = mk.sbuf("r_bf2", [1, 512], BF16)
    r_bfs = [r_bf, r_bf2]

    psum = [mk.psum(f"pb{i}", [PART, 512], F32) for i in range(8)]
    ps_guard = [None] * 8
    ps_rr = [0]

    def ps_acquire():
        i = 3 + (ps_rr[0] % 5)
        ps_rr[0] += 1
        g = ps_guard[i]
        if g is not None:
            mk.pe.wait(g[0], g[1])
            ps_guard[i] = None
        return i

    def use_bank(i):
        g = ps_guard[i]
        if g is not None:
            mk.pe.wait(g[0], g[1])
            ps_guard[i] = None

    def ps_release(i, sem):
        ps_guard[i] = (sem, sem.n)

    ws = WStream(mk, S, "w", wbufs, [S["wb0"], S["wb1"]])

    mk.sp.dma(e16[:, :, :].rearrange("p a b -> p (a b)"), e16_d[:, :],
              inc=S["s_xdma"])
    mk.sp.dma(sel[:, :, :].rearrange("p a b -> p (a b)"), sel_d[:, :],
              inc=S["s_xdma"])
    mk.sp.dma(rt_t[:, :], rt_d[:, :], inc=S["s_xdma"])
    mk.sp.dma(eyet[:, :], eye_d[:, :], inc=S["s_xdma"])
    mk.sp.dma(qs_c[:, :], qs_d[:, :], inc=S["s_xdma"])
    mk.sp.dma(ones_col[:, :], onc_d[:, :], inc=S["s_xdma"])
    mk.sp.dma(ones_row[:, :], onr_d[:, :], inc=S["s_xdma"])
    consts_loaded = S["s_xdma"].n

    for c in range(NCH):
        off = 2 if c < 4 else 1
        mk.pool.dma(sx[:, c, :],
                    xh_d[c * PART:(c + 1) * PART, off:off + L0F_TE],
                    inc=S["s_xa"])
    sx_done = S["s_xa"].n

    def gemm_in(w_dram, n_slab, dst, width, tok_off, gates=(), evac=None):
        for slab in range(n_slab):
            mk.pool.wait(S["s_xa"], sx_done)
            seq = ws.fetch(w_dram[slab].rearrange("p c o -> p (c o)"),
                           gate=ws.gates.get(ws.count - 2))
            ws.wait_pe(seq)
            for g, v in gates:
                mk.pe.wait(g, v)
            buf = ws.buf(seq).rearrange("p (c o) -> p c o", c=8)
            for m in range(2):
                oc = slab * 2 + m
                for tt in range((width + 511) // 512):
                    w0 = tt * 512
                    wdt = min(512, width - w0)
                    b0 = ps_acquire()
                    for kc in range(NCH):
                        mk.pe.op("matmul", psum[b0][:, 0:wdt],
                                 buf[:, kc, m * 128:(m + 1) * 128].opt(),
                                 sx[:, kc, tok_off + w0:tok_off + w0 + wdt],
                                 start=(kc == 0), stop=(kc == NCH - 1),
                                 inc=(S["s_pe"] if kc == NCH - 1 else None))
                    mk.act.wait(S["s_pe"], S["s_pe"].n)
                    mk.act.op("activation", dst[:, oc, w0:w0 + wdt],
                              psum[b0][:, 0:wdt], AF.Copy, inc=S["s_act"])
                    ps_release(b0, S["s_act"])
            ws.gates[seq] = S["s_pe"].n

    gemm_in(wq_d, 4, q_t, T, 256)
    q_raw_done = S["s_act"].n
    gemm_in(wk_d, 4, k_t, L0F_TE, 0)
    k_raw_done = S["s_act"].n
    qk_pe_end = S["s_pe"].n

    mk.pool.wait(S["s_pe"], qk_pe_end)
    for kc in range(NCH):
        mk.pool.dma(wtokb[:, kc, :], wv_tok_d[kc], inc=S["wtk"])
    wv_loaded = S["wtk"].n
    mk.pool.op("memset", v_t4[:, :, :, 64:65], 1.0, inc=S["s_pool"])
    vones = S["s_pool"].n
    for tc in range(L0F_NTC):
        csl = slice(tc * 128, (tc + 1) * 128)
        b0 = ps_acquire()
        b1 = ps_acquire()
        mk.pe.wait(S["wtk"], wv_loaded)
        mk.pe.wait(S["s_xa"], sx_done)
        for kc in range(NCH):
            lhs = sx[:, kc, csl].opt()
            mk.pe.op("matmul", psum[b0][:, :], lhs,
                     wtokb[:, kc, 0:512].opt(), start=(kc == 0),
                     stop=(kc == NCH - 1))
            mk.pe.op("matmul", psum[b1][:, :], lhs,
                     wtokb[:, kc, 512:1024].opt(), start=(kc == 0),
                     stop=(kc == NCH - 1),
                     inc=(S["s_pe"] if kc == NCH - 1 else None))
        mmd = S["s_pe"].n
        mk.act.wait(S["s_pe"], mmd)
        mk.act.wait(S["s_pool"], vones)
        mk.act.op("activation", v_t4[:, tc, 0:8, 0:64],
                  psum[b0][:, :].rearrange("p (a b) -> p a b", a=8), AF.Copy)
        mk.act.op("activation", v_t4[:, tc, 8:16, 0:64],
                  psum[b1][:, :].rearrange("p (a b) -> p a b", a=8), AF.Copy,
                  inc=S["s_act"])
        ps_release(b0, S["s_act"])
        ps_release(b1, S["s_act"])
    v_done = S["s_act"].n
    v_pe_end = S["s_pe"].n

    t2s = [t2_st, view(69, 1, BF16)]
    scr_guard = {}

    def sumsq_rows(src_t, width, ready_val, rn_dst, idx0):
        idx = idx0
        for tt in range((width + 511) // 512):
            w0 = tt * 512
            wdt = min(512, width - w0)
            ba = ps_acquire()
            for c in range(NCH):
                slot = idx % 2
                mk.act.wait(S["s_act"], ready_val)
                g = scr_guard.get(slot)
                if g is not None:
                    mk.act.wait(g[0], g[1])
                mk.act.op("activation", t2s[slot][:, 0:wdt],
                          src_t[:, c, w0:w0 + wdt], AF.Square, inc=S["s_act"])
                mk.pe.wait(S["s_act"], S["s_act"].n)
                mk.pe.wait(S["s_xdma"], consts_loaded)
                mk.pe.op("matmul", psum[ba][0:16, 0:wdt], e16[:, c, :].opt(),
                         t2s[slot][:, 0:wdt], start=(c == 0),
                         stop=(c == NCH - 1), inc=S["s_pe"])
                scr_guard[slot] = (S["s_pe"], S["s_pe"].n)
                idx += 1
            mk.dve.wait(S["s_pe"], S["s_pe"].n)
            g = scr_guard.get("rows_war")
            if g is not None:
                mk.dve.wait(g[0], g[1])
            mk.dve.op("tensor_copy", rows16[0:16, w0:w0 + wdt],
                      psum[ba][0:16, 0:wdt], inc=S["s_dve"])
            ps_release(ba, S["s_dve"])
        mk.act.wait(S["s_dve"], S["s_dve"].n)
        mk.act.op("activation", rows16[0:16, 0:width],
                  rows16[0:16, 0:width], AF.Sqrt, inc=S["s_act"])
        mk.dve.wait(S["s_act"], S["s_act"].n)
        mk.dve.op("tensor_scalar", rows16[0:16, 0:width],
                  rows16[0:16, 0:width], 1e-12, None, ALU.max, chain=True)
        mk.dve.op("reciprocal", rows16[0:16, 0:width],
                  rows16[0:16, 0:width], chain=True)
        mk.dve.op("tensor_copy", rn_dst[0:16, 0:width],
                  rows16[0:16, 0:width], chain=True, inc=S["s_dve"])
        scr_guard["rows_war"] = (S["s_dve"], S["s_dve"].n)
        return S["s_dve"].n

    rnq_done = sumsq_rows(q_t, T, q_raw_done, rnq_bf, 0)
    rnk_done = sumsq_rows(k_t, L0F_TE, k_raw_done, rnk_bf, 0)

    mk.sp.wait(S["s_pe"], v_pe_end)
    mk.sp.wait(S["s_pe"], qk_pe_end)
    mk.sp.dma(csq_r[:, 0:T], csq_d[:, 256:], inc=S["s_tb"])
    mk.sp.dma(ssq_r[:, 0:T], ssq_d[:, 256:], inc=S["s_tb"])
    mk.sp.dma(csk_r[:, 0:L0F_TE], csk_d[:, :], inc=S["s_tb"])
    mk.sp.dma(ssk_r[:, 0:L0F_TE], ssk_d[:, :], inc=S["s_tb"])
    tbls_loaded = S["s_tb"].n

    tbl_used = {"n": 0}
    s1s = [t2_st, view(69, 1, BF16)]
    s2s = [view(70, 1, BF16), view(83, 1, BF16)]

    def rot_pass(src_t, width, rn_src, rn_ready, cs_t, ss_t, qs_col, qsp_col,
                 tag):
        nbt = (width + 511) // 512
        idx = 0
        rnb_free[(tag, 0)] = S["s_dve"].n
        for c in range(NCH):
            for tt in range(nbt):
                w0 = tt * 512
                wdt = min(512, width - w0)
                bb = ps_acquire()
                mk.pe.wait(S["s_dve"], rn_ready)
                mk.pe.wait(S["s_xdma"], consts_loaded)
                mk.pe.op("matmul", psum[bb][:, 0:wdt], sel[:, c, :].opt(),
                         rn_src[0:16, w0:w0 + wdt], start=True, stop=True,
                         inc=S["s_pe"])
                mk.act.wait(S["s_pe"], S["s_pe"].n)
                mk.act.wait(S["s_dve"], rnb_free.get((tag, c), 0))
                mk.act.op("activation", rnb_t[:, w0:w0 + wdt],
                          psum[bb][:, 0:wdt], AF.Copy, inc=S["s_act"])
                ps_release(bb, S["s_act"])
            rnb_done = S["s_act"].n
            for tt in range(nbt):
                w0 = tt * 512
                wdt = min(512, width - w0)
                zsl = src_t[:, c, w0:w0 + wdt]
                slot = idx % 2
                s1 = s1s[slot]
                s2 = s2s[slot]
                br = ps_acquire()
                mk.pe.op("matmul", psum[br][:, 0:wdt], rt_t[:, :], zsl,
                         start=True, stop=True, inc=S["s_pe"])
                rot_mm = S["s_pe"].n
                for gk in (("d1", slot), slot):
                    g = scr_guard.get(gk)
                    if g is not None:
                        mk.dve.wait(g[0], g[1])
                mk.dve.wait(S["s_tb"], tbls_loaded)
                mk.dve.op("scalar_tensor_tensor", s1[:, 0:wdt], zsl, qs_col,
                          cs_t[:, w0:w0 + wdt], ALU.mult, ALU.mult,
                          inc=S["s_dve"])
                mk.dve.wait(S["s_pe"], rot_mm)
                mk.dve.op("scalar_tensor_tensor", s2[:, 0:wdt],
                          psum[br][:, 0:wdt], qsp_col,
                          ss_t[:, w0:w0 + wdt], ALU.mult, ALU.mult,
                          chain=True, inc=S["s_dve"])
                ps_release(br, S["s_dve"])
                mk.dve.op("tensor_tensor", s1[:, 0:wdt], s1[:, 0:wdt],
                          s2[:, 0:wdt], ALU.add, chain=True)
                mk.dve.wait(S["s_act"], rnb_done)
                mk.dve.op("tensor_tensor", zsl, s1[:, 0:wdt],
                          rnb_t[:, w0:w0 + wdt], ALU.mult, chain=True,
                          inc=S["s_dve"])
                scr_guard[("d1", slot)] = (S["s_dve"], S["s_dve"].n)
                tbl_used["n"] = S["s_dve"].n
                idx += 1
            rnb_free[(tag, c + 1)] = S["s_dve"].n
        return S["s_dve"].n

    rnb_free = {}
    qrot_done = rot_pass(q_t, T, rnq_bf, rnq_done, csq_r, ssq_r,
                         qs_c[:, 0:1], qs_c[:, 1:2], "q")
    krot_done = rot_pass(k_t, L0F_TE, rnk_bf, rnk_done, csk_r, ssk_r,
                         qs_c[:, 2:3], qs_c[:, 3:4], "k")

    for i_ in range(8):
        mk.sp.dma(mpre_t[:, i_, :].opt(), mpre_d[i_], inc=S["s_xs"])
    masks_loaded = S["s_xs"].n

    units = [(h, qt) for h in range(H) for qt in range(L0F_NQT)]
    NU = len(units)
    for b_ in range(8):
        g_ = ps_guard[b_]
        if g_ is not None:
            mk.pe.wait(g_[0], g_[1])
            ps_guard[b_] = None
    sb_guard = {}
    av_guard = {}
    s_done = {}
    exp_written = {}
    av_done = {}
    avu_done = {}
    recip_done = {}
    bc_done = {}
    rb_evac = {}
    rb_read = {}
    exp_slot_guard = {}

    def emit_S(u, mis):
        h, qt = units[u]
        hp, hr, q0 = h // 2, (h % 2) * 64, qt * 512
        qfs = q_t[hr:hr + 64, hp, q0:q0 + 512].opt()
        if u == 0 and mis[0] == 0:
            mk.pe.wait(S["s_dve"], qrot_done)
            mk.pe.wait(S["s_dve"], krot_done)
            mk.pe.wait(S["s_xs"], masks_loaded)
            mk.pe.wait(S["s_xdma"], consts_loaded)
        for mi in mis:
            m = OFFS[mi]
            kk0 = ((q0 + m + 256) // 128) * 128
            kfs = k_t[hr:hr + 64, hp, kk0:kk0 + 128].opt()
            b = (u * 6 + mi) % 4
            g = sb_guard.get(b)
            if g is not None:
                mk.pe.wait(S["s_act"], g)
            tbl = (6 + mi) if (qt == 0 and mi < 2) else mi
            mk.pe.op("matmul", psum[b][0:128, 0:512], kfs, qfs,
                     start=True, stop=False)
            mk.pe.op("matmul", psum[b][0:128, 0:512], eyet[:, :],
                     mpre_t[:, tbl, :], start=False, stop=True,
                     inc=S["s_pe"])
            s_done[(u, mi)] = S["s_pe"].n

    def emit_E(u, mis):
        h, qt = units[u]
        for mi in mis:
            b = (u * 6 + mi) % 4
            mk.act.wait(S["s_pe"], s_done[(u, mi)])
            if u >= 1:
                mk.act.wait(S["s_pe"], av_done.get((u - 1, mi), 0))
            mk.act.op("activation", smB[:, mi, :], psum[b][0:128, 0:512],
                      AF.Exp, scale=8.0, inc=S["s_act"])
            sb_guard[b] = S["s_act"].n
            exp_written[(u, mi)] = S["s_act"].n

    def emit_AV(u):
        h, qt = units[u]
        q0 = qt * 512
        bAV = 5 + (u % 3)
        g = av_guard.get(bAV)
        if g is not None:
            mk.pe.wait(S["s_dve"], g)
        if u == 0:
            mk.pe.wait(S["s_act"], v_done)
        for mi, m in enumerate(OFFS):
            jc = (q0 + m + 256) // 128
            mk.pe.wait(S["s_act"], exp_written[(u, mi)])
            mk.pe.op("matmul", psum[bAV][0:65, 0:512],
                     v_t4[:, jc, h, :].opt(), smB[:, mi, :],
                     start=(mi == 0), stop=(mi == 5), inc=S["s_pe"])
            av_done[(u, mi)] = S["s_pe"].n
        avu_done[u] = S["s_pe"].n

    def emit_recip(u):
        bAV = 5 + (u % 3)
        mk.dve.wait(S["s_pe"], avu_done[u])
        mk.dve.wait(S["s_pe"], bc_done.get(u - 2, 0))
        mk.dve.op("reciprocal", r_f[:, :], psum[bAV][64:65, 0:512],
                  chain=True)
        mk.dve.op("tensor_copy", r_bfs[u % 2][:, :], r_f[:, :], chain=True,
                  inc=S["s_dve"])
        recip_done[u] = S["s_dve"].n

    def emit_bcast(u):
        mk.pe.wait(S["s_dve"], recip_done[u])
        mk.pe.wait(S["s_act"], rb_evac.get(u - 1, 0))
        mk.pe.op("matmul", psum[4][:, 0:512], ones_row[:, :],
                 r_bfs[u % 2][:, :], start=True, stop=True, inc=S["s_pe"])
        bc_done[u] = S["s_pe"].n

    def emit_divrest(u):
        h, qt = units[u]
        hp, hr, q0 = h // 2, (h % 2) * 64, qt * 512
        bAV = 5 + (u % 3)
        mk.act.wait(S["s_pe"], bc_done[u])
        mk.act.wait(S["s_dve"], rb_read.get(u - 2, 0))
        mk.act.op("activation", rb_dvs[u % 2][:, 0:512], psum[4][:, 0:512],
                  AF.Copy, inc=S["s_act"])
        rb_evac[u] = S["s_act"].n
        mk.dve.wait(S["s_act"], rb_evac[u])
        mk.dve.op("tensor_tensor", attn[hr:hr + 64, hp, q0:q0 + 512],
                  psum[bAV][0:64, 0:512], rb_dvs[u % 2][0:64, 0:512],
                  ALU.mult, inc=S["s_dve"])
        rb_read[u] = S["s_dve"].n
        av_guard[bAV] = S["s_dve"].n

    for u in range(NU):
        emit_S(u, [0, 1, 2, 3])
        if u >= 1:
            emit_AV(u - 1)
        emit_E(u, [0, 1, 2, 3])
        emit_S(u, [4, 5])
        if u >= 2:
            emit_bcast(u - 2)
        emit_E(u, [4, 5])
        if u >= 1:
            emit_recip(u - 1)
        if u >= 2:
            emit_divrest(u - 2)
    emit_AV(NU - 1)
    emit_bcast(NU - 2)
    emit_recip(NU - 1)
    emit_divrest(NU - 2)
    emit_bcast(NU - 1)
    emit_divrest(NU - 1)
    attn_done = S["s_dve"].n
    for i_ in range(8):
        ps_guard[i_] = (S["s_dve"], attn_done)

    x_sems = [S["g0"], S["g1"]]
    x_loaded = {}
    x_freed = {}

    slot_store = {}

    def x_reload(oc, gate=None):
        b = oc % 2
        mk.sp.wait(S["s_dve"], attn_done)
        mk.sp.wait(S["s_odma"], S["s_odma"].n)
        if gate is not None:
            mk.sp.wait(S["s_dve"], gate)
        mk.sp.dma(xrl[b][:, 0:T], xh_d[oc * PART:(oc + 1) * PART, 258:],
                  inc=x_sems[b])
        x_loaded[oc] = (x_sems[b], x_sems[b].n)

    x_reload(0)
    x_reload(1)
    for slab in range(4):
        mk.pool.wait(S["s_dve"], attn_done)
        seq = ws.fetch(wo_d[slab].rearrange("p c o -> p (c o)"),
                       gate=ws.gates.get(ws.count - 2))
        ws.wait_pe(seq)
        mk.pe.wait(S["s_dve"], attn_done)
        buf = ws.buf(seq).rearrange("p (c o) -> p c o", c=8)
        for m in range(2):
            oc = slab * 2 + m
            for tt in range(4):
                tsl = slice(tt * 512, (tt + 1) * 512)
                b0 = ps_acquire()
                for kc in range(NCH):
                    mk.pe.op("matmul", psum[b0][:, :],
                             buf[:, kc, m * 128:(m + 1) * 128].opt(),
                             attn[:, kc, tsl], start=(kc == 0),
                             stop=(kc == NCH - 1),
                             inc=(S["s_pe"] if kc == NCH - 1 else None))
                mmd = S["s_pe"].n
                mk.dve.wait(S["s_pe"], mmd)
                mk.dve.wait(*x_loaded[oc])
                mk.dve.op("tensor_tensor", xrl[oc % 2][:, tsl],
                          psum[b0][:, :], xrl[oc % 2][:, tsl], ALU.add,
                          inc=S["s_dve"])
                ps_release(b0, S["s_dve"])
            ydone = S["s_dve"].n
            x_freed[oc] = ydone
            mk.sp.wait(S["s_dve"], ydone)
            mk.sp.dma(y_d[oc * PART:(oc + 1) * PART, :], xrl[oc % 2][:, 0:T],
                      inc=S["s_odma"])
            if tail is not None:
                tap, tn = tail
                mk.sp.dma(tap[oc * PART:(oc + 1) * PART, 0:tn],
                          xrl[oc % 2][:, T - tn:T], inc=S["s_odma"])
            slot_store[oc % 2] = S["s_odma"].n
            if oc + 2 < NCH:
                x_reload(oc + 2, gate=x_freed[oc])
        ws.gates[seq] = S["s_pe"].n
    mk.sp.wait(S["s_odma"], S["s_odma"].n)
    mk.sp.flush()
    mk.pe.flush(); mk.act.flush(); mk.dve.flush(); mk.pool.flush()


def emit_l0f_ffn(mk, x_d, hc_d, y_d, tail=None):
    S = mk.std_sems(extra=["wb0", "wb1", "g0", "g1", "s_xs", "wtk"])
    HCH = 44
    KCH2 = 22

    w1_d = mk.din("w1d", [11, PART, 8, 512])
    w2_d = mk.din("w2d", [8, PART, KCH2, 128])
    b1_d = mk.din("b1d", [PART, HCH])
    b2_d = mk.din("b2d", [PART, 8])
    g_d = mk.din("gd", [PART, 8])
    bln_d = mk.din("blnd", [PART, 8])
    eps_d = mk.din("c_eps", [PART, 1])
    onc_d = mk.din("c_ones_col", [PART, 1], BF16)
    onr_d = mk.din("c_ones_row", [1, PART], BF16)

    arena = mk.sbuf("arena", [PART, 196 * KB], mybir.dt.uint8)

    def view(off_kb, kb, dtype, shape=None):
        ap = arena[:, int(off_kb * KB):int((off_kb + kb) * KB)].bitcast(dtype)
        if shape is None:
            return ap
        return ap.rearrange("p (a b) -> p a b", a=shape[0])

    sxf = view(0, 32, BF16, [NCH, T])
    xn = view(32, 32, BF16, [NCH, T])
    hg = view(64, 88, BF16, [KCH2, T])
    lnscr_b = view(64, 8, BF16, [NCH, 512])
    lnscr_sq = view(72, 8, BF16, [NCH, 512])
    lnscr2_b = view(80, 8, BF16, [NCH, 512])
    lnscr2_sq = view(88, 8, BF16, [NCH, 512])
    mubc = view(152, 4, BF16)
    rbc = view(156, 4, BF16)
    tmp2 = view(160, 8, F32, [2, 1024])
    xrl = [view(168, 8, F32), view(176, 8, F32)]
    a_tmps = [view(184, 4, BF16), view(188, 4, BF16)]
    wbufs = [view(0, 8, BF16), view(8, 8, BF16)]

    mu_f = mk.sbuf("mu_f", [1, 512], F32)
    tA = mk.sbuf("tA", [1, 512], F32)
    tB = mk.sbuf("tB", [1, 512], F32)
    r_f = mk.sbuf("r_f", [1, 512], F32)
    mu_bf = mk.sbuf("mu_bf", [1, 512], BF16)
    r_bf = mk.sbuf("r_bf", [1, 512], BF16)
    ones_col = mk.sbuf("ones_col", [PART, 1], BF16)
    ones_row = mk.sbuf("ones_row", [1, PART], BF16)
    epsc = mk.sbuf("epsc", [PART, 1], F32)
    b1c = mk.sbuf("b1c", [PART, HCH], F32)
    g_c = mk.sbuf("g_c", [PART, 8], F32)
    bln_c = mk.sbuf("bln_c", [PART, 8], F32)
    b2_c = mk.sbuf("b2_c", [PART, 8], F32)

    pp = PsumPool8(mk)
    psum = pp.banks
    lnt = LNTiles(mu_f, tA, tB, r_f, mu_bf, r_bf, mubc, rbc)
    ws = WStream(mk, S, "w", wbufs, [S["wb0"], S["wb1"]])

    for t_, d_ in [(ones_col, onc_d), (ones_row, onr_d), (epsc, eps_d),
                   (b1c, b1_d), (g_c, g_d), (bln_c, bln_d),
                   (b2_c, b2_d)]:
        mk.sp.dma(t_[:, :], d_[:, :], inc=S["s_xdma"])
    consts_loaded = S["s_xdma"].n
    b2_loaded = consts_loaded
    x_freed = {}
    st_gate = {}
    slot_store = {}

    for c in range(NCH):
        if c < 4:
            mk.pool.dma(sxf[:, c, :], x_d[c * PART:(c + 1) * PART, 0:T],
                        inc=S["wtk"])
        else:
            mk.pool.dma(sxf[:, c, 0:1], hc_d[c * PART:(c + 1) * PART, 0:1],
                        inc=S["wtk"])
            mk.pool.dma(sxf[:, c, 1:T], x_d[c * PART:(c + 1) * PART, 0:T - 1],
                        inc=S["wtk"])
    sx_done = S["wtk"].n

    bc = emit_ln_stats(mk, S, pp, lnt, sxf, [lnscr_b, lnscr2_b],
                       [lnscr_sq, lnscr2_sq], NCH, T,
                       ones_col, ones_row, epsc,
                       ready=[(S["wtk"], sx_done),
                              (S["s_xdma"], consts_loaded)])
    xn_done = emit_ln_apply(mk, S, lnt, sxf, tmp2, bc, NCH, dst_bf=xn,
                            gamma=g_c, beta=bln_c, W=T)

    at_guard = {}
    hg_parts = {}
    for slab in range(11):
        seq = ws.fetch(w1_d[slab].rearrange("p c o -> p (c o)"),
                       gate=ws.gates.get(ws.count - 2))
        ws.wait_pe(seq)
        mk.pe.wait(S["s_dve"], xn_done)
        buf = ws.buf(seq).rearrange("p (c o) -> p c o", c=8)
        for pr in range(2):
            k = slab * 2 + pr
            at = a_tmps[k % 2]
            a_done = {}
            for half, fn in ((0, None), (1, None)):
                for tt in range(4):
                    w0 = tt * 512
                    b0 = pp.acquire()
                    for kc in range(NCH):
                        mk.pe.op("matmul", psum[b0][:, :],
                                 buf[:, kc,
                                     (2 * pr + half) * 128:
                                     (2 * pr + half + 1) * 128].opt(),
                                 xn[:, kc, w0:w0 + 512],
                                 start=(kc == 0), stop=(kc == NCH - 1),
                                 inc=(S["s_pe"] if kc == NCH - 1 else None))
                    mk.act.wait(S["s_pe"], S["s_pe"].n)
                    if half == 0:
                        if k >= 2:
                            mk.act.wait(S["s_dve"], at_guard.get(k - 2, 0))
                        mk.act.op("activation", at[:, w0:w0 + 512],
                                  psum[b0][:, :], AF.Identity,
                                  bias=b1c[:, 2 * k:2 * k + 1],
                                  inc=S["s_act"])
                    else:
                        mk.act.op("activation", hg[:, k, w0:w0 + 512],
                                  psum[b0][:, :], AF.Gelu,
                                  bias=b1c[:, 2 * k + 1:2 * k + 2],
                                  inc=S["s_act"])
                    pp.release(b0, S["s_act"])
                    if half == 0:
                        a_done[tt] = S["s_act"].n
            mk.dve.wait(S["s_act"], S["s_act"].n)
            mk.dve.op("tensor_tensor", hg[:, k, :], hg[:, k, :],
                      at[:, 0:T], ALU.mult, inc=S["s_dve"])
            at_guard[k] = S["s_dve"].n
            hg_parts[k] = S["s_dve"].n
        ws.gates[seq] = S["s_pe"].n
    hg_done = S["s_dve"].n

    for slab in range(8):
        seq = ws.fetch(w2_d[slab].rearrange("p c o -> p (c o)"),
                       gate=ws.gates.get(ws.count - 2))
        ws.wait_pe(seq)
        mk.pe.wait(S["s_dve"], hg_done)
        oc = slab
        buf = ws.buf(seq)[:, 0:KCH2 * 128].rearrange(
            "p (c o) -> p c o", c=KCH2)
        b_ = oc % 2
        mk.sp.wait(S["s_dve"], x_freed.get(oc - 2, 0))
        mk.sp.wait(S["s_odma"], S["s_odma"].n)
        mk.sp.wait(S["s_dve"], hg_done)
        mk.sp.dma(xrl[b_][:, 0:T], x_d[oc * PART:(oc + 1) * PART, 0:T],
                  inc=S["g0" if b_ == 0 else "g1"])
        xl_sem = S["g0" if b_ == 0 else "g1"]
        xl_n = xl_sem.n
        for tt in range(4):
            w0 = tt * 512
            b0 = pp.acquire()
            for kc in range(KCH2):
                mk.pe.op("matmul", psum[b0][:, :],
                         buf[:, kc, :].opt(),
                         hg[:, kc, w0:w0 + 512], start=(kc == 0),
                         stop=(kc == KCH2 - 1),
                         inc=(S["s_pe"] if kc == KCH2 - 1 else None))
            mmd = S["s_pe"].n
            mk.dve.wait(S["s_pe"], mmd)
            mk.dve.wait(xl_sem, xl_n)
            mk.dve.wait(S["s_xdma"], b2_loaded)
            mk.dve.op("scalar_tensor_tensor",
                      xrl[b_][:, w0:w0 + 512],
                      psum[b0][:, :], b2_c[:, oc:oc + 1],
                      xrl[b_][:, w0:w0 + 512],
                      ALU.add, ALU.add, inc=S["s_dve"])
            pp.release(b0, S["s_dve"])
        ydone = S["s_dve"].n
        x_freed[oc] = ydone
        mk.sp.wait(S["s_dve"], ydone)
        mk.sp.dma(y_d[oc * PART:(oc + 1) * PART, 0:T], xrl[b_][:, 0:T],
                  inc=S["s_odma"])
        if tail is not None:
            tap, tn = tail
            mk.sp.dma(tap[oc * PART:(oc + 1) * PART, 0:tn],
                      xrl[b_][:, T - tn:T], inc=S["s_odma"])
        slot_store[b_] = S["s_odma"].n
        ws.gates[seq] = S["s_pe"].n

    mk.sp.wait(S["s_odma"], S["s_odma"].n)
    mk.sp.flush()
    mk.pe.flush(); mk.act.flush(); mk.dve.flush(); mk.pool.flush()



def emit_lnst(mk, residual, x_d, xr_d, y_d, tail=None):
    S = mk.std_sems(extra=["g0", "g1", "s_xs"])
    eps_d = mk.din("c_eps", [PART, 1])
    onc_d = mk.din("c_ones_col", [PART, 1], BF16)
    onr_d = mk.din("c_ones_row", [1, PART], BF16)

    arena = mk.sbuf("arena", [PART, 140 * KB], mybir.dt.uint8)

    def view(off_kb, kb, dtype, shape=None):
        ap = arena[:, int(off_kb * KB):int((off_kb + kb) * KB)].bitcast(dtype)
        if shape is None:
            return ap
        return ap.rearrange("p (a b) -> p a b", a=shape[0])

    xt = view(0, 64, F32, [NCH, T])
    lnscr_b = view(64, 8, BF16, [NCH, 512])
    lnscr_sq = view(72, 8, BF16, [NCH, 512])
    lnscr2_b = view(112, 8, BF16, [NCH, 512])
    lnscr2_sq = view(120, 8, BF16, [NCH, 512])
    mubc = view(80, 4, BF16)
    rbc = view(84, 4, BF16)
    tmp2 = view(88, 8, F32, [2, 1024])
    xr_rl = [view(96, 8, F32), view(104, 8, F32)]

    mu_f = mk.sbuf("mu_f", [1, 512], F32)
    tA = mk.sbuf("tA", [1, 512], F32)
    tB = mk.sbuf("tB", [1, 512], F32)
    r_f = mk.sbuf("r_f", [1, 512], F32)
    mu_bf = mk.sbuf("mu_bf", [1, 512], BF16)
    r_bf = mk.sbuf("r_bf", [1, 512], BF16)
    ones_col = mk.sbuf("ones_col", [PART, 1], BF16)
    ones_row = mk.sbuf("ones_row", [1, PART], BF16)
    epsc = mk.sbuf("epsc", [PART, 1], F32)

    pp = PsumPool8(mk)
    lnt = LNTiles(mu_f, tA, tB, r_f, mu_bf, r_bf, mubc, rbc)

    for t_, d_ in [(ones_col, onc_d), (ones_row, onr_d), (epsc, eps_d)]:
        mk.sp.dma(t_[:, :], d_[:, :], inc=S["s_xdma"])
    for c in range(NCH):
        mk.sp.dma(xt[:, c, :], x_d[c * PART:(c + 1) * PART, :], inc=S["s_xs"])
    x_loaded = S["s_xs"].n

    bc = emit_ln_stats(mk, S, pp, lnt, xt, [lnscr_b, lnscr2_b],
                       [lnscr_sq, lnscr2_sq], NCH, T,
                       ones_col, ones_row, epsc,
                       ready=[(S["s_xs"], x_loaded),
                              (S["s_xdma"], S["s_xdma"].n)])
    mk.pool.wait(S["s_act"], bc)
    x2_sems = [S["g0"], S["g1"]]
    x2_loaded = {}
    x2_freed = {}
    if residual:
        def xr_load(c, gate=None):
            b = c % 2
            mk.sp.wait(S["s_act"], bc)
            if gate is not None:
                mk.sp.wait(S["s_dve"], gate)
            mk.sp.dma(xr_rl[b][:, 0:T], xr_d[c * PART:(c + 1) * PART, :],
                      inc=x2_sems[b])
            x2_loaded[c] = (x2_sems[b], x2_sems[b].n)
        xr_load(0)
        xr_load(1)
    tmp_use = {0: 0, 1: 0}
    for c in range(NCH):
        b = c % 2
        for hf in range(2):
            hsl = slice(hf * 1024, (hf + 1) * 1024)
            mk.pool.wait(S["s_dve"], tmp_use[hf])
            mk.pool.op("tensor_tensor", tmp2[:, hf, :], xt[:, c, hsl],
                       mubc[:, hsl], ALU.subtract, inc=S["s_pool"])
            mk.dve.wait(S["s_pool"], S["s_pool"].n)
            if residual:
                mk.dve.wait(*x2_loaded[c])
            mk.dve.op("tensor_tensor", tmp2[:, hf, :], tmp2[:, hf, :],
                      rbc[:, hsl], ALU.mult)
            if residual:
                mk.dve.op("tensor_tensor", xt[:, c, hsl], xr_rl[b][:, hsl],
                          tmp2[:, hf, :], ALU.add, chain=True, inc=S["s_dve"])
            else:
                mk.dve.op("tensor_copy", xt[:, c, hsl], tmp2[:, hf, :],
                          chain=True, inc=S["s_dve"])
            tmp_use[hf] = S["s_dve"].n
        yc = S["s_dve"].n
        x2_freed[c] = yc
        mk.sp.wait(S["s_dve"], yc)
        mk.sp.dma(y_d[c * PART:(c + 1) * PART, :], xt[:, c, :],
                  inc=S["s_odma"])
        if tail is not None:
            tap, tn = tail
            mk.sp.dma(tap[c * PART:(c + 1) * PART, 0:tn],
                      xt[:, c, T - tn:T], inc=S["s_odma"])
        if residual and c + 2 < NCH:
            xr_load(c + 2, gate=x2_freed[c])
    mk.sp.wait(S["s_odma"], S["s_odma"].n)
    mk.sp.flush()
    mk.pe.flush(); mk.act.flush(); mk.dve.flush(); mk.pool.flush()



def make_slabs(w, kcs, ow):
    K, N = w.shape
    n_slab = N // ow
    return np.ascontiguousarray(
        w.reshape(kcs, 128, n_slab, ow).transpose(2, 1, 0, 3))


def rotary_tables_np(n=4096):
    half = np.arange(0, 64, 2, dtype=np.float64)
    inv_freq = 1.0 / 10000.0 ** (half / 64)
    t = np.arange(n, dtype=np.float64)
    freqs = t[:, None] * inv_freq[None, :]
    freqs = np.concatenate([freqs, freqs], -1)
    base_scale = (half + 0.4 * 64) / (1.4 * 64)
    power = (t - n // 2) / 128
    scale = base_scale[None, :] ** power[:, None]
    scale = np.concatenate([scale, scale], -1)
    return freqs, scale


def attn_host_prep(x_ext, Wqkv, qs, ks, Wo, h):
    import ml_dtypes
    d = {}
    d["xh"] = np.ascontiguousarray(x_ext.T)
    d["wqd"] = make_slabs(np.ascontiguousarray(Wqkv[:, 0:1024]), 8, 256)
    d["wkd"] = make_slabs(np.ascontiguousarray(Wqkv[:, 1024:2048]), 8, 256)
    d["wvtokd"] = np.ascontiguousarray(Wqkv[:, 2048:3072].reshape(8, 128, 1024))
    d["wod"] = make_slabs(np.asarray(Wo), 8, 256)
    freqs, scale = rotary_tables_np()
    g0 = h * 2048 - 256
    pos = np.arange(g0, g0 + L0F_TE)
    posc = np.clip(pos, 0, 4095)
    cs = np.cos(freqs[posc])
    sn = np.sin(freqs[posc])
    sc = scale[posc]
    csq = (cs * sc).T
    ssq = (sn * sc).T
    csk = (cs / sc).T
    ssk = (sn / sc).T
    tile2 = lambda a: np.ascontiguousarray(
        np.concatenate([a, a], 0)).astype(ml_dtypes.bfloat16)
    d["csqd"] = tile2(csq)
    d["ssqd"] = tile2(ssq)
    d["cskd"] = tile2(csk)
    d["sskd"] = tile2(ssk)
    qs_col = np.tile(qs, 2)
    ks_col = np.tile(ks, 2)
    qsp = np.tile(np.concatenate([qs[32:64], qs[0:32]]), 2)
    ksp = np.tile(np.concatenate([ks[32:64], ks[0:32]]), 2)
    d["qsd"] = np.stack([qs_col, qsp, ks_col, ksp], 1).astype(np.float32)
    R = np.zeros((128, 128), np.float64)
    for hh in range(2):
        o = hh * 64
        for dd in range(32):
            R[o + dd, o + dd + 32] = -1.0
            R[o + dd + 32, o + dd] = 1.0
    d["c_rt"] = np.ascontiguousarray(R.T).astype(ml_dtypes.bfloat16)
    e16 = np.zeros((128, 8, 16), np.float64)
    sel = np.zeros((16, 8, 128), np.float64)
    for c in range(8):
        for j in range(2):
            hh = c * 2 + j
            e16[j * 64:(j + 1) * 64, c, hh] = 1.0
            sel[hh, c, j * 64:(j + 1) * 64] = 1.0
    d["c_e16"] = e16.reshape(128, 128).astype(ml_dtypes.bfloat16)
    d["c_sel"] = sel.reshape(16, 1024).astype(ml_dtypes.bfloat16)
    ji = np.arange(128)[:, None]
    ti = np.arange(512)[None, :]
    mpre = np.zeros((8, 128, 512), np.float32)
    for i, m in enumerate(OFFS):
        dd = ti - m - ji
        bad = (dd < 0) | (dd > 256)
        mpre[i][bad] = -1e9 / 8
    for i in range(2):
        if h == 0:
            mpre[6 + i][:] = -1e9 / 8
        else:
            mpre[6 + i] = mpre[i]
    d["c_mpre"] = mpre.astype(ml_dtypes.bfloat16)
    d["c_eye"] = np.eye(128).astype(ml_dtypes.bfloat16)
    d["c_ones_col"] = np.ones((128, 1), ml_dtypes.bfloat16)
    d["c_ones_row"] = np.ones((1, 128), ml_dtypes.bfloat16)
    return d


def ffn_host_prep(x_ext1, g, bln, W1, b1, W2, b2):
    import ml_dtypes
    d = {}
    d["xh"] = np.ascontiguousarray(x_ext1.T)
    a_half = np.zeros((1024, 2816), np.float32)
    a_half[:, 0:2730] = W1[:, 0:2730]
    g_half = np.zeros((1024, 2816), np.float32)
    g_half[:, 0:2730] = W1[:, 2730:5460]
    ba = np.zeros(2816, np.float32)
    ba[0:2730] = b1[0:2730]
    bg = np.zeros(2816, np.float32)
    bg[0:2730] = b1[2730:5460]
    W1p = np.zeros((1024, 5632), np.float32)
    b1p = np.zeros(5632, np.float32)
    for k in range(22):
        W1p[:, 256 * k:256 * k + 128] = a_half[:, 128 * k:128 * k + 128]
        W1p[:, 256 * k + 128:256 * k + 256] = g_half[:, 128 * k:128 * k + 128]
        b1p[256 * k:256 * k + 128] = ba[128 * k:128 * k + 128]
        b1p[256 * k + 128:256 * k + 256] = bg[128 * k:128 * k + 128]
    W2p = np.zeros((2816, 1024), np.float32)
    W2p[0:2730] = W2
    d["w1d"] = make_slabs(W1p, 8, 512)
    d["w2d"] = make_slabs(W2p, 22, 128)
    d["b1d"] = np.ascontiguousarray(b1p.reshape(44, 128).T)
    d["b2d"] = np.ascontiguousarray(np.asarray(b2).reshape(8, 128).T)
    d["gd"] = np.ascontiguousarray(np.asarray(g).reshape(8, 128).T)
    d["blnd"] = np.ascontiguousarray(np.asarray(bln).reshape(8, 128).T)
    d["c_eps"] = np.full((128, 1), 1e-5, np.float32)
    d["c_ones_col"] = np.ones((128, 1), ml_dtypes.bfloat16)
    d["c_ones_row"] = np.ones((1, 128), ml_dtypes.bfloat16)
    return d


def ln_host_prep(x_core, xr_core=None):
    import ml_dtypes
    d = {"x": np.ascontiguousarray(x_core.T)}
    if xr_core is not None:
        d["xr"] = np.ascontiguousarray(xr_core.T)
    d["c_eps"] = np.full((128, 1), 1e-5, np.float32)
    d["c_ones_col"] = np.ones((128, 1), ml_dtypes.bfloat16)
    d["c_ones_row"] = np.ones((1, 128), ml_dtypes.bfloat16)
    return d





from concourse.bass_utils import run_bass_kernel_spmd


def emit_halo(mk, ncols, hsrc, hdst, dst_ap, flag_d, groups):
    S = mk.std_sems(extra=["rcv", "cc2"])
    arena = mk.sbuf("arena", [PART, 12 * KB], mybir.dt.uint8)
    hst = arena[:, 0:NCH * ncols * 4].bitcast(F32).rearrange(
        "p (a b) -> p a b", a=NCH)
    flagc = mk.sbuf("flagc", [PART, 1], F32)
    mk.sp.dma(flagc[:, :], flag_d[:, :], inc=S["s_xdma"])
    mk.pool.op("collective_compute", "AllGather", ALU.bypass,
               replica_groups=groups, ins=[hsrc[:, :].opt()],
               outs=[hdst[:, :].opt()], inc=S["cc2"], inc_by=1,
               inc_default=True)
    mk.pool.wait(S["cc2"], S["cc2"].n)
    for c in range(NCH):
        mk.pool.dma(hst[:, c, :], hdst[c * PART:(c + 1) * PART, :],
                    inc=S["rcv"])
    mk.pool.wait(S["rcv"], S["rcv"].n)
    mk.pool.wait(S["s_xdma"], S["s_xdma"].n)
    for c in range(NCH):
        mk.pool.op("tensor_scalar", hst[:, c, :], hst[:, c, :],
                   flagc[:, 0:1], None, ALU.mult, chain=True,
                   inc=(S["s_pool"] if c == NCH - 1 else None))
    mk.sp.wait(S["s_pool"], S["s_pool"].n)
    for c in range(NCH):
        mk.sp.dma(dst_ap[c * PART:(c + 1) * PART, :], hst[:, c, :],
                  inc=S["s_odma"])
    mk.sp.wait(S["s_odma"], S["s_odma"].n)


ALL_EXTRA = ["wb0", "wb1", "wtk", "wtk2", "wtk3", "wtk4", "cc1", "cc2",
             "cc3", "f0", "f1", "g0", "g1", "rcv", "s_spill", "s_iota",
             "s_xs", "s_tb", "s_xa"]


def build_fused(n_pairs=4):
    mk = MK("fused", fused=True)
    groups = [[2 * i, 2 * i + 1] for i in range(n_pairs)]

    x_in = mk.din("x", [D, T])
    y_out = mk.dout("y", [D, T])
    flag_d = mk.din("g_flag", [PART, 1])
    xbuf1 = mk.dtmp("xbuf1", [D, T])
    xpost = mk.dtmp("xpost", [D, T])
    x2b = mk.dtmp("x2b", [D, T])
    x3b = mk.dtmp("x3b", [D, T])
    xha = [mk.dtmp(f"xha{i}", [D, 258 + T]) for i in range(6)]
    xfm = [mk.dtmp(f"xfm{i}", [D, T]) for i in range(6)]
    hcol = [mk.dtmp(f"hcol{i}", [D, 1]) for i in range(6)]
    hsrcA = [mk.dtmp(f"hsrcA{i}", [D, 258]) for i in range(6)]
    hdstA = [mk.dtmp(f"hdstA{i}", [2 * D, 258]) for i in range(6)]
    hsrcF = [mk.dtmp(f"hsrcF{i}", [D, 1]) for i in range(6)]
    hdstF = [mk.dtmp(f"hdstF{i}", [2 * D, 1]) for i in range(6)]
    mk.std_sems(extra=ALL_EXTRA)

    mk.new_block("b0_")
    emit_l0attn(mk, x_in, xbuf1, n_pairs)
    mk.barrier()
    mk.new_block("b1_")
    emit_lnst(mk, False, xbuf1, None, xha[0][:, 258:], tail=(hsrcA[0], 258))
    mk.barrier()
    for li in range(6):
        mk.new_block(f"ha{li}_")
        emit_halo(mk, 258, hsrcA[li], hdstA[li], xha[li][:, 0:258], flag_d,
                  groups)
        mk.barrier()
        mk.new_block(f"a{li}_")
        emit_l0f_attn(mk, xha[li], xfm[li], tail=(hsrcF[li], 1))
        mk.barrier()
        mk.new_block(f"hf{li}_")
        emit_halo(mk, 1, hsrcF[li], hdstF[li], hcol[li], flag_d, groups)
        mk.barrier()
        mk.new_block(f"f{li}_")
        emit_l0f_ffn(mk, xfm[li], hcol[li],
                     xha[li + 1][:, 258:] if li < 5 else xpost,
                     tail=((hsrcA[li + 1], 258) if li < 5 else None))
        mk.barrier()
    mk.new_block("b2_")
    emit_lnst(mk, True, xpost, xbuf1, x2b)
    mk.barrier()
    mk.new_block("b3_")
    emit_l1attn(mk, x2b, x3b, n_pairs)
    mk.barrier()
    mk.new_block("b4_")
    emit_l1ffn(mk, x3b, y_out)
    return mk


def _fused_base(inp, h):
    dummy = np.zeros((1, 1), np.float32)

    def put(d, pfx, dd, drop=()):
        for k, v in dd.items():
            if k in drop:
                continue
            d[pfx + k] = v
    d = {"g_flag": np.full((128, 1), float(h), np.float32)}
    put(d, "b0_", host_prep_l0attn(dummy, inp, h), drop=("x",))
    put(d, "b1_", ln_host_prep(dummy), drop=("x",))
    for li in range(6):
        put(d, f"a{li}_", attn_host_prep(
            dummy, inp["L0_ffn_Wqkv"][li], inp["L0_ffn_q_scale"][li],
            inp["L0_ffn_k_scale"][li], inp["L0_ffn_Wo"][li], h),
            drop=("xh",))
        put(d, f"f{li}_", ffn_host_prep(
            dummy, inp["L0_ffn_ln_g"][li], inp["L0_ffn_ln_b"][li],
            inp["L0_ffn_W1"][li], inp["L0_ffn_b1"][li],
            inp["L0_ffn_W2"][li], inp["L0_ffn_b2"][li]), drop=("xh",))
    put(d, "b2_", ln_host_prep(dummy, dummy), drop=("x", "xr"))
    put(d, "b3_", host_prep_l1attn(dummy, inp, h), drop=("x",))
    put(d, "b4_", host_prep_l1ffn(dummy, inp["L1_ffn_w1"],
                                  inp["L1_ffn_b1"], inp["L1_ffn_w2"],
                                  inp["L1_ffn_b2"]), drop=("x",))
    return d


_prog_cache = {}


def _get(name, fn):
    if name not in _prog_cache:
        mkk = fn()
        _prog_cache[name] = (mkk, mkk.finalize())
    return _prog_cache[name][1]


def kernel(**inp):
    inp = {k: np.asarray(v) for k, v in inp.items()}
    nc = _get("fused", lambda: build_fused(4))
    x0 = inp["L0_attn_x"]
    base = {h: _fused_base(inp, h) for h in (0, 1)}
    datas = []
    for c in range(8):
        b, h = c // 2, c % 2
        d = dict(base[h])
        d["x"] = np.ascontiguousarray(
            x0[b, h * 2048:(h + 1) * 2048].astype(np.float32).T)
        datas.append(d)
    res = run_bass_kernel_spmd(nc, datas, core_ids=list(range(8)))
    out = np.zeros((4, 4096, 1024), np.float32)
    for c in range(8):
        b, h = c // 2, c % 2
        out[b, h * 2048:(h + 1) * 2048] = np.asarray(res.results[c]["y"]).T
    return out


# version 11 (final)
# speedup vs baseline: 1.0454x; 1.0454x over previous
import numpy as np
import concourse.bass as bass
import concourse.mybir as mybir

F32 = mybir.dt.float32
BF16 = mybir.dt.bfloat16
AF = mybir.ActivationFunctionType
ALU = mybir.AluOpType
AX = mybir.AxisListType

PART = 128


class Sem:
    def __init__(self, handle, name):
        self.h = handle
        self.name = name
        self.n = 0

    def inc(self, k=1):
        self.n += k
        return self


class _OpRec:
    __slots__ = ["kind", "fname", "args", "kwargs", "sem", "inc_by", "fn",
                 "inc_default"]

    def __init__(self, kind, fname=None, args=None, kwargs=None, sem=None,
                 inc_by=1, fn=None, inc_default=False):
        self.kind = kind
        self.fname = fname
        self.args = args
        self.kwargs = kwargs
        self.sem = sem
        self.inc_by = inc_by
        self.fn = fn
        self.inc_default = inc_default


class EngQ:

    def __init__(self, mk, name):
        self.mk = mk
        self.name = name
        self.recs = []
        self._pending_waits = []
        self.self_sem = None
        self.self_dma_sem = None
        self._last_op = None

    def wait(self, sem, val=None):
        if val is None:
            val = sem.n
        if val <= 0:
            return
        self._pending_waits.append((sem, val))

    def _flush_waits(self):
        waits, self._pending_waits = self._pending_waits, []
        best = {}
        for sem, val in waits:
            if sem.name not in best or best[sem.name][1] < val:
                best[sem.name] = (sem, val)
        for sem, val in best.values():
            self.recs.append(_OpRec("wait", sem=sem, inc_by=val))

    def self_edge(self):
        lo = self._last_op
        if lo is None:
            return
        if lo.sem is None:
            if lo.fname == "dma_start":
                sem = self.self_dma_sem
                lo.inc_by = 16
            else:
                sem = self.self_sem
                lo.inc_by = 1
            lo.sem = sem
            sem.inc(lo.inc_by)
        self.wait(lo.sem, lo.sem.n)

    def op(self, fname, *args, inc=None, inc_by=1, chain=False,
           inc_default=False, **kwargs):
        if chain:
            self.self_edge()
        self._flush_waits()
        if inc is not None:
            inc.inc(inc_by)
        rec = _OpRec("op", fname=fname, args=args, kwargs=kwargs, sem=inc,
                     inc_by=inc_by, inc_default=inc_default)
        self.recs.append(rec)
        self._last_op = rec

    def dma(self, out, in_, inc=None, **kw):
        self.op("dma_start", inc=inc, inc_by=16, out=out, in_=in_, **kw)

    def raw(self, fn):
        self.recs.append(_OpRec("raw", fn=fn))

    def flush(self):
        self._flush_waits()

    def replay(self, e):
        for r in self.recs:
            if r.kind == "op":
                if r.sem is not None:
                    if r.inc_default:
                        getattr(e, r.fname)(*r.args, **r.kwargs).then_inc(
                            r.sem.h)
                    else:
                        getattr(e, r.fname)(*r.args, **r.kwargs).then_inc(
                            r.sem.h, r.inc_by)
                else:
                    getattr(e, r.fname)(*r.args, **r.kwargs)
            elif r.kind == "wait":
                e.wait_ge(r.sem.h, r.inc_by)
            else:
                r.fn(e)


SBUF_ARENA_MAX = 200 * 1024
SBUF_TOTAL = 212480


class MK:
    def __init__(self, name="mk", fused=False):
        self.nc = bass.Bass("TRN2", target_bir_lowering=False)
        self.name = name
        self.fused = fused
        self.prefix = ""
        self._ctxs = []
        self.sb = {}
        self.ps = {}
        self.dram = {}
        self.sems = {}
        self.pe = EngQ(self, "pe")
        self.act = EngQ(self, "act")
        self.dve = EngQ(self, "dve")
        self.pool = EngQ(self, "pool")
        self.sp = EngQ(self, "sp")
        for s in ["s_pe", "s_act", "s_dve", "s_pool", "s_wdma", "s_xdma", "s_odma", "s_cc"]:
            self.sems[s] = None
        self._sem_names = list(self.sems)
        self.sbuf_bytes = 0
        if fused:
            self.g_arena = self._real_sbuf("g_arena", [PART, SBUF_TOTAL],
                                           mybir.dt.uint8)
            self.g_psum = [self._real_psum(f"gpb{i}", [PART, 512], F32)
                           for i in range(8)]
            self._bump = 0
            self._psum_n = 0

    def new_block(self, prefix):
        self.prefix = prefix
        self._bump = 0
        self._psum_n = 0

    def din(self, name, shape, dtype=F32):
        name = self.prefix + name
        assert name not in self.dram, name
        ap = self.nc.dram_tensor(name, list(shape), dtype, kind="ExternalInput").ap()
        self.dram[name] = ap
        return ap

    def dout(self, name, shape, dtype=F32):
        name = self.prefix + name
        ap = self.nc.dram_tensor(name, list(shape), dtype, kind="ExternalOutput").ap()
        self.dram[name] = ap
        return ap

    def dtmp(self, name, shape, dtype=F32):
        name = self.prefix + name
        h = self.nc.dram_tensor(name, list(shape), dtype, kind="Internal")
        self.dram[name] = h.ap()
        return self.dram[name]

    def _real_sbuf(self, name, shape, dtype):
        g = self.nc.sbuf_tensor(name, list(shape), dtype)
        t = g.__enter__()
        self._ctxs.append(g)
        self.sb[name] = t
        return t

    def _real_psum(self, name, shape, dtype):
        g = self.nc.psum_tensor(name, list(shape), dtype)
        t = g.__enter__()
        self._ctxs.append(g)
        self.ps[name] = t
        return t

    def sbuf(self, name, shape, dtype=F32):
        if not self.fused:
            return self._real_sbuf(name, shape, dtype)
        if name == "arena":
            n = shape[1]
            assert shape[0] == PART and n <= SBUF_ARENA_MAX
            self._bump = n
            return self.g_arena[:, 0:n]
        esz = 4 if dtype == F32 else (2 if dtype == BF16 else 1)
        nb = 1
        for s_ in shape[1:]:
            nb *= s_
        nb *= esz
        off = (self._bump + 3) & ~3
        self._bump = off + nb
        assert self._bump <= SBUF_TOTAL, (name, self._bump)
        ap = self.g_arena[0:shape[0], off:off + nb].bitcast(dtype)
        if len(shape) == 3:
            ap = ap.rearrange("p (a b) -> p a b", a=shape[1])
        return ap

    def psum(self, name, shape, dtype=F32):
        if not self.fused:
            return self._real_psum(name, shape, dtype)
        fel = 1
        for s_ in shape[1:]:
            fel *= s_
        esz = 4 if dtype == F32 else 2
        assert fel * esz <= 2048, (name, shape, dtype)
        i = self._psum_n
        self._psum_n += 1
        assert i < 8, name
        if dtype == F32:
            ap = self.g_psum[i][0:shape[0], 0:fel]
        else:
            ap = self.g_psum[i][0:shape[0], 0:(fel * esz) // 4].bitcast(dtype)
        if len(shape) == 3:
            ap = ap.rearrange("p (a b) -> p a b", a=shape[1])
        return ap

    def sem(self, name):
        if self.fused and self.sems.get(name) is not None:
            return self.sems[name]
        g = self.nc.semaphore(name)
        h = g.__enter__()
        self._ctxs.append(g)
        s = Sem(h, name)
        self.sems[name] = s
        return s

    def std_sems(self, extra=()):
        for s in ["s_pe", "s_act", "s_dve", "s_pool", "s_sp",
                  "s_bpd", "s_bsd",
                  "s_wdma", "s_xdma", "s_odma", *extra]:
            if self.sems.get(s) is None:
                self.sem(s)
        self.pe.self_sem = self.sems["s_pe"]
        self.act.self_sem = self.sems["s_act"]
        self.dve.self_sem = self.sems["s_dve"]
        self.pool.self_sem = self.sems["s_pool"]
        self.sp.self_sem = self.sems["s_sp"]
        self.pool.self_dma_sem = self.sems["s_bpd"]
        self.sp.self_dma_sem = self.sems["s_bsd"]
        return self.sems

    def barrier(self):
        qs = [self.pe, self.act, self.dve, self.pool, self.sp]
        for q in qs:
            q.self_edge()
        sems = [s for s in self.sems.values() if isinstance(s, Sem)]
        for q in qs:
            for s in sems:
                q.wait(s, s.n)
            q.flush()

    def setup_sems(self):
        for n in self._sem_names:
            self.sem(n)

    def finalize(self):
        blk_g = self.nc.Block()
        block = blk_g.__enter__()
        self._ctxs.append(blk_g)
        pe_q, act_q, dve_q, pool_q, sp_q = (
            self.pe, self.act, self.dve, self.pool, self.sp)

        @block.tensor
        def _(e):
            pe_q.replay(e)

        @block.scalar
        def _(e):
            act_q.replay(e)

        @block.vector
        def _(e):
            dve_q.replay(e)

        @block.gpsimd
        def _(e):
            pool_q.replay(e)

        @block.sync
        def _(e):
            sp_q.replay(e)

        for g in reversed(self._ctxs):
            g.__exit__(None, None, None)
        return self.nc



def load_const_tiles(mk: MK):
    onesc = mk.sbuf("ones_col", [PART, 1], BF16)
    onesr = mk.sbuf("ones_row", [1, 512], BF16)
    negc = mk.sbuf("neg_col", [PART, 1], BF16)
    mk.din("c_ones", [PART, 1])
    mk.din("c_ones_row", [1, 512])
    mk.din("c_neg", [PART, 1])
    mk.sp.dma(onesc[:], mk.dram["c_ones"][:], inc=mk.sems["s_xdma"])
    mk.sp.dma(onesr[:], mk.dram["c_ones_row"][:], inc=mk.sems["s_xdma"])
    mk.sp.dma(negc[:], mk.dram["c_neg"][:], inc=mk.sems["s_xdma"])
    return onesc, onesr, negc


class PsumPool:

    def __init__(self, mk):
        self.mk = mk
        self.banks = [mk.psum(f"pb{i}", [PART, 512], F32) for i in range(8)]
        self.guard = [None] * 8

    def acquire(self, i):
        g = self.guard[i]
        if g is not None:
            self.mk.pe.wait(g[0], g[1])
            self.guard[i] = None
        return self.banks[i]

    def release(self, i, sem):
        self.guard[i] = (sem, sem.n)



class PsumPool8:
    def __init__(self, mk):
        self.mk = mk
        self.banks = [mk.psum(f"pb{i}", [PART, 512], F32) for i in range(8)]
        self.guard = [None] * 8
        self.rr = 0

    def acquire(self):
        i = self.rr % 8
        self.rr += 1
        g = self.guard[i]
        if g is not None:
            self.mk.pe.wait(g[0], g[1])
            self.guard[i] = None
        return i

    def release(self, i, sem):
        self.guard[i] = (sem, sem.n)


class LNTiles:

    def __init__(self, mu_f, tA, tB, r_f, mu_bf, r_bf, mubc, rbc):
        self.mu_f, self.tA, self.tB, self.r_f = mu_f, tA, tB, r_f
        self.mu_bf, self.r_bf, self.mubc, self.rbc = mu_bf, r_bf, mubc, rbc


def emit_ln_stats(mk, S, pp, lnt, src, src_b, src_sq, nch, W, ones_col,
                  ones_row, epsc, ready=(), d=1024):
    for sem, val in ready:
        mk.pool.wait(sem, val)
    if not isinstance(src_b, list):
        src_b = [src_b]
        src_sq = [src_sq]
    nslot = len(src_b)
    slot_rows = {}
    psum = pp.banks
    src_fn = src if callable(src) else (lambda c, sl: src[:, c, sl])
    for sub in range((W + 511) // 512):
        w0 = sub * 512
        w1 = min(w0 + 512, W)
        sl = slice(w0, w1)
        rsl = slice(0, w1 - w0)
        slot = sub % nslot
        sb_ = src_b[slot]
        ssq_ = src_sq[slot]
        g = slot_rows.get(slot)
        if g is not None:
            mk.pool.wait(S["s_pe"], g)
        for c in range(nch):
            mk.pool.op("tensor_copy", sb_[:, c, rsl], src_fn(c, sl),
                       inc=(S["s_pool"] if c == nch - 1 else None))
        pool_done = S["s_pool"].n
        mk.act.wait(S["s_pool"], pool_done)
        if g is not None:
            mk.act.wait(S["s_pe"], g)
        for c in range(nch):
            mk.act.op("activation", ssq_[:, c, rsl], sb_[:, c, rsl],
                      AF.Square, inc=(S["s_act"] if c == nch - 1 else None))
        sq_done = S["s_act"].n
        ba = pp.acquire()
        bb = pp.acquire()
        mk.pe.wait(S["s_pool"], pool_done)
        mk.pe.wait(S["s_act"], sq_done)
        rows_a = psum[ba][0:1, rsl]
        rows_b = psum[bb][0:1, rsl]
        for c in range(nch):
            mk.pe.op("matmul", rows_a, ones_col[:, :], sb_[:, c, rsl],
                     start=(c == 0), stop=(c == nch - 1))
        for c in range(nch):
            mk.pe.op("matmul", rows_b, ones_col[:, :], ssq_[:, c, rsl],
                     start=(c == 0), stop=(c == nch - 1),
                     inc=(S["s_pe"] if c == nch - 1 else None))
        slot_rows[slot] = S["s_pe"].n
        rows_done = S["s_pe"].n
        mk.dve.wait(S["s_pe"], rows_done)
        mk.dve.op("tensor_scalar", lnt.mu_f[:, rsl], rows_a, 1.0 / d, None,
                  ALU.mult)
        mk.dve.op("tensor_tensor", lnt.tA[:, rsl], lnt.mu_f[:, rsl],
                  lnt.mu_f[:, rsl], ALU.mult, chain=True)
        mk.dve.op("scalar_tensor_tensor", lnt.tB[:, rsl], rows_b, 1.0 / d,
                  lnt.tA[:, rsl], ALU.mult, ALU.subtract, chain=True,
                  inc=S["s_dve"])
        var_done = S["s_dve"].n
        mk.act.wait(S["s_dve"], var_done)
        mk.act.op("activation", lnt.tA[:, rsl], lnt.tB[:, rsl], AF.Sqrt,
                  bias=epsc[0:1, 0:1], inc=S["s_act"])
        sd_done = S["s_act"].n
        mk.dve.wait(S["s_act"], sd_done)
        mk.dve.op("reciprocal", lnt.r_f[:, rsl], lnt.tA[:, rsl])
        mk.dve.self_edge()
        mk.dve.op("tensor_copy", lnt.mu_bf[:, rsl], lnt.mu_f[:, rsl])
        mk.dve.op("tensor_copy", lnt.r_bf[:, rsl], lnt.r_f[:, rsl],
                  inc=S["s_dve"])
        rows_bf = S["s_dve"].n
        pp.release(ba, S["s_dve"])
        pp.release(bb, S["s_dve"])
        bm = pp.acquire()
        br = pp.acquire()
        mk.pe.wait(S["s_dve"], rows_bf)
        mk.pe.op("matmul", psum[bm][:, rsl], ones_row[:, :], lnt.mu_bf[:, rsl],
                 start=True, stop=True)
        mk.pe.op("matmul", psum[br][:, rsl], ones_row[:, :], lnt.r_bf[:, rsl],
                 start=True, stop=True, inc=S["s_pe"])
        bc_done = S["s_pe"].n
        mk.act.wait(S["s_pe"], bc_done)
        mk.act.op("activation", lnt.mubc[:, sl], psum[bm][:, rsl], AF.Copy)
        mk.act.op("activation", lnt.rbc[:, sl], psum[br][:, rsl], AF.Copy,
                  inc=S["s_act"])
        pp.release(bm, S["s_act"])
        pp.release(br, S["s_act"])
    return S["s_act"].n


def emit_ln_apply(mk, S, lnt, src, tmp, bc_ready, nch, dst_bf=None,
                  gamma=None, beta=None, residual=None, dst_f32=None,
                  extra_ready=(), W=None, tmpw=None):
    mk.pool.wait(S["s_act"], bc_ready)
    for sem, val in extra_ready:
        mk.pool.wait(sem, val)
    if W is None:
        W = src.shape[2]
    if tmpw is None:
        tmpw = tmp.shape[2]
    src_fn = src if callable(src) else (lambda c, sl: src[:, c, sl])
    res_fn = residual if (residual is None or callable(residual)) else (
        lambda c, sl: residual[:, c, sl])
    dstf_fn = dst_f32 if (dst_f32 is None or callable(dst_f32)) else (
        lambda c, sl: dst_f32[:, c, sl])
    dstb_fn = dst_bf if (dst_bf is None or callable(dst_bf)) else (
        lambda c, sl: dst_bf[:, c, sl])
    uses = []
    i = 0
    for c in range(nch):
        for s0 in range(0, W, tmpw):
            sl = slice(s0, min(s0 + tmpw, W))
            n = sl.stop - sl.start
            pp_ = i % 2
            if i >= 2:
                mk.pool.wait(S["s_dve"], uses[i - 2])
            mk.pool.op("tensor_tensor", tmp[:, pp_, 0:n], src_fn(c, sl),
                       lnt.mubc[:, sl], ALU.subtract, inc=S["s_pool"])
            u_done = S["s_pool"].n
            mk.dve.wait(S["s_pool"], u_done)
            if residual is not None:
                mk.dve.op("tensor_tensor", tmp[:, pp_, 0:n], tmp[:, pp_, 0:n],
                          lnt.rbc[:, sl], ALU.mult)
                mk.dve.op("tensor_tensor", dstf_fn(c, sl),
                          res_fn(c, sl), tmp[:, pp_, 0:n], ALU.add,
                          chain=True, inc=S["s_dve"])
            elif gamma is None:
                mk.dve.op("tensor_tensor", dstb_fn(c, sl), tmp[:, pp_, 0:n],
                          lnt.rbc[:, sl], ALU.mult, inc=S["s_dve"])
            else:
                mk.dve.op("scalar_tensor_tensor", tmp[:, pp_, 0:n],
                          tmp[:, pp_, 0:n], gamma[:, c:c + 1], lnt.rbc[:, sl],
                          ALU.mult, ALU.mult)
                mk.dve.op("tensor_scalar", dstb_fn(c, sl), tmp[:, pp_, 0:n],
                          beta[:, c:c + 1], None, ALU.add, chain=True,
                          inc=S["s_dve"])
            uses.append(S["s_dve"].n)
            i += 1
    return S["s_dve"].n


class WStream:

    def __init__(self, mk, S, name, bufs, sems):
        self.mk = mk
        self.S = S
        self.bufs = bufs
        self.sems = sems
        self.n = len(bufs)
        self.count = 0
        self.loaded = {}
        self.gates = {}

    def fetch(self, dram_ap, gate=None):
        seq = self.count
        self.count += 1
        b = seq % self.n
        if gate is not None:
            self.mk.pool.wait(self.S["s_pe"], gate)
        nel = dram_ap.shape[-1]
        self.mk.pool.dma(self.bufs[b][:, 0:nel], dram_ap, inc=self.sems[b])
        self.loaded[seq] = (self.sems[b], self.sems[b].n)
        return seq

    def buf(self, seq):
        return self.bufs[seq % self.n]

    def wait_pe(self, seq):
        self.mk.pe.wait(*self.loaded[seq])


"""L1_ffn sandwich stage as a standalone launch:
   y = x + LN( gelu(LN(x) @ w1 + b1) @ w2 + b2 )
Feature-major: x, y as [1024, 2048] fp32 per core (host pre-transposed).
"""

KB = 1024
D = 1024
NCH = 8
TCORE = 2048
TH = 1024
NH = TCORE // TH
HID = 4096
HCH = 32


def emit_l1ffn(mk, x_d, y_d):
    w1_d = mk.din("w1d", [8, PART, 8, 512])
    w2_d = mk.din("w2d", [8, 2, PART, 16, 128])
    b1_d = mk.din("b1cd", [PART, HCH])
    b2_d = mk.din("b2cd", [PART, NCH])
    oc_d = mk.din("c_ones_col", [PART, 1], BF16)
    or_d = mk.din("c_ones_row", [1, PART], BF16)
    eps_d = mk.din("c_eps", [PART, 1])

    S = mk.std_sems(extra=["wb0", "wb1", "wtk", "wtk2", "wtk3", "wtk4"])

    arena = mk.sbuf("arena", [PART, 196 * KB], mybir.dt.uint8)

    def view(off_kb, nbytes_kb, dtype, shape=None):
        ap = arena[:, int(off_kb * KB):int((off_kb + nbytes_kb) * KB)].bitcast(dtype)
        if shape is None:
            return ap
        return ap.rearrange("p (a b) -> p a b", a=shape[0])

    xt = view(0, 64, F32, [NCH, TCORE])
    xnb = view(64, 16, BF16, [NCH, TH])
    xb = view(80, 16, BF16, [NCH, TH])
    xsq = view(96, 16, BF16, [NCH, TH])
    h_t = view(80, 64, BF16, [HCH, TH])
    yb = view(80, 32, F32, [NCH, TH])
    osq = view(112, 16, BF16, [NCH, TH])
    ob = view(128, 16, BF16, [NCH, TH])
    u_tmp = view(112, 8, F32, [2, TH])
    u2_tmp = view(144 - 8, 8, F32, [2, TH])
    o_t = view(144, 32, F32, [NCH, TH])
    w1s = [view(176, 8, BF16, [8, 512]), view(184, 8, BF16, [8, 512])]
    w2s = [view(176, 4, BF16, [16, 128]), view(180, 4, BF16, [16, 128]),
           view(184, 4, BF16, [16, 128]), view(188, 4, BF16, [16, 128])]
    mubc = view(192, 2, BF16)
    rbc = view(194, 2, BF16)

    mu_f = mk.sbuf("mu_f", [1, 512], F32)
    tA = mk.sbuf("tA", [1, 512], F32)
    tB = mk.sbuf("tB", [1, 512], F32)
    r_f = mk.sbuf("r_f", [1, 512], F32)
    mu_bf = mk.sbuf("mu_bf", [1, 512], BF16)
    r_bf = mk.sbuf("r_bf", [1, 512], BF16)
    ones_col = mk.sbuf("ones_col", [PART, 1], BF16)
    ones_row = mk.sbuf("ones_row", [1, PART], BF16)
    b1c = mk.sbuf("b1c", [PART, HCH], F32)
    b2c = mk.sbuf("b2c", [PART, NCH], F32)
    epsc = mk.sbuf("epsc", [PART, 1], F32)

    psum = [mk.psum(f"pb{i}", [PART, 512], F32) for i in range(8)]
    ps_guard = [None] * 8
    ps_rr = [0]

    def ps_acquire():
        i = ps_rr[0] % 8
        ps_rr[0] += 1
        g = ps_guard[i]
        if g is not None:
            mk.pe.wait(g[0], g[1])
            ps_guard[i] = None
        return i

    def ps_release(i, sem):
        ps_guard[i] = (sem, sem.n)

    mk.sp.dma(ones_col[:, :], oc_d[:, :], inc=S["s_xdma"])
    mk.sp.dma(ones_row[:, :], or_d[:, :], inc=S["s_xdma"])
    mk.sp.dma(b1c[:, :], b1_d[:, :], inc=S["s_xdma"])
    mk.sp.dma(b2c[:, :], b2_d[:, :], inc=S["s_xdma"])
    mk.sp.dma(epsc[:, :], eps_d[:, :], inc=S["s_xdma"])
    for c in range(NCH):
        mk.sp.dma(xt[:, c, :], x_d[c * PART:(c + 1) * PART, :], inc=S["s_xdma"])
    x_loaded = S["s_xdma"].n

    w1_sem = {}
    w2_sem = {}
    slab_pe_done = {}
    w2_pe_done = {}
    y_store_done = {0: 0}

    def fetch_w1(th, s, gate=None):
        buf = w1s[s % 2]
        bsem = S[["wb0", "wb1"][s % 2]]
        if gate is not None:
            mk.pool.wait(S["s_pe"], gate)
        mk.pool.dma(buf[:, :, :], w1_d[s].rearrange("p c o -> p (c o)"),
                    inc=bsem)
        w1_sem[(th, s)] = (bsem, bsem.n)

    fetch_w1(0, 0)
    fetch_w1(0, 1)

    def ln_stats(src, src_b, src_sq, ready_sem, ready_val):
        mk.pool.wait(ready_sem, ready_val)
        for c in range(NCH):
            mk.pool.op("tensor_copy", src_b[:, c, :], src[:, c, :],
                       inc=(S["s_pool"] if c == NCH - 1 else None))
        pool_done = S["s_pool"].n
        mk.act.wait(S["s_pool"], pool_done)
        for c in range(NCH):
            mk.act.op("activation", src_sq[:, c, :], src_b[:, c, :], AF.Square,
                      inc=(S["s_act"] if c == NCH - 1 else None))
        sq_done = S["s_act"].n

        for sub in range(TH // 512):
            sl = slice(sub * 512, (sub + 1) * 512)
            ba = ps_acquire()
            bb = ps_acquire()
            mk.pe.wait(S["s_pool"], pool_done)
            mk.pe.wait(S["s_act"], sq_done)
            rows_a = psum[ba][0:1, :]
            rows_b = psum[bb][0:1, :]
            for c in range(NCH):
                mk.pe.op("matmul", rows_a, ones_col[:, :], src_b[:, c, sl],
                         start=(c == 0), stop=(c == NCH - 1))
            for c in range(NCH):
                mk.pe.op("matmul", rows_b, ones_col[:, :], src_sq[:, c, sl],
                         start=(c == 0), stop=(c == NCH - 1),
                         inc=(S["s_pe"] if c == NCH - 1 else None))
            rows_done = S["s_pe"].n
            mk.dve.wait(S["s_pe"], rows_done)
            mk.dve.op("tensor_scalar", mu_f[:, :], rows_a, 1.0 / D, None, ALU.mult)
            mk.dve.op("tensor_tensor", tA[:, :], mu_f[:, :], mu_f[:, :], ALU.mult,
                      chain=True)
            mk.dve.op("scalar_tensor_tensor", tB[:, :], rows_b, 1.0 / D, tA[:, :],
                      ALU.mult, ALU.subtract, chain=True, inc=S["s_dve"])
            var_done = S["s_dve"].n
            mk.act.wait(S["s_dve"], var_done)
            mk.act.op("activation", tA[:, :], tB[:, :], AF.Sqrt, bias=epsc[0:1, 0:1],
                      inc=S["s_act"])
            sd_done = S["s_act"].n
            mk.dve.wait(S["s_act"], sd_done)
            mk.dve.op("reciprocal", r_f[:, :], tA[:, :])
            mk.dve.self_edge()
            mk.dve.op("tensor_copy", mu_bf[:, :], mu_f[:, :])
            mk.dve.op("tensor_copy", r_bf[:, :], r_f[:, :], inc=S["s_dve"])
            rows_bf = S["s_dve"].n
            ps_release(ba, S["s_dve"])
            ps_release(bb, S["s_dve"])
            bm = ps_acquire()
            br = ps_acquire()
            mk.pe.wait(S["s_dve"], rows_bf)
            mk.pe.op("matmul", psum[bm][:, :], ones_row[:, :], mu_bf[:, :],
                     start=True, stop=True)
            mk.pe.op("matmul", psum[br][:, :], ones_row[:, :], r_bf[:, :],
                     start=True, stop=True, inc=S["s_pe"])
            bc_done = S["s_pe"].n
            mk.act.wait(S["s_pe"], bc_done)
            mk.act.op("activation", mubc[:, sl], psum[bm][:, :], AF.Copy)
            mk.act.op("activation", rbc[:, sl], psum[br][:, :], AF.Copy,
                      inc=S["s_act"])
            ps_release(bm, S["s_act"])
            ps_release(br, S["s_act"])
        return S["s_act"].n

    def ln_apply(src, tmp, bc_ready, dst_bf=None, residual=None, dst_f32=None):
        mk.pool.wait(S["s_act"], bc_ready)
        uses = []
        for c in range(NCH):
            pp = c % 2
            if c >= 2:
                mk.pool.wait(S["s_dve"], uses[c - 2])
            mk.pool.op("tensor_tensor", tmp[:, pp, :], src[:, c, :],
                       mubc[:, :], ALU.subtract, inc=S["s_pool"])
            u_done = S["s_pool"].n
            mk.dve.wait(S["s_pool"], u_done)
            if residual is None:
                mk.dve.op("tensor_tensor", dst_bf[:, c, :], tmp[:, pp, :],
                          rbc[:, :], ALU.mult, inc=S["s_dve"])
            else:
                mk.dve.op("tensor_tensor", tmp[:, pp, :], tmp[:, pp, :],
                          rbc[:, :], ALU.mult)
                mk.dve.op("tensor_tensor", dst_f32[:, c, :], residual[:, c, :],
                          tmp[:, pp, :], ALU.add, chain=True, inc=S["s_dve"])
            uses.append(S["s_dve"].n)
        return S["s_dve"].n

    for th in range(NH):
        src = xt[:, :, th * TH:(th + 1) * TH]
        if th > 0:
            mk.pool.wait(S["s_odma"], y_store_done[th])
            mk.act.wait(S["s_odma"], y_store_done[th])
            fetch_w1(th, 0, gate=w2_pe_done[(th - 1, 15)])
            fetch_w1(th, 1)

        bc_ready = ln_stats(src, xb, xsq, S["s_xdma"], x_loaded)
        xn_done = ln_apply(src, u_tmp, bc_ready, dst_bf=xnb)

        for s in range(8):
            buf = w1s[s % 2]
            mk.pe.wait(*w1_sem[(th, s)])
            mk.pe.wait(S["s_dve"], xn_done)
            for m in range(4):
                b0 = ps_acquire()
                b1_ = ps_acquire()
                for kc in range(NCH):
                    w_ap = buf[:, kc, m * 128:(m + 1) * 128]
                    last = kc == NCH - 1
                    mk.pe.op("matmul", psum[b0][:, :], w_ap,
                             xnb[:, kc, 0:512], start=(kc == 0), stop=last)
                    mk.pe.op("matmul", psum[b1_][:, :], w_ap,
                             xnb[:, kc, 512:1024], start=(kc == 0), stop=last,
                             inc=(S["s_pe"] if last else None))
                mm_done = S["s_pe"].n
                hc = s * 4 + m
                mk.act.wait(S["s_pe"], mm_done)
                mk.act.op("activation", h_t[:, hc, 0:512], psum[b0][:, :],
                          AF.Gelu, bias=b1c[:, hc:hc + 1])
                mk.act.op("activation", h_t[:, hc, 512:1024], psum[b1_][:, :],
                          AF.Gelu, bias=b1c[:, hc:hc + 1], inc=S["s_act"])
                ps_release(b0, S["s_act"])
                ps_release(b1_, S["s_act"])
            slab_pe_done[(th, s)] = S["s_pe"].n
            nxt = s + 2
            if nxt < 8:
                fetch_w1(th, nxt, gate=slab_pe_done[(th, s)])
        h_done = S["s_act"].n
        gemm1_all = slab_pe_done[(th, 7)]

        def fetch_w2(oc, hh, gate):
            gi = oc * 2 + hh
            buf = w2s[gi % 4]
            mk.pool.wait(S["s_pe"], gate)
            bsem = S[["wtk", "wtk2", "wtk3", "wtk4"][gi % 4]]
            mk.pool.dma(buf[:, :, :], w2_d[oc, hh].rearrange("p c o -> p (c o)"),
                        inc=bsem)
            w2_sem[(th, gi)] = (bsem, bsem.n)

        for gi0 in range(4):
            fetch_w2(gi0 // 2, gi0 % 2, gemm1_all)

        for oc in range(NCH):
            b0 = ps_acquire()
            b1_ = ps_acquire()
            mk.pe.wait(S["s_act"], h_done)
            for hh in range(2):
                gi = oc * 2 + hh
                mk.pe.wait(*w2_sem[(th, gi)])
                buf = w2s[gi % 4]
                for kc in range(16):
                    kglob = hh * 16 + kc
                    first = (hh == 0 and kc == 0)
                    last = (hh == 1 and kc == 15)
                    w_ap = buf[:, kc, :]
                    mk.pe.op("matmul", psum[b0][:, :], w_ap,
                             h_t[:, kglob, 0:512], start=first, stop=last)
                    mk.pe.op("matmul", psum[b1_][:, :], w_ap,
                             h_t[:, kglob, 512:1024], start=first, stop=last,
                             inc=(S["s_pe"] if kc == 15 else None))
                w2_pe_done[(th, gi)] = S["s_pe"].n
                ngi = gi + 4
                if ngi < 16:
                    fetch_w2(ngi // 2, ngi % 2, w2_pe_done[(th, ngi - 4)])
            mm_done = S["s_pe"].n
            mk.act.wait(S["s_pe"], mm_done)
            mk.act.op("activation", o_t[:, oc, 0:512], psum[b0][:, :],
                      AF.Identity, bias=b2c[:, oc:oc + 1])
            mk.act.op("activation", o_t[:, oc, 512:1024], psum[b1_][:, :],
                      AF.Identity, bias=b2c[:, oc:oc + 1], inc=S["s_act"])
            ps_release(b0, S["s_act"])
            ps_release(b1_, S["s_act"])
        o_done = S["s_act"].n

        bc2 = ln_stats(o_t, ob, osq, S["s_act"], o_done)
        y_done = ln_apply(o_t, u2_tmp, bc2, residual=src, dst_f32=yb)

        mk.sp.wait(S["s_dve"], y_done)
        for c in range(NCH):
            mk.sp.dma(y_d[c * PART:(c + 1) * PART, th * TH:(th + 1) * TH],
                      yb[:, c, :], inc=S["s_odma"])
        y_store_done[th + 1] = S["s_odma"].n

    mk.sp.wait(S["s_odma"], S["s_odma"].n)
    mk.sp.flush()


def host_prep_l1ffn(x_core, w1, b1, w2, b2):
    d = {}
    d["x"] = np.ascontiguousarray(x_core.T)
    d["w1d"] = np.ascontiguousarray(
        w1.reshape(8, 128, 8, 512).transpose(2, 1, 0, 3))
    d["w2d"] = np.ascontiguousarray(
        w2.reshape(2, 16, 128, 8, 128).transpose(3, 0, 2, 1, 4))
    d["b1cd"] = np.ascontiguousarray(b1.reshape(HCH, PART).T)
    d["b2cd"] = np.ascontiguousarray(b2.reshape(NCH, PART).T)
    import ml_dtypes
    d["c_ones_col"] = np.ones((PART, 1), ml_dtypes.bfloat16)
    d["c_ones_row"] = np.ones((1, PART), ml_dtypes.bfloat16)
    d["c_eps"] = np.full((PART, 1), 1e-5, np.float32)
    return d




"""L1_attn (RWKV block) sandwich stage as a standalone launch:
   y = x + LN( rwkv_block( LN(x) ) )
x, y: [1024, 2048] fp32 feature-major per core. Cores paired (h0|h1) per batch.
"""

KB = 1024
D = 1024
NCH = 8
T = 2048
L1_NTH = 4
L1_THW = 512
L1_CMT = 512
L1_NCM = T // L1_CMT


def emit_l1attn(mk, x_d, y_d, n_pairs=4):
    S = mk.std_sems(extra=["wb0", "wb1", "cc1", "cc2", "cc3",
                           "f0", "f1", "g0", "g1", "s_spill", "rcv",
                           "s_iota", "s_xs", "wtk2"])

    cols_d = mk.din("colsd", [PART, 11 * NCH])
    wk_d = mk.din("wkd", [2, PART, 8, 512])
    wv_d = mk.din("wvd", [2, PART, 8, 512])
    wr_d = mk.din("wrd", [2, PART, 8, 512])
    wo_d = mk.din("wod", [2, PART, 8, 512])
    wkey_d = mk.din("wkeyd", [8, PART, 8, 512])
    wval_d = mk.din("wvald", [8, PART, 32, 128])
    wrec_d = mk.din("wrecd", [2, PART, 8, 512])
    flag_d = mk.din("c_flag", [PART, 1])
    iota_d = mk.din("c_iota", [PART, T], BF16)
    eps_d = mk.din("c_eps", [PART, 1])
    onc_d = mk.din("c_ones_col", [PART, 1], BF16)
    onr_d = mk.din("c_ones_row", [1, PART], BF16)

    wkey_bf = mk.dtmp("wkey_bf", [8, PART, 8, 512], BF16)
    wval_bf = mk.dtmp("wval_bf", [8, PART, 32, 128], BF16)
    xi_spill = mk.dtmp("xi_spill", [PART, NCH * T])
    e1_src = mk.dtmp("e1_src", [PART, NCH])
    e1_dst = mk.dtmp("e1_dst", [2 * PART, NCH])
    e2_src = mk.dtmp("e2_src", [PART, 2 * NCH])
    e2_dst = mk.dtmp("e2_dst", [2 * PART, 2 * NCH])
    e3_src = mk.dtmp("e3_src", [PART, NCH])
    e3_dst = mk.dtmp("e3_dst", [2 * PART, NCH])
    groups = [[2 * i, 2 * i + 1] for i in range(n_pairs)]

    arena = mk.sbuf("arena", [PART, 195 * KB], mybir.dt.uint8)

    def view(off_kb, kb, dtype, shape=None):
        ap = arena[:, int(off_kb * KB):int((off_kb + kb) * KB)].bitcast(dtype)
        if shape is None:
            return ap
        return ap.rearrange("p (a b) -> p a b", a=shape[0])

    xi = view(0, 64, F32, [NCH, T])
    num_t = view(0, 32, BF16, [NCH, T])
    den_t = view(32, 32, BF16, [NCH, T])
    rwkv = num_t
    kk_t = view(0, 32, BF16, [32, L1_CMT])
    xmid_lo = view(32, 32, F32, [NCH, 1024])
    lnscrA_b = view(0, 8, BF16, [NCH, 512])
    lnscrA_sq = view(8, 8, BF16, [NCH, 512])
    lnscrA2_b = view(16, 8, BF16, [NCH, 512])
    lnscrA2_sq = view(24, 8, BF16, [NCH, 512])
    h_pad3 = arena[:, 64 * KB:64 * KB + NCH * 2049 * 2].bitcast(BF16).rearrange(
        "p (a b) -> p a b", a=NCH)
    x_stage = view(97, 16, F32, [NCH, 512])
    lnscrC_b = view(113, 8, BF16, [NCH, 512])
    lnscrC_sq = view(121, 8, BF16, [NCH, 512])
    mix_m = view(97, 8, BF16, [NCH, L1_THW])
    ek_t = view(105, 8, BF16, [NCH, L1_THW])
    ekv_t = view(113, 8, BF16, [NCH, L1_THW])
    apad = view(121, 9, BF16, [NCH, 513])
    bpad = view(130, 9, BF16, [NCH, 513])
    sr3 = view(139, 8, BF16, [NCH, L1_THW])
    iotab = view(97, 4, BF16)
    xmid_hi = view(97, 32, F32, [NCH, 1024])
    wbufs = [view(147, 8, BF16), view(155, 8, BF16)]
    pw_st = view(147, 2, F32)
    pw_bf = view(149, 1, BF16)
    rden_st = view(150, 4, F32)
    x2_rl = [view(147, 8, F32), view(155, 8, F32)]
    mubc = view(163, 4, BF16)
    rbc = view(167, 4, BF16)
    sr2 = view(163, 8, BF16, [NCH, L1_THW])
    sig_cm = view(163, 8, BF16, [NCH, L1_CMT])
    tmp2 = view(171, 8, F32, [2, 1024])
    xi_rl = [view(171, 8, F32), view(179, 8, F32)]
    tkv_st2 = [view(171, 1, BF16), view(172, 1, BF16)]
    sr0 = view(179, 8, BF16, [NCH, L1_THW])
    xr2_m = view(179, 8, BF16, [NCH, L1_CMT])
    sr1 = view(187, 8, BF16, [NCH, L1_THW])
    xk2_m = view(187, 8, BF16, [NCH, L1_CMT])
    tkv_st = None

    def sr_tile(th):
        return [sr0, sr1, sr2, sr3][th]

    def xmid_fn(c, sl):
        if sl.stop <= 1024:
            return xmid_lo[:, c, sl]
        assert sl.start >= 1024
        return xmid_hi[:, c, slice(sl.start - 1024, sl.stop - 1024)]

    mu_f = mk.sbuf("mu_f", [1, 512], F32)
    tA = mk.sbuf("tA", [1, 512], F32)
    tB = mk.sbuf("tB", [1, 512], F32)
    r_f = mk.sbuf("r_f", [1, 512], F32)
    mu_bf = mk.sbuf("mu_bf", [1, 512], BF16)
    r_bf = mk.sbuf("r_bf", [1, 512], BF16)
    ones_col = mk.sbuf("ones_col", [PART, 1], BF16)
    ones_row = mk.sbuf("ones_row", [1, PART], BF16)
    epsc = mk.sbuf("epsc", [PART, 1], F32)
    flagc = mk.sbuf("flagc", [PART, 1], F32)
    colsb = mk.sbuf("colsb", [PART, 11 * NCH], F32)
    lamc = mk.sbuf("lamc", [PART, NCH], F32)
    euc = mk.sbuf("euc", [PART, NCH], F32)
    tec = mk.sbuf("tec", [PART, NCH], F32)
    wcolc = mk.sbuf("wcolc", [PART, NCH], F32)
    carry_a = mk.sbuf("carry_a", [PART, NCH], F32)
    carry_b = mk.sbuf("carry_b", [PART, NCH], F32)
    e1cmp = mk.sbuf("e1cmp", [PART, NCH], F32)
    e1rcv = mk.sbuf("e1rcv", [PART, NCH], F32)
    e2cmp = mk.sbuf("e2cmp", [PART, 2 * NCH], F32)
    e2rcv = mk.sbuf("e2rcv", [PART, 2 * NCH], F32)
    amid = mk.sbuf("amid", [PART, NCH], F32)
    bmid = mk.sbuf("bmid", [PART, NCH], F32)
    e3cmp = mk.sbuf("e3cmp", [PART, NCH], F32)
    e3rcv = mk.sbuf("e3rcv", [PART, NCH], F32)
    tmk1 = mk.sbuf("tmk1", [PART, NCH], F32)
    tmv1 = mk.sbuf("tmv1", [PART, NCH], F32)
    tmr1 = mk.sbuf("tmr1", [PART, NCH], F32)
    ftmk1 = mk.sbuf("ftmk1", [PART, NCH], F32)
    ftmr1 = mk.sbuf("ftmr1", [PART, NCH], F32)

    def col(i):
        return colsb[:, i * NCH:(i + 1) * NCH]
    ln1w, ln1b, ln2w, ln2b = col(0), col(1), col(2), col(3)
    tmk, tmv, tmr, ftmk, ftmr = col(4), col(5), col(6), col(7), col(8)
    tdc, tfc = col(9), col(10)

    pp = PsumPool8(mk)
    psum = pp.banks
    lnt = LNTiles(mu_f, tA, tB, r_f, mu_bf, r_bf, mubc, rbc)
    ws = WStream(mk, S, "w", wbufs, [S["wb0"], S["wb1"]])

    mk.sp.dma(ones_col[:, :], onc_d[:, :], inc=S["s_xdma"])
    mk.sp.dma(ones_row[:, :], onr_d[:, :], inc=S["s_xdma"])
    mk.sp.dma(epsc[:, :], eps_d[:, :], inc=S["s_xdma"])
    mk.sp.dma(flagc[:, :], flag_d[:, :], inc=S["s_xdma"])
    mk.sp.dma(colsb[:, :], cols_d[:, :], inc=S["s_xdma"])
    consts_loaded = S["s_xdma"].n

    mk.dve.wait(S["s_xdma"], consts_loaded)
    for dst, src in [(tmk1, tmk), (tmv1, tmv), (tmr1, tmr),
                     (ftmk1, ftmk), (ftmr1, ftmr)]:
        mk.dve.op("tensor_scalar", dst[:, :], src, -1.0, 1.0, ALU.mult, ALU.add)
    mk.dve.op("memset", carry_a[:, :], 0.0)
    mk.dve.op("memset", carry_b[:, :], 0.0, inc=S["s_dve"])
    cols1_done = S["s_dve"].n
    mk.act.wait(S["s_xdma"], consts_loaded)
    mk.act.op("activation", tec[:, :], tdc, AF.Exp)
    mk.act.op("activation", lamc[:, :], tec[:, :], AF.Exp, scale=-1.0,
              chain=True)
    mk.act.op("activation", euc[:, :], tfc, AF.Exp, inc=S["s_act"])
    lam_done = S["s_act"].n
    mk.dve.wait(S["s_act"], lam_done)
    mk.dve.op("tensor_scalar", wcolc[:, :], tec[:, :], -1.0, None, ALU.mult,
              inc=S["s_dve"])
    wcol_done = S["s_dve"].n

    xstage_free = 0
    for q in range(4):
        sl = slice(q * 512, (q + 1) * 512)
        mk.sp.wait(S["s_pool"], xstage_free)
        for c in range(NCH):
            mk.sp.dma(x_stage[:, c, :], x_d[c * PART:(c + 1) * PART, sl],
                      inc=S["s_xs"])
        xl = S["s_xs"].n
        lnt_v = LNTiles(mu_f, tA, tB, r_f, mu_bf, r_bf,
                        mubc[:, 0:512], rbc[:, 0:512])
        bc = emit_ln_stats(mk, S, pp, lnt_v, x_stage, lnscrC_b, lnscrC_sq,
                           NCH, 512, ones_col, ones_row, epsc,
                           ready=[(S["s_xs"], xl)])
        mk.pool.wait(S["s_act"], bc)
        uses = []
        for c in range(NCH):
            ppg = c % 2
            if c >= 2:
                mk.pool.wait(S["s_dve"], uses[c - 2])
            mk.pool.op("tensor_tensor", tmp2[:, ppg, 0:512], x_stage[:, c, :],
                       mubc[:, 0:512], ALU.subtract, inc=S["s_pool"])
            mk.dve.wait(S["s_pool"], S["s_pool"].n)
            mk.dve.op("tensor_tensor", xi[:, c, sl], tmp2[:, ppg, 0:512],
                      rbc[:, 0:512], ALU.mult, inc=S["s_dve"])
            uses.append(S["s_dve"].n)
        xstage_free = S["s_pool"].n
    xi_done = S["s_dve"].n

    bc = emit_ln_stats(mk, S, pp, lnt, xi, lnscrC_b, lnscrC_sq, NCH, T,
                       ones_col, ones_row, epsc,
                       ready=[(S["s_dve"], xi_done)])
    h_done = emit_ln_apply(mk, S, lnt, xi, tmp2, bc, NCH,
                           dst_bf=h_pad3[:, :, 1:2049],
                           gamma=ln1w, beta=ln1b, W=T)

    mk.pool.wait(S["s_dve"], h_done)
    mk.pool.op("tensor_copy", e1cmp[:, :], h_pad3[:, :, 2048:2049].squeeze(2),
               inc=S["s_pool"])
    mk.pool.dma(e1_src[:, :], e1cmp[:, :], chain=True, inc=S["rcv"])
    mk.pool.wait(S["rcv"], S["rcv"].n)
    mk.pool.op("collective_compute", "AllGather", ALU.bypass,
               replica_groups=groups, ins=[e1_src[:, :].opt()],
               outs=[e1_dst[:, :].opt()], inc=S["cc1"], inc_by=1,
               inc_default=True)
    mk.pool.wait(S["cc1"], S["cc1"].n)
    mk.pool.dma(e1rcv[:, :], e1_dst[0:PART, :], inc=S["rcv"])
    mk.pool.wait(S["rcv"], S["rcv"].n)
    mk.pool.wait(S["s_xdma"], consts_loaded)
    mk.pool.op("tensor_scalar", h_pad3[:, :, 0:1].squeeze(2), e1rcv[:, :],
               flagc[:, 0:1], None, ALU.mult, inc=S["s_pool"])
    e1_done = S["s_pool"].n

    mk.sp.wait(S["s_dve"], h_done)
    for c in range(NCH):
        mk.sp.dma(xi_spill[:, c * T:(c + 1) * T], xi[:, c, :], inc=S["s_spill"])
    spill_done = S["s_spill"].n

    h_full = h_pad3[:, :, 1:2049]
    hh_full = h_pad3[:, :, 0:2048]

    def mixes(sl, which, hf, hhf, gates=()):
        for g, v in gates:
            mk.pool.wait(g, v)
        uses = []
        i = 0
        w = sl.stop - sl.start
        for (dst, tmc, tm1c) in which:
            for c in range(NCH):
                ppg = i % 2
                if i >= 2:
                    mk.pool.wait(S["s_dve"], uses[i - 2])
                mk.pool.op("tensor_scalar", tmp2[:, ppg, 0:w],
                           hhf[:, c, sl], tm1c[:, c:c + 1], None, ALU.mult,
                           inc=S["s_pool"])
                mk.dve.wait(S["s_pool"], S["s_pool"].n)
                mk.dve.op("scalar_tensor_tensor", dst[:, c, :],
                          hf[:, c, sl], tmc[:, c:c + 1], tmp2[:, ppg, 0:w],
                          ALU.mult, ALU.add, inc=S["s_dve"])
                uses.append(S["s_dve"].n)
                i += 1
        return S["s_dve"].n

    def gemm_1024(x_src, w_dram, evac, gates=(), tw=L1_THW):
        for slab in range(2):
            seq = ws.fetch(w_dram[slab].rearrange("p c o -> p (c o)"),
                           gate=ws.gates.get(ws.count - 2))
            ws.wait_pe(seq)
            for g, v in gates:
                mk.pe.wait(g, v)
            buf = ws.buf(seq).rearrange("p (c o) -> p c o", c=8)
            for m in range(4):
                oc = slab * 4 + m
                b0 = pp.acquire()
                for kc in range(NCH):
                    mk.pe.op("matmul", psum[b0][:, 0:tw],
                             buf[:, kc, m * 128:(m + 1) * 128],
                             x_src[:, kc, :], start=(kc == 0),
                             stop=(kc == NCH - 1),
                             inc=(S["s_pe"] if kc == NCH - 1 else None))
                evac(oc, b0, S["s_pe"].n)
            ws.gates[seq] = S["s_pe"].n

    stage_jobs = ([(wkey_bf[s_], wkey_d[s_]) for s_ in range(8)] +
                  [(wval_bf[s_], wval_d[s_]) for s_ in range(8)])
    stage_state = {"i": 0}

    def emit_stage(n):
        for _ in range(n):
            i_ = stage_state["i"]
            if i_ >= len(stage_jobs):
                return
            dst_, src_ = stage_jobs[i_]
            mk.pool.dma(dst_.rearrange("p c o -> p (c o)"),
                        src_.rearrange("p c o -> p (c o)"), inc=S["wtk2"])
            stage_state["i"] = i_ + 1

    prev_numden = 0
    prev_r_mms = 0
    for th in range(L1_NTH):
        sl = slice(th * L1_THW, (th + 1) * L1_THW)
        srt = sr_tile(th)
        gates = [(S["s_pool"], e1_done), (S["s_dve"], cols1_done)]
        if th > 0:
            gates.append((S["s_dve"], prev_numden))
            gates.append((S["s_pe"], prev_r_mms))

        mixk_done = mixes(sl, [(mix_m, tmk, tmk1)], h_full, hh_full,
                          gates=gates)

        def evac_k(oc, b0, mmd):
            mk.act.wait(S["s_pe"], mmd)
            mk.act.op("activation", ek_t[:, oc, :], psum[b0][:, 0:L1_THW], AF.Exp,
                      inc=S["s_act"])
            pp.release(b0, S["s_act"])
        gemm_1024(mix_m, wk_d, evac_k, gates=[(S["s_dve"], mixk_done)])
        k_mms = S["s_pe"].n
        ek_done = S["s_act"].n

        mixv_done = mixes(sl, [(mix_m, tmv, tmv1)], h_full, hh_full,
                          gates=[(S["s_pe"], k_mms)])

        def evac_v(oc, b0, mmd):
            mk.dve.wait(S["s_pe"], mmd)
            mk.dve.wait(S["s_act"], ek_done)
            mk.dve.op("tensor_tensor", ekv_t[:, oc, :], psum[b0][:, 0:L1_THW],
                      ek_t[:, oc, :], ALU.mult, inc=S["s_dve"])
            pp.release(b0, S["s_dve"])
        gemm_1024(mix_m, wv_d, evac_v, gates=[(S["s_dve"], mixv_done)])
        v_mms = S["s_pe"].n
        ekv_done = S["s_dve"].n

        mixr_done = mixes(sl, [(mix_m, tmr, tmr1)], h_full, hh_full,
                          gates=[(S["s_pe"], v_mms)])

        def evac_r(oc, b0, mmd):
            mk.act.wait(S["s_pe"], mmd)
            mk.act.op("activation", srt[:, oc, :], psum[b0][:, 0:L1_THW],
                      AF.Sigmoid, inc=S["s_act"])
            pp.release(b0, S["s_act"])
        gemm_1024(mix_m, wr_d, evac_r, gates=[(S["s_dve"], mixr_done)])
        prev_r_mms = S["s_pe"].n
        emit_stage(4)

        mk.pool.wait(S["s_dve"], ekv_done)
        mk.pool.wait(S["s_pool"], S["s_pool"].n)
        mk.pool.op("tensor_copy", apad[:, :, 0:1].squeeze(2), carry_a[:, :])
        mk.pool.op("tensor_copy", bpad[:, :, 0:1].squeeze(2), carry_b[:, :],
                   chain=True, inc=S["s_pool"])
        pads_done = S["s_pool"].n
        mk.dve.wait(S["s_pool"], pads_done)
        mk.dve.wait(S["s_act"], lam_done)
        for c in range(NCH):
            lam_b = lamc[:, c:c + 1].broadcast_to([PART, L1_THW])
            mk.dve.op("tensor_tensor_scan", apad[:, c, 1:513], lam_b,
                      ekv_t[:, c, :], apad[:, c, 0:1], ALU.mult, ALU.add)
            mk.dve.op("tensor_tensor_scan", bpad[:, c, 1:513], lam_b,
                      ek_t[:, c, :], bpad[:, c, 0:1], ALU.mult, ALU.add,
                      inc=(S["s_dve"] if c == NCH - 1 else None))
        scans_done = S["s_dve"].n
        mk.pool.wait(S["s_dve"], scans_done)
        mk.pool.op("tensor_copy", carry_a[:, :], apad[:, :, 512:513].squeeze(2),
                   inc=S["s_pool"])
        mk.pool.op("tensor_copy", carry_b[:, :], bpad[:, :, 512:513].squeeze(2),
                   inc=S["s_pool"])

        mk.dve.wait(S["s_spill"], spill_done)
        mk.dve.wait(S["s_dve"], scans_done)
        for c in range(NCH):
            mk.dve.op("scalar_tensor_tensor", num_t[:, c, sl], ekv_t[:, c, :],
                      euc[:, c:c + 1], apad[:, c, 0:512], ALU.mult, ALU.add)
            mk.dve.op("scalar_tensor_tensor", den_t[:, c, sl], ek_t[:, c, :],
                      euc[:, c:c + 1], bpad[:, c, 0:512], ALU.mult, ALU.add,
                      inc=(S["s_dve"] if c == NCH - 1 else None))
        prev_numden = S["s_dve"].n

    mk.pool.wait(S["s_pool"], S["s_pool"].n)
    mk.pool.op("tensor_copy", e2cmp[:, 0:NCH], carry_a[:, :])
    mk.pool.op("tensor_copy", e2cmp[:, NCH:2 * NCH], carry_b[:, :],
               chain=True, inc=S["s_pool"])
    mk.pool.dma(e2_src[:, :], e2cmp[:, :], chain=True, inc=S["rcv"])
    mk.pool.wait(S["rcv"], S["rcv"].n)
    mk.pool.op("collective_compute", "AllGather", ALU.bypass,
               replica_groups=groups, ins=[e2_src[:, :].opt()],
               outs=[e2_dst[:, :].opt()], inc=S["cc2"], inc_by=1,
               inc_default=True)
    mk.pool.wait(S["cc2"], S["cc2"].n)
    mk.pool.dma(e2rcv[:, :], e2_dst[0:PART, :], inc=S["rcv"])
    mk.pool.wait(S["rcv"], S["rcv"].n)
    mk.pool.op("tensor_scalar", amid[:, :], e2rcv[:, 0:NCH], flagc[:, 0:1],
               None, ALU.mult)
    mk.pool.op("tensor_scalar", bmid[:, :], e2rcv[:, NCH:2 * NCH],
               flagc[:, 0:1], None, ALU.mult, chain=True, inc=S["s_pool"])
    mid_done = S["s_pool"].n

    mk.sp.wait(S["s_dve"], prev_numden)
    mk.sp.wait(S["s_pe"], prev_r_mms)
    mk.sp.dma(iotab[:, :], iota_d[:, :], inc=S["s_iota"])
    iota_loaded = S["s_iota"].n
    last_w_gemm = ws.gates.get(ws.count - 1, 0)
    mk.dve.wait(S["s_pe"], last_w_gemm)
    mk.dve.wait(S["s_iota"], iota_loaded)
    mk.dve.wait(S["s_pool"], mid_done)
    mk.dve.wait(S["s_dve"], wcol_done)
    for c in range(NCH):
        for th in range(L1_NTH):
            sl = slice(th * L1_THW, (th + 1) * L1_THW)
            mk.dve.op("tensor_scalar", pw_st[:, 0:L1_THW], iotab[:, sl],
                      wcolc[:, c:c + 1], None, ALU.mult, chain=True,
                      inc=S["s_dve"])
            mk.act.wait(S["s_dve"], S["s_dve"].n)
            mk.act.op("activation", pw_bf[:, 0:L1_THW], pw_st[:, 0:L1_THW], AF.Exp,
                      inc=S["s_act"])
            mk.dve.wait(S["s_act"], S["s_act"].n)
            mk.dve.op("scalar_tensor_tensor", num_t[:, c, sl], pw_bf[:, 0:L1_THW],
                      amid[:, c:c + 1], num_t[:, c, sl], ALU.mult, ALU.add)
            mk.dve.op("scalar_tensor_tensor", den_t[:, c, sl], pw_bf[:, 0:L1_THW],
                      bmid[:, c:c + 1], den_t[:, c, sl], ALU.mult, ALU.add,
                      inc=S["s_dve"])
    corr_done = S["s_dve"].n

    mk.act.wait(S["s_dve"], corr_done)
    for c in range(NCH):
        for hf in range(2):
            hsl = slice(hf * 1024, (hf + 1) * 1024)
            mk.act.wait(S["s_dve"], S["s_dve"].n)
            mk.act.op("activation", rden_st[:, 0:1024], den_t[:, c, hsl],
                      AF.Ln, chain=True, inc=S["s_act"])
            mk.act.op("activation", den_t[:, c, hsl], rden_st[:, 0:1024],
                      AF.Exp, scale=-1.0, chain=True, inc=S["s_act"])
            mk.dve.wait(S["s_act"], S["s_act"].n)
            mk.dve.op("tensor_tensor", num_t[:, c, hsl], num_t[:, c, hsl],
                      den_t[:, c, hsl], ALU.mult, inc=S["s_dve"])
        for th in range(L1_NTH):
            sl = slice(th * L1_THW, (th + 1) * L1_THW)
            mk.dve.op("tensor_tensor", num_t[:, c, sl], num_t[:, c, sl],
                      sr_tile(th)[:, c, :], ALU.mult, chain=True,
                      inc=S["s_dve"])
    rwkv_done = S["s_dve"].n

    xi_sems = [S["f0"], S["f1"]]
    xi_loaded = {}
    xi_freed = {}

    def xi_reload(oc, gate=None):
        b = oc % 2
        mk.sp.wait(S["s_dve"], rwkv_done)
        if gate is not None:
            mk.sp.wait(S["s_dve"], gate)
        mk.sp.dma(xi_rl[b][:, :], xi_spill[:, oc * T:(oc + 1) * T],
                  inc=xi_sems[b])
        xi_loaded[oc] = (xi_sems[b], xi_sems[b].n)

    xi_reload(0)
    xi_reload(1)
    mk.pool.wait(S["s_dve"], rwkv_done)
    for slab in range(2):
        seq = ws.fetch(wo_d[slab].rearrange("p c o -> p (c o)"),
                       gate=ws.gates.get(ws.count - 2))
        ws.wait_pe(seq)
        mk.pe.wait(S["s_dve"], rwkv_done)
        buf = ws.buf(seq).rearrange("p (c o) -> p c o", c=8)
        for m in range(4):
            oc = slab * 4 + m
            for tt in range(4):
                tsl = slice(tt * 512, (tt + 1) * 512)
                b0 = pp.acquire()
                for kc in range(NCH):
                    mk.pe.op("matmul", psum[b0][:, :],
                             buf[:, kc, m * 128:(m + 1) * 128],
                             rwkv[:, kc, tsl], start=(kc == 0),
                             stop=(kc == NCH - 1),
                             inc=(S["s_pe"] if kc == NCH - 1 else None))
                mmd = S["s_pe"].n
                mk.dve.wait(S["s_pe"], mmd)
                mk.dve.wait(*xi_loaded[oc])
                mk.dve.op("tensor_tensor", xmid_fn(oc, tsl), psum[b0][:, :],
                          xi_rl[oc % 2][:, tsl], ALU.add, inc=S["s_dve"])
                pp.release(b0, S["s_dve"])
            xi_freed[oc] = S["s_dve"].n
            if oc + 2 < NCH:
                xi_reload(oc + 2, gate=xi_freed[oc])
        ws.gates[seq] = S["s_pe"].n
    xmid_done = S["s_dve"].n

    bc = emit_ln_stats(mk, S, pp, lnt, xmid_fn, [lnscrA_b, lnscrA2_b],
                       [lnscrA_sq, lnscrA2_sq], NCH, T,
                       ones_col, ones_row, epsc,
                       ready=[(S["s_dve"], xmid_done)])
    h2_done = emit_ln_apply(mk, S, lnt, xmid_fn, tmp2, bc, NCH,
                            dst_bf=h_pad3[:, :, 1:2049],
                            gamma=ln2w, beta=ln2b, W=T)

    mk.pool.wait(S["s_dve"], h2_done)
    mk.pool.op("tensor_copy", e3cmp[:, :], h_pad3[:, :, 2048:2049].squeeze(2),
               inc=S["s_pool"])
    mk.pool.dma(e3_src[:, :], e3cmp[:, :], chain=True, inc=S["rcv"])
    mk.pool.wait(S["rcv"], S["rcv"].n)
    mk.pool.op("collective_compute", "AllGather", ALU.bypass,
               replica_groups=groups, ins=[e3_src[:, :].opt()],
               outs=[e3_dst[:, :].opt()], inc=S["cc3"], inc_by=1,
               inc_default=True)
    mk.pool.wait(S["cc3"], S["cc3"].n)
    mk.pool.dma(e3rcv[:, :], e3_dst[0:PART, :], inc=S["rcv"])
    mk.pool.wait(S["rcv"], S["rcv"].n)
    mk.pool.op("tensor_scalar", h_pad3[:, :, 0:1].squeeze(2), e3rcv[:, :],
               flagc[:, 0:1], None, ALU.mult, inc=S["s_pool"])
    e3_done = S["s_pool"].n

    wbufs2 = [view(130, 8, BF16), view(138, 8, BF16)]
    w2sems = [S["f0"], S["f1"]]
    w2st = {"count": 0, "loaded": {}, "gates": {}}
    cm_free_dve = S["s_dve"].n
    cm_free_pool = S["s_pool"].n

    def fetch2(dram_flat):
        seq = w2st["count"]
        w2st["count"] = seq + 1
        b = seq % 2
        g = w2st["gates"].get(seq - 2)
        if g is not None:
            mk.sp.wait(S["s_pe"], g)
        if seq < 2:
            mk.sp.wait(S["s_dve"], cm_free_dve)
            mk.sp.wait(S["s_pool"], cm_free_pool)
            mk.sp.wait(S["wtk2"], S["wtk2"].n)
        mk.sp.dma(wbufs2[b][:, :], dram_flat, inc=w2sems[b])
        w2st["loaded"][seq] = (w2sems[b], w2sems[b].n)
        return seq

    tkv_freed = {}
    prev_fin = 0
    for tt in range(L1_NCM):
        sl = slice(tt * L1_CMT, (tt + 1) * L1_CMT)
        gates = [(S["s_pool"], e3_done), (S["s_dve"], cols1_done)]
        if tt > 0:
            gates.append((S["s_pool"], prev_fin))
        cm_done = mixes(sl, [(xk2_m, ftmk, ftmk1), (xr2_m, ftmr, ftmr1)],
                        h_pad3[:, :, 1:2049], h_pad3[:, :, 0:2048],
                        gates=gates)

        def evac_sig(oc, b0, mmd):
            mk.act.wait(S["s_pe"], mmd)
            mk.act.op("activation", sig_cm[:, oc, :], psum[b0][:, 0:L1_CMT],
                      AF.Sigmoid, inc=S["s_act"])
            pp.release(b0, S["s_act"])
        gemm_1024(xr2_m, wrec_d, evac_sig, gates=[(S["s_dve"], cm_done)],
                  tw=L1_CMT)
        sig_done = S["s_act"].n

        for slab in range(8):
            seq = fetch2(wkey_bf[slab].rearrange("p c o -> p (c o)"))
            mk.pe.wait(*w2st["loaded"][seq])
            buf = wbufs2[seq % 2].rearrange("p (c o) -> p c o", c=8)
            for m in range(4):
                oc = slab * 4 + m
                b0 = pp.acquire()
                for kc in range(NCH):
                    mk.pe.op("matmul", psum[b0][:, :],
                             buf[:, kc, m * 128:(m + 1) * 128],
                             xk2_m[:, kc, :], start=(kc == 0),
                             stop=(kc == NCH - 1),
                             inc=(S["s_pe"] if kc == NCH - 1 else None))
                mmd = S["s_pe"].n
                mk.dve.wait(S["s_pe"], mmd)
                mk.dve.op("tensor_scalar", kk_t[:, oc, :], psum[b0][:, :],
                          0.0, None, ALU.max, inc=S["s_dve"])
                pp.release(b0, S["s_dve"])
            w2st["gates"][seq] = S["s_pe"].n
        kkr_done = S["s_dve"].n
        mk.act.wait(S["s_dve"], kkr_done)
        for c in range(32):
            mk.act.op("activation", kk_t[:, c, :], kk_t[:, c, :], AF.Square,
                      inc=(S["s_act"] if c == 31 else None))
        kksq_done = S["s_act"].n

        for oc in range(NCH):
            seq = fetch2(wval_bf[oc].rearrange("p c o -> p (c o)"))
            mk.pe.wait(*w2st["loaded"][seq])
            mk.pe.wait(S["s_act"], kksq_done)
            buf = wbufs2[seq % 2].rearrange("p (c o) -> p c o", c=32)
            b0 = pp.acquire()
            for kc in range(32):
                mk.pe.op("matmul", psum[b0][:, 0:L1_CMT], buf[:, kc, :],
                         kk_t[:, kc, :], start=(kc == 0), stop=(kc == 31),
                         inc=(S["s_pe"] if kc == 31 else None))
            mmd = S["s_pe"].n
            w2st["gates"][seq] = S["s_pe"].n
            tb = tkv_st2[oc % 2]
            mk.dve.wait(S["s_pe"], mmd)
            mk.dve.wait(S["s_act"], sig_done)
            if oc >= 2:
                mk.dve.wait(S["s_pool"], tkv_freed[oc - 2])
            mk.dve.op("tensor_tensor", tb[:, 0:L1_CMT], psum[b0][:, 0:L1_CMT],
                      sig_cm[:, oc, :], ALU.mult, inc=S["s_dve"])
            pp.release(b0, S["s_dve"])
            mk.pool.wait(S["s_dve"], S["s_dve"].n)
            mk.pool.op("tensor_tensor", xmid_fn(oc, sl), xmid_fn(oc, sl),
                       tb[:, 0:L1_CMT], ALU.add, inc=S["s_pool"])
            tkv_freed[oc] = S["s_pool"].n
        prev_fin = S["s_pool"].n
    xfin_done = S["s_pool"].n

    bc = emit_ln_stats(mk, S, pp, lnt, xmid_fn, [lnscrA_b, lnscrA2_b],
                       [lnscrA_sq, lnscrA2_sq], NCH, T,
                       ones_col, ones_row, epsc,
                       ready=[(S["s_pool"], xfin_done)])
    x2_sems = [S["g0"], S["g1"]]
    x2_loaded = {}
    x2_freed = {}

    def x2_reload(c, gate=None):
        b = c % 2
        mk.sp.wait(S["s_pool"], xfin_done)
        if gate is not None:
            mk.sp.wait(S["s_dve"], gate)
        mk.sp.dma(x2_rl[b][:, :], x_d[c * PART:(c + 1) * PART, :],
                  inc=x2_sems[b])
        x2_loaded[c] = (x2_sems[b], x2_sems[b].n)

    x2_reload(0)
    x2_reload(1)
    mk.pool.wait(S["s_act"], bc)
    store_cnt = {}
    tmp_use = {0: 0, 1: 0}
    for c in range(NCH):
        b = c % 2
        for hf in range(2):
            hsl = slice(hf * 1024, (hf + 1) * 1024)
            mk.pool.wait(S["s_dve"], tmp_use[hf])
            mk.pool.op("tensor_tensor", tmp2[:, hf, :],
                       xmid_fn(c, hsl), mubc[:, hsl], ALU.subtract,
                       inc=S["s_pool"])
            mk.dve.wait(S["s_pool"], S["s_pool"].n)
            mk.dve.wait(*x2_loaded[c])
            mk.dve.op("tensor_tensor", tmp2[:, hf, :], tmp2[:, hf, :],
                      rbc[:, hsl], ALU.mult)
            mk.dve.op("tensor_tensor", xmid_fn(c, hsl), x2_rl[b][:, hsl],
                      tmp2[:, hf, :], ALU.add, chain=True, inc=S["s_dve"])
            tmp_use[hf] = S["s_dve"].n
        yc_done = S["s_dve"].n
        x2_freed[c] = yc_done
        mk.sp.wait(S["s_dve"], yc_done)
        mk.sp.dma(y_d[c * PART:(c + 1) * PART, 0:1024],
                  xmid_fn(c, slice(0, 1024)), inc=S["s_odma"])
        mk.sp.dma(y_d[c * PART:(c + 1) * PART, 1024:2048],
                  xmid_fn(c, slice(1024, 2048)), inc=S["s_odma"])
        store_cnt[c] = S["s_odma"].n
        if c + 2 < NCH:
            x2_reload(c + 2, gate=x2_freed[c])

    mk.sp.wait(S["s_odma"], S["s_odma"].n)
    mk.sp.flush()
    mk.pe.flush(); mk.act.flush(); mk.dve.flush(); mk.pool.flush()


def make_slabs(w, kcs, ow):
    K, N = w.shape
    n_slab = N // ow
    return np.ascontiguousarray(
        w.reshape(kcs, 128, n_slab, ow).transpose(2, 1, 0, 3))


def host_prep_l1attn(x_core, inp, h):
    d = {}
    d["x"] = np.ascontiguousarray(x_core.T)
    cols = []
    for k in ["ln1_w", "ln1_b", "ln2_w", "ln2_b"]:
        cols.append(np.asarray(inp[f"L1_attn_{k}"]))
    for k in ["att_tmk", "att_tmv", "att_tmr", "ffn_tmk", "ffn_tmr"]:
        cols.append(np.asarray(inp[f"L1_attn_{k}"]).reshape(-1))
    cols.append(np.asarray(inp["L1_attn_time_decay"]))
    cols.append(np.asarray(inp["L1_attn_time_first"]))
    d["colsd"] = np.ascontiguousarray(
        np.stack([c.reshape(8, 128).T for c in cols], 1).reshape(128, 88))
    d["wkd"] = make_slabs(np.ascontiguousarray(inp["L1_attn_Wk"].T), 8, 512)
    d["wvd"] = make_slabs(np.ascontiguousarray(inp["L1_attn_Wv"].T), 8, 512)
    d["wrd"] = make_slabs(np.ascontiguousarray(inp["L1_attn_Wr"].T), 8, 512)
    d["wod"] = make_slabs(np.ascontiguousarray(inp["L1_attn_Wo"].T), 8, 512)
    d["wkeyd"] = make_slabs(np.ascontiguousarray(inp["L1_attn_Wkey"].T), 8, 512)
    d["wvald"] = make_slabs(np.ascontiguousarray(inp["L1_attn_Wval"].T), 32, 128)
    d["wrecd"] = make_slabs(np.ascontiguousarray(inp["L1_attn_Wrec"].T), 8, 512)
    import ml_dtypes
    d["c_flag"] = np.full((128, 1), float(h), np.float32)
    d["c_iota"] = np.broadcast_to(
        np.arange(T, dtype=np.float32)[None, :], (128, T)).astype(
            ml_dtypes.bfloat16)
    d["c_eps"] = np.full((128, 1), 1e-5, np.float32)
    d["c_ones_col"] = np.ones((128, 1), ml_dtypes.bfloat16)
    d["c_ones_row"] = np.ones((1, 128), ml_dtypes.bfloat16)
    return d




"""L0_attn (gated linear attention) sandwich stage:
   y = x + LN( L0a( LN(x) ) )
x, y: [1024, 2048] fp32 feature-major per core; cores paired (h0|h1).
"""

KB = 1024
D = 1024
NCH = 8
T = 2048
L0A_NC = 16
H = 16
HD = 64
L0A_EPS = 1e-5


def emit_l0attn(mk, x_d, y_d, n_pairs=4):
    sm_used.clear(); delta_used.clear(); prefix_done.clear()
    S = mk.std_sems(extra=["wb0", "wb1", "cc1", "g0", "g1", "rcv", "s_xs",
                           "wtk"])

    wq_d = mk.din("wqd", [8, PART, 8, 128])
    wk_d = mk.din("wkd", [8, PART, 8, 128])
    wg_d = mk.din("wgd", [8, PART, 8, 128])
    wproj_d = mk.din("wprojd", [8, PART, 8, 128])
    wv_tok_d = mk.din("wvtokd", [NCH, PART, 1024])
    bqkv_d = mk.din("bqkvd", [PART, 16])
    bv_row_d = mk.din("bvrowd", [1, 1024])
    bg_d = mk.din("bgd", [PART, 8])
    bproj_d = mk.din("bprojd", [PART, 8])
    lng_d = mk.din("lngd", [PART, 8])
    lnb_d = mk.din("lnbd", [PART, 8])
    mask_d = mk.din("c_mask", [PART, PART], BF16)
    eye_d = mk.din("c_eye", [PART, PART], BF16)
    flag_d = mk.din("c_flag", [PART, 1])
    eps_d = mk.din("c_eps", [PART, 1])
    onc_d = mk.din("c_ones_col", [PART, 1], BF16)
    onr_d = mk.din("c_ones_row", [1, PART], BF16)

    st_src = mk.dtmp("st_src", [HD, H * 65], BF16)
    st_dst = mk.dtmp("st_dst", [2 * HD, H * 65], BF16)
    groups = [[2 * i, 2 * i + 1] for i in range(n_pairs)]

    arena = mk.sbuf("arena", [PART, 196 * KB], mybir.dt.uint8)

    def view(off_kb, kb, dtype, shape=None):
        ap = arena[:, int(off_kb * KB):int((off_kb + kb) * KB)].bitcast(dtype)
        if shape is None:
            return ap
        return ap.rearrange("p (a b) -> p a b", a=shape[0])

    xi = view(0, 64, F32, [NCH, T])
    q2 = view(0, 32, BF16, [NCH, T])
    v_t4 = arena[:, 32 * KB:32 * KB + L0A_NC * H * 64 * 2].bitcast(BF16).rearrange(
        "p (c h a) -> p c h a", c=L0A_NC, h=H)
    o_t = view(0, 64, F32, [NCH, T])
    xn = view(64, 32, BF16, [NCH, T])
    k2t = view(64, 32, BF16).rearrange("p (c g d) -> p c g d", c=L0A_NC, g=8)
    lnscrF_b = view(64, 8, BF16, [NCH, 512])
    lnscrF_sq = view(72, 8, BF16, [NCH, 512])
    x2_rl = [view(80, 8, F32), view(88, 8, F32)]
    xib = view(96, 32, BF16, [NCH, T])
    k2f = view(96, 32, BF16, [NCH, T])
    tmp2 = view(128, 8, F32, [2, 1024])
    mubc = view(136, 4, BF16)
    rbc = view(140, 4, BF16)
    gate_f = view(128, 32, BF16, [NCH, T])
    attn = view(128, 32, BF16, [NCH, T])
    lnscr_b = view(160, 8, BF16, [NCH, 512])
    lnscr_sq = view(168, 8, BF16, [NCH, 512])
    wtokb = view(160, 16, BF16, [NCH, 1024])
    rgbc = view(176, 4, BF16)
    sm_stage = [arena[:, 176 * KB:176 * KB + 256].bitcast(BF16),
                arena[:, 176 * KB + 256:176 * KB + 512].bitcast(BF16)]
    rb_dv = view(177, 1, BF16)
    delta_st = [arena[:, 178 * KB:178 * KB + 130].bitcast(BF16),
                arena[:, 178 * KB + 256:178 * KB + 386].bitcast(BF16)]
    bvrow = view(180, 2, BF16)[0:1, 0:1024]
    mu_f = view(182, 2, F32)[0:1, 0:512]
    tA = view(184, 2, F32)[0:1, 0:512]
    tB = view(186, 2, F32)[0:1, 0:512]
    r_f = view(188, 2, F32)[0:1, 0:512]
    mu_bf = view(190, 1, BF16)[0:1, 0:512]
    r_bf = view(191, 1, BF16)[0:1, 0:512]
    wbufs = [view(192, 2, BF16), view(194, 2, BF16)]

    state_h = mk.sbuf("state_h", [PART, L0A_NC, 65], BF16)
    ends = mk.sbuf("ends", [HD, H * 65], BF16)
    init_st = mk.sbuf("init_st", [PART, H, 65], BF16)
    rcv_sb = mk.sbuf("rcv_sb", [HD, H * 65], BF16)
    ones_col = mk.sbuf("ones_col", [PART, 1], BF16)
    ones_row = mk.sbuf("ones_row", [1, PART], BF16)
    epsc = mk.sbuf("epsc", [PART, 1], F32)
    flagc = mk.sbuf("flagc", [PART, 1], F32)
    maskt = mk.sbuf("maskt", [PART, PART], BF16)
    eyet = mk.sbuf("eyet", [PART, PART], BF16)
    bqkv = mk.sbuf("bqkv", [PART, 16], F32)
    bg_c = mk.sbuf("bg_c", [PART, 8], F32)
    bproj_c = mk.sbuf("bproj_c", [PART, 8], F32)
    lng_c = mk.sbuf("lng_c", [PART, 8], F32)
    lnb_c = mk.sbuf("lnb_c", [PART, 8], F32)

    psum = [mk.psum(f"pb{i}", [PART, 512], F32) for i in range(6)]
    pt = [mk.psum("pt0", [PART, 128], BF16), mk.psum("pt1", [PART, 128], BF16)]
    ps_guard = [None] * 6
    ps_rr = [0]

    def ps_acquire():
        i = ps_rr[0] % 6
        ps_rr[0] += 1
        g = ps_guard[i]
        if g is not None:
            mk.pe.wait(g[0], g[1])
            ps_guard[i] = None
        return i

    def ps_release(i, sem):
        ps_guard[i] = (sem, sem.n)

    class PPShim:
        acquire = staticmethod(ps_acquire)
        release = staticmethod(ps_release)
        banks = psum
    pp = PPShim()

    lnt = LNTiles(mu_f, tA, tB, r_f, mu_bf, r_bf, mubc, rbc)
    ws = WStream(mk, S, "w", wbufs, [S["wb0"], S["wb1"]])

    for t_, d_ in [(ones_col, onc_d), (ones_row, onr_d), (epsc, eps_d),
                   (flagc, flag_d), (maskt, mask_d), (eyet, eye_d),
                   (bqkv, bqkv_d), (bg_c, bg_d),
                   (bproj_c, bproj_d), (lng_c, lng_d), (lnb_c, lnb_d)]:
        mk.sp.dma(t_[:, :], d_[:, :], inc=S["s_xdma"])
    consts_loaded = S["s_xdma"].n
    mk.pool.dma(bvrow, bv_row_d[:, :], inc=S["wtk"])
    bvrow_loaded = S["wtk"].n
    for c in range(NCH):
        mk.sp.dma(xi[:, c, :], x_d[c * PART:(c + 1) * PART, :], inc=S["s_xs"])
    x_loaded = S["s_xs"].n

    mk.pool.wait(S["s_xdma"], consts_loaded)
    bc = emit_ln_stats(mk, S, pp, lnt, xi, [lnscr_b, lnscrF_b],
                       [lnscr_sq, lnscrF_sq], NCH, T,
                       ones_col, ones_row, epsc, ready=[(S["s_xs"], x_loaded)])
    mk.pool.wait(S["s_act"], bc)
    uses = []
    i = 0
    for c in range(NCH):
        for hf in range(2):
            hsl = slice(hf * 1024, (hf + 1) * 1024)
            ppg = i % 2
            if i >= 2:
                mk.pool.wait(S["s_dve"], uses[i - 2])
            mk.pool.op("tensor_tensor", tmp2[:, ppg, :], xi[:, c, hsl],
                       mubc[:, hsl], ALU.subtract, inc=S["s_pool"])
            mk.dve.wait(S["s_pool"], S["s_pool"].n)
            mk.dve.op("tensor_tensor", xi[:, c, hsl], tmp2[:, ppg, :],
                      rbc[:, hsl], ALU.mult, inc=S["s_dve"])
            uses.append(S["s_dve"].n)
            i += 1
    xi_done = S["s_dve"].n

    bc = emit_ln_stats(mk, S, pp, lnt, xi, [lnscr_b, lnscrF_b],
                       [lnscr_sq, lnscrF_sq], NCH, T,
                       ones_col, ones_row, epsc, ready=[(S["s_dve"], xi_done)])
    xn_done = emit_ln_apply(mk, S, lnt, xi, tmp2, bc, NCH, dst_bf=xn,
                            gamma=lng_c, beta=lnb_c, W=T)
    mk.pool.wait(S["s_dve"], xn_done)
    for c in range(NCH):
        mk.pool.op("tensor_copy", xib[:, c, :], xi[:, c, :],
                   inc=(S["s_pool"] if c == NCH - 1 else None))
    xib_done = S["s_pool"].n

    def gemm_fm(x_src, w_dram, evac, gates=()):
        for slab in range(8):
            seq = ws.fetch(w_dram[slab].rearrange("p c o -> p (c o)"),
                           gate=ws.gates.get(ws.count - 2))
            ws.wait_pe(seq)
            for g, v in gates:
                mk.pe.wait(g, v)
            buf = ws.buf(seq).rearrange("p (c o) -> p c o", c=8)
            oc = slab
            for tt in range(4):
                tsl = slice(tt * 512, (tt + 1) * 512)
                b0 = ps_acquire()
                for kc in range(NCH):
                    mk.pe.op("matmul", psum[b0][:, :],
                             buf[:, kc, :].opt(),
                             x_src[:, kc, tsl], start=(kc == 0),
                             stop=(kc == NCH - 1),
                             inc=(S["s_pe"] if kc == NCH - 1 else None))
                evac(oc, tt, b0, S["s_pe"].n)
            ws.gates[seq] = S["s_pe"].n

    def evac_gate(oc, tt, b0, mmd):
        mk.act.wait(S["s_pe"], mmd)
        mk.act.op("activation", gate_f[:, oc, tt * 512:(tt + 1) * 512],
                  psum[b0][:, :], AF.Sigmoid, bias=bg_c[:, oc:oc + 1],
                  inc=S["s_act"])
        ps_release(b0, S["s_act"])
    gemm_fm(xib, wg_d, evac_gate,
            gates=[(S["s_pool"], xib_done), (S["s_dve"], xn_done)])
    gate_pe_end = S["s_pe"].n
    gatef_done = S["s_act"].n

    for tt in range(4):
        tsl = slice(tt * 512, (tt + 1) * 512)
        ba = ps_acquire()
        mk.pe.wait(S["s_act"], gatef_done)
        rows = psum[ba][0:1, :]
        for c in range(NCH):
            mk.pe.op("matmul", rows, ones_col[:, :], gate_f[:, c, tsl],
                     start=(c == 0), stop=(c == NCH - 1),
                     inc=(S["s_pe"] if c == NCH - 1 else None))
        mk.dve.wait(S["s_pe"], S["s_pe"].n)
        mk.dve.op("tensor_scalar", mu_f[:, :], rows, 1.0 / D, L0A_EPS,
                  ALU.mult, ALU.add, chain=True)
        mk.dve.op("reciprocal", mu_f[:, :], mu_f[:, :], chain=True)
        mk.dve.op("tensor_copy", mu_bf[:, :], mu_f[:, :], chain=True,
                  inc=S["s_dve"])
        ps_release(ba, S["s_dve"])
        bb = ps_acquire()
        mk.pe.wait(S["s_dve"], S["s_dve"].n)
        mk.pe.op("matmul", psum[bb][:, :], ones_row[:, :], mu_bf[:, :],
                 start=True, stop=True, inc=S["s_pe"])
        mk.act.wait(S["s_pe"], S["s_pe"].n)
        mk.act.op("activation", rgbc[:, tsl], psum[bb][:, :], AF.Copy,
                  inc=S["s_act"])
        ps_release(bb, S["s_act"])
    rgbc_done = S["s_act"].n

    def mk_evac(dst, bias_c, boff):
        def evac(oc, tt, b0, mmd):
            mk.act.wait(S["s_pe"], mmd)
            mk.act.op("activation", dst[:, oc, tt * 512:(tt + 1) * 512],
                      psum[b0][:, :], AF.Identity,
                      bias=bias_c[:, boff + oc:boff + oc + 1], inc=S["s_act"])
            ps_release(b0, S["s_act"])
        return evac
    gemm_fm(xn, wq_d, mk_evac(q2, bqkv, 0), gates=[(S["s_dve"], xn_done)])
    q_done = S["s_act"].n

    def relu2_fm(tile, gdone):
        mk.pool.wait(S["s_act"], gdone)
        mk.pool.wait(S["s_act"], rgbc_done)
        for c in range(NCH):
            mk.pool.op("tensor_tensor", tile[:, c, :], tile[:, c, :],
                       gate_f[:, c, :], ALU.mult, chain=True,
                       inc=S["s_pool"])
            mk.pool.op("tensor_tensor", tile[:, c, :], tile[:, c, :],
                       rgbc[:, :], ALU.mult, chain=True, inc=S["s_pool"])
            mk.dve.wait(S["s_pool"], S["s_pool"].n)
            mk.dve.op("scalar_tensor_tensor", tile[:, c, :], tile[:, c, :],
                      0.0, tile[:, c, :], ALU.max, ALU.mult, chain=True)
            mk.dve.op("tensor_scalar", tile[:, c, :], tile[:, c, :],
                      L0A_EPS, None, ALU.add, chain=True, inc=S["s_dve"])
        return S["s_dve"].n

    q2_done = relu2_fm(q2, q_done)

    for kc in range(NCH):
        mk.pool.dma(wtokb[:, kc, :], wv_tok_d[kc], inc=S["wtk"])
    wv_loaded = S["wtk"].n

    for tc in range(L0A_NC):
        csl = slice(tc * 128, (tc + 1) * 128)
        b0 = ps_acquire()
        b1 = ps_acquire()
        mk.pe.wait(S["wtk"], wv_loaded)
        mk.pe.wait(S["wtk"], bvrow_loaded)
        mk.pe.wait(S["s_dve"], xn_done)
        for kc in range(NCH):
            lhs = xn[:, kc, csl].opt()
            mk.pe.op("matmul", psum[b0][:, :], lhs,
                     wtokb[:, kc, 0:512].opt(), start=(kc == 0), stop=False)
            mk.pe.op("matmul", psum[b1][:, :], lhs,
                     wtokb[:, kc, 512:1024].opt(), start=(kc == 0), stop=False)
        mk.pe.op("matmul", psum[b0][:, :], ones_row[:, 0:128],
                 bvrow[:, 0:512], start=False, stop=True)
        mk.pe.op("matmul", psum[b1][:, :], ones_row[:, 0:128],
                 bvrow[:, 512:1024], start=False, stop=True, inc=S["s_pe"])
        mmd = S["s_pe"].n
        mk.act.wait(S["s_pe"], mmd)
        mk.act.wait(S["s_pe"], gate_pe_end)
        mk.act.op("activation", v_t4[:, tc, 0:8, :],
                  psum[b0][:, :].rearrange("p (a b) -> p a b", a=8), AF.Copy)
        mk.act.op("activation", v_t4[:, tc, 8:16, :],
                  psum[b1][:, :].rearrange("p (a b) -> p a b", a=8), AF.Copy,
                  inc=S["s_act"])
        ps_release(b0, S["s_act"])
        ps_release(b1, S["s_act"])
    v_done = S["s_act"].n
    vtok_pe_end = S["s_pe"].n

    gemm_fm(xn, wk_d, mk_evac(k2f, bqkv, 8),
            gates=[(S["s_dve"], xn_done), (S["s_pe"], vtok_pe_end)])
    k_done = S["s_act"].n
    k2f_done = relu2_fm(k2f, k_done)

    mk.pe.wait(S["s_dve"], k2f_done)
    mk.pe.wait(S["s_dve"], q2_done)
    tp_hist = {}
    for idx in range(L0A_NC * NCH):
        tc = idx // NCH
        hp = idx % NCH
        b = idx % 2
        g = tp_hist.get(idx - 2)
        if g is not None:
            mk.pe.wait(S["s_act"], g)
        mk.pe.op("transpose", pt[b][:, :],
                 k2f[:, hp, tc * 128:(tc + 1) * 128].opt(), eyet[:, :],
                 inc=S["s_pe"])
        mk.act.wait(S["s_pe"], S["s_pe"].n)
        mk.act.op("activation", k2t[:, tc, hp, :], pt[b][:, :], AF.Copy,
                  inc=S["s_act"])
        tp_hist[idx] = S["s_act"].n
    k2t_done = S["s_act"].n

    bank_guard = {}

    def use_bank(i, eng=None):
        g = bank_guard.get(i)
        if g is not None:
            (eng or mk.pe).wait(g[0], g[1])
            bank_guard[i] = None

    for h in range(H):
        hp = h // 2
        hr = (h % 2) * 64
        bU = h % 4
        use_bank(bU)
        stp = psum[bU]
        mk.pe.wait(S["s_act"], k2t_done)
        mk.pe.wait(S["s_act"], v_done)
        for tc in range(L0A_NC):
            lhs = k2t[:, tc, hp, hr:hr + 64].opt()
            mk.pe.op("matmul", stp[0:64, 0:64], lhs, v_t4[:, tc, h, :].opt(),
                     start=(tc == 0), stop=False)
            mk.pe.op("matmul", stp[0:64, 64:65], lhs, ones_col[:, :],
                     start=False, stop=(tc == L0A_NC - 1),
                     inc=(S["s_pe"] if tc == L0A_NC - 1 else None))
        mk.act.wait(S["s_pe"], S["s_pe"].n)
        mk.act.op("activation", ends[:, h * 65:(h + 1) * 65], stp[0:64, 0:65],
                  AF.Copy, inc=S["s_act"])
        bank_guard[bU] = (S["s_act"], S["s_act"].n)
    ends_done = S["s_act"].n

    mk.pool.wait(S["s_act"], ends_done)
    mk.pool.dma(st_src[:, :], ends[:, :], inc=S["rcv"])
    mk.pool.wait(S["rcv"], S["rcv"].n)
    mk.pool.op("collective_compute", "AllGather", ALU.bypass,
               replica_groups=groups, ins=[st_src[:, :].opt()],
               outs=[st_dst[:, :].opt()], inc=S["cc1"], inc_by=1,
               inc_default=True)
    mk.pool.wait(S["cc1"], S["cc1"].n)
    mk.pool.dma(rcv_sb[:, :], st_dst[0:HD, :], inc=S["rcv"])
    mk.pool.wait(S["rcv"], S["rcv"].n)
    mk.pool.wait(S["s_xdma"], consts_loaded)
    mk.pool.op("tensor_scalar",
               init_st[0:HD, :, :].rearrange("p a b -> p (a b)"),
               rcv_sb[:, :], flagc[0:HD, 0:1], None, ALU.mult,
               inc=S["s_pool"])
    mk.pool.op("tensor_scalar",
               init_st[HD:PART, :, :].rearrange("p a b -> p (a b)"),
               rcv_sb[:, :], flagc[0:HD, 0:1], None, ALU.mult,
               inc=S["s_pool"])
    init_done = S["s_pool"].n

    sm16 = view(160, 4, BF16).rearrange("p (a b) -> p a b", a=L0A_NC)
    mk.pe.wait(S["s_pool"], init_done)
    mk.dve.wait(S["s_dve"], S["s_dve"].n)
    prev_av_pe = 0
    for h in range(H):
        hp = h // 2
        hr = (h % 2) * 64
        sm_ready = {}
        for c in range(L0A_NC):
            csl = slice(c * 128, (c + 1) * 128)
            kfs = k2f[hr:hr + 64, hp, csl].opt()
            qfs = q2[hr:hr + 64, hp, csl].opt()
            ktk = k2t[:, c, hp, hr:hr + 64].opt()
            vtk = v_t4[:, c, h, :].opt()
            bS = 3 + (c % 2)
            use_bank(bS)
            mk.pe.op("matmul", psum[bS][0:128, 0:128], kfs, qfs,
                     start=True, stop=True, inc=S["s_pe"])
            mk.dve.wait(S["s_pe"], S["s_pe"].n)
            mk.dve.wait(S["s_pe"], prev_av_pe)
            mk.dve.op("tensor_tensor", sm16[:, c, :],
                      psum[bS][0:128, 0:128], maskt[:, :], ALU.mult,
                      inc=S["s_dve"])
            sm_ready[c] = S["s_dve"].n
            bank_guard[bS] = (S["s_dve"], S["s_dve"].n)
            if c < L0A_NC - 1:
                bD = [0, 1, 2, 5][c % 4]
                use_bank(bD)
                stp = psum[bD]
                mk.pe.op("matmul", stp[0:64, 0:64], ktk, vtk,
                         start=True, stop=False)
                mk.pe.op("matmul", stp[64:128, 0:64], ktk, vtk,
                         start=True, stop=False, tile_position=(0, 64),
                         skip_group_check=True)
                mk.pe.op("matmul", stp[0:64, 64:65], ktk,
                         ones_col[:, :], start=False, stop=True)
                mk.pe.op("matmul", stp[64:128, 64:65], ktk,
                         ones_col[:, :], start=False, stop=True,
                         tile_position=(0, 64), skip_group_check=True,
                         inc=S["s_pe"])
                mk.act.wait(S["s_pe"], S["s_pe"].n)
                mk.act.wait(S["s_pe"], prev_av_pe)
                mk.act.op("activation", state_h[:, c, :],
                          stp[0:128, 0:65], AF.Copy, inc=S["s_act"])
                bank_guard[bD] = (S["s_act"], S["s_act"].n)
        deltas_done = S["s_act"].n
        mk.dve.wait(S["s_act"], deltas_done)
        for c in range(1, L0A_NC - 1):
            mk.dve.op("tensor_tensor", state_h[:, c, :],
                      state_h[:, c - 1, :], state_h[:, c, :], ALU.add,
                      chain=True, inc=(S["s_dve"] if c == L0A_NC - 2
                                       else None))
        prefix_all = S["s_dve"].n
        for cg in range(4):
            bAV = 1 + (cg % 2)
            use_bank(bAV)
            av = psum[bAV]
            gsl = slice(cg * 512, (cg + 1) * 512)
            mk.pe.wait(S["s_dve"], prefix_all)
            for ci in range(4):
                c = cg * 4 + ci
                csl = slice(c * 128, (c + 1) * 128)
                fo = slice(ci * 128, (ci + 1) * 128)
                qfs = q2[hr:hr + 64, hp, csl].opt()
                vtk = v_t4[:, c, h, :].opt()
                smb = sm16[:, c, :]
                first = (ci == 0)
                mk.pe.wait(S["s_dve"], sm_ready[c])
                mk.pe.op("matmul", av[0:65, fo],
                         init_st[hr:hr + 64, h, :].opt(), qfs,
                         start=first, stop=False)
                mk.pe.op("matmul", av[0:64, fo], vtk, smb,
                         start=False, stop=False)
                if c > 0:
                    mk.pe.op("matmul", av[64:65, fo], ones_col[:, :],
                             smb, start=False, stop=False)
                    mk.pe.op("matmul", av[0:65, fo],
                             state_h[hr:hr + 64, c - 1, :].opt(), qfs,
                             start=False, stop=(ci == 3), inc=S["s_pe"])
                else:
                    mk.pe.op("matmul", av[64:65, fo], ones_col[:, :],
                             smb, start=False, stop=(ci == 3),
                             inc=S["s_pe"])
            av_mmd = S["s_pe"].n
            mk.dve.wait(S["s_pe"], av_mmd)
            mk.dve.op("reciprocal", r_f[:, :], av[64:65, :], chain=True)
            mk.dve.op("tensor_copy", r_bf[:, :], r_f[:, :], chain=True,
                      inc=S["s_dve"])
            bdv = 5
            use_bank(bdv)
            mk.pe.wait(S["s_dve"], S["s_dve"].n)
            mk.pe.op("matmul", psum[bdv][:, :], ones_row[:, :], r_bf[:, :],
                     start=True, stop=True, inc=S["s_pe"])
            mk.act.wait(S["s_pe"], S["s_pe"].n)
            mk.act.op("activation", rb_dv[:, :], psum[bdv][:, :], AF.Copy,
                      inc=S["s_act"])
            bank_guard[bdv] = (S["s_act"], S["s_act"].n)
            mk.dve.wait(S["s_act"], S["s_act"].n)
            mk.dve.op("tensor_tensor", attn[hr:hr + 64, hp, gsl],
                      av[0:64, :], rb_dv[0:64, :], ALU.mult, inc=S["s_dve"])
            bank_guard[bAV] = (S["s_dve"], S["s_dve"].n)
        prev_av_pe = S["s_pe"].n
    attn_done = S["s_dve"].n

    def evac_o(oc, tt, b0, mmd):
        mk.act.wait(S["s_pe"], mmd)
        mk.act.op("activation", o_t[:, oc, tt * 512:(tt + 1) * 512],
                  psum[b0][:, :], AF.Identity, bias=bproj_c[:, oc:oc + 1],
                  inc=S["s_act"])
        ps_release(b0, S["s_act"])
    for i_ in range(6):
        bank_guard[i_] = (S["s_dve"], attn_done)
    ps_rr[0] = 0
    for i_ in range(6):
        ps_guard[i_] = bank_guard.get(i_)
    gemm_fm(attn, wproj_d, evac_o, gates=[(S["s_dve"], attn_done)])
    o_done = S["s_act"].n

    bc = emit_ln_stats(mk, S, pp, lnt, o_t, [lnscrF_b, lnscr_b],
                       [lnscrF_sq, lnscr_sq], NCH, T,
                       ones_col, ones_row, epsc, ready=[(S["s_act"], o_done)])
    x2_sems = [S["g0"], S["g1"]]
    x2_loaded = {}
    x2_freed = {}

    def x2_reload(c, gate=None):
        b = c % 2
        mk.sp.wait(S["s_act"], o_done)
        if gate is not None:
            mk.sp.wait(S["s_dve"], gate)
        mk.sp.dma(x2_rl[b][:, :], x_d[c * PART:(c + 1) * PART, :],
                  inc=x2_sems[b])
        x2_loaded[c] = (x2_sems[b], x2_sems[b].n)

    x2_reload(0)
    x2_reload(1)
    mk.pool.wait(S["s_act"], bc)
    tmp_use = {0: 0, 1: 0}
    for c in range(NCH):
        b = c % 2
        for hf in range(2):
            hsl = slice(hf * 1024, (hf + 1) * 1024)
            mk.pool.wait(S["s_dve"], tmp_use[hf])
            mk.pool.op("tensor_tensor", tmp2[:, hf, :], o_t[:, c, hsl],
                       mubc[:, hsl], ALU.subtract, inc=S["s_pool"])
            mk.dve.wait(S["s_pool"], S["s_pool"].n)
            mk.dve.wait(*x2_loaded[c])
            mk.dve.op("tensor_tensor", tmp2[:, hf, :], tmp2[:, hf, :],
                      rbc[:, hsl], ALU.mult)
            mk.dve.op("tensor_tensor", o_t[:, c, hsl], x2_rl[b][:, hsl],
                      tmp2[:, hf, :], ALU.add, chain=True, inc=S["s_dve"])
            tmp_use[hf] = S["s_dve"].n
        yc_done = S["s_dve"].n
        x2_freed[c] = yc_done
        mk.sp.wait(S["s_dve"], yc_done)
        mk.sp.dma(y_d[c * PART:(c + 1) * PART, :], o_t[:, c, :],
                  inc=S["s_odma"])
        if c + 2 < NCH:
            x2_reload(c + 2, gate=x2_freed[c])

    mk.sp.wait(S["s_odma"], S["s_odma"].n)
    mk.sp.flush()
    mk.pe.flush(); mk.act.flush(); mk.dve.flush(); mk.pool.flush()


sm_used = {}
delta_used = {}
prefix_done = {}


def make_slabs(w, kcs, ow):
    K, N = w.shape
    n_slab = N // ow
    return np.ascontiguousarray(
        w.reshape(kcs, 128, n_slab, ow).transpose(2, 1, 0, 3))


def host_prep_l0attn(x_core, inp, h):
    d = {}
    d["x"] = np.ascontiguousarray(x_core.T)
    Wqkv = np.asarray(inp["L0_attn_W_qkv"])
    bqkv = np.asarray(inp["L0_attn_b_qkv"])
    d["wqd"] = make_slabs(np.ascontiguousarray(Wqkv[:, 0:1024]), 8, 128)
    d["wkd"] = make_slabs(np.ascontiguousarray(Wqkv[:, 1024:2048]), 8, 128)
    d["wgd"] = make_slabs(np.asarray(inp["L0_attn_W_gate"]), 8, 128)
    d["wprojd"] = make_slabs(np.asarray(inp["L0_attn_W_proj"]), 8, 128)
    d["wvtokd"] = np.ascontiguousarray(Wqkv[:, 2048:3072].reshape(8, 128, 1024))
    d["bqkvd"] = np.ascontiguousarray(
        np.concatenate([bqkv[0:1024].reshape(8, 128).T,
                        bqkv[1024:2048].reshape(8, 128).T], axis=1))
    d["bvrowd"] = bqkv[2048:3072].reshape(1, 1024).astype(np.float32)
    d["bgd"] = np.ascontiguousarray(
        np.asarray(inp["L0_attn_b_gate"]).reshape(8, 128).T)
    d["bprojd"] = np.ascontiguousarray(
        np.asarray(inp["L0_attn_b_proj"]).reshape(8, 128).T)
    d["lngd"] = np.ascontiguousarray(
        np.asarray(inp["L0_attn_ln_g"]).reshape(8, 128).T)
    d["lnbd"] = np.ascontiguousarray(
        np.asarray(inp["L0_attn_ln_b"]).reshape(8, 128).T)
    import ml_dtypes
    d["c_mask"] = np.triu(np.ones((128, 128))).astype(ml_dtypes.bfloat16)
    d["c_eye"] = np.eye(128).astype(ml_dtypes.bfloat16)
    d["c_flag"] = np.full((128, 1), float(h), np.float32)
    d["c_eps"] = np.full((128, 1), 1e-5, np.float32)
    d["c_ones_col"] = np.ones((128, 1), ml_dtypes.bfloat16)
    d["c_ones_row"] = np.ones((1, 128), ml_dtypes.bfloat16)
    return d




"""L0_ffn per-layer launch builders (host-sliced halos, no collectives)."""

KB = 1024
D = 1024
NCH = 8
T = 2048
L0F_TE = 2304
L0F_NTC = 18
H = 16
HD = 64
L0F_NQT = 4
OFFS = [-256, -128, 0, 128, 256, 384]


def emit_l0f_attn(mk, xh_d, y_d, tail=None):
    S = mk.std_sems(extra=["wb0", "wb1", "wtk", "g0", "g1", "s_xs", "s_tb",
                           "s_xa"])

    wq_d = mk.din("wqd", [4, PART, 8, 256])
    wk_d = mk.din("wkd", [4, PART, 8, 256])
    wv_tok_d = mk.din("wvtokd", [NCH, PART, 1024])
    wo_d = mk.din("wod", [4, PART, 8, 256])
    csq_d = mk.din("csqd", [PART, L0F_TE], BF16)
    ssq_d = mk.din("ssqd", [PART, L0F_TE], BF16)
    csk_d = mk.din("cskd", [PART, L0F_TE], BF16)
    ssk_d = mk.din("sskd", [PART, L0F_TE], BF16)
    qs_d = mk.din("qsd", [PART, 4])
    rt_d = mk.din("c_rt", [PART, PART], BF16)
    e16_d = mk.din("c_e16", [PART, NCH * 16], BF16)
    sel_d = mk.din("c_sel", [16, NCH * PART], BF16)
    mpre_d = mk.din("c_mpre", [8, PART, 512], BF16)
    eye_d = mk.din("c_eye", [PART, PART], BF16)
    onc_d = mk.din("c_ones_col", [PART, 1], BF16)
    onr_d = mk.din("c_ones_row", [1, PART], BF16)

    arena = mk.sbuf("arena", [PART, 196 * KB], mybir.dt.uint8)

    def view(off_kb, kb, dtype, shape=None):
        ap = arena[:, int(off_kb * KB):int((off_kb + kb) * KB)].bitcast(dtype)
        if shape is None:
            return ap
        return ap.rearrange("p (a b) -> p a b", a=shape[0])

    q_t = view(0, 32, BF16, [NCH, T])
    v_t4 = arena[:, 32 * KB:32 * KB + L0F_NTC * H * 65 * 2].bitcast(BF16).rearrange(
        "p (c h a) -> p c h a", c=L0F_NTC, h=H)
    exp_st = view(69, 2, F32)
    smB = view(71, 6, BF16).rearrange("p (a b) -> p a b", a=6)
    rnb_t = view(77, 5, BF16)
    t2_st = view(82, 1, BF16)
    rb_dv = view(83, 1, BF16)
    xrl = [view(84, 8, F32), view(92, 8, F32)]
    sx = view(100, 36, BF16, [NCH, L0F_TE])
    attn = view(100, 32, BF16, [NCH, T])
    k_t = view(136, 36, BF16, [NCH, L0F_TE])
    wtokb = view(172, 16, BF16, [NCH, 1024])
    tblslc = [view(172, 1, BF16), view(173, 1, BF16),
              view(174, 1, BF16), view(175, 1, BF16)]
    wbufs = [view(188, 4, BF16), view(192, 4, BF16)]
    mpre_t = view(84, 8, BF16).rearrange("p (a b) -> p a b", a=8)
    rb_dvs = [view(83, 1, BF16), view(79, 1, BF16)]
    csq_r = view(119, 4, BF16)
    ssq_r = view(123, 4, BF16)
    csk_r = view(127, 4.5, BF16)
    ssk_r = view(131.5, 4.5, BF16)

    rows16 = view(100, 10, F32)[0:16, :]
    rnq_bf = view(110, 4, BF16)[0:16, :]
    rnk_bf = view(114, 5, BF16)[0:16, :]
    e16 = mk.sbuf("e16", [PART, NCH, 16], BF16)
    sel = mk.sbuf("sel", [16, NCH, PART], BF16)
    rt_t = mk.sbuf("rt_t", [PART, PART], BF16)
    eyet = mk.sbuf("eyet", [PART, PART], BF16)
    qs_c = mk.sbuf("qs_c", [PART, 4], F32)
    ones_col = mk.sbuf("ones_col", [PART, 1], BF16)
    ones_row = mk.sbuf("ones_row", [1, PART], BF16)
    r_f = mk.sbuf("r_f", [1, 512], F32)
    r_bf = mk.sbuf("r_bf", [1, 512], BF16)
    r_bf2 = mk.sbuf("r_bf2", [1, 512], BF16)
    r_bfs = [r_bf, r_bf2]

    psum = [mk.psum(f"pb{i}", [PART, 512], F32) for i in range(8)]
    ps_guard = [None] * 8
    ps_rr = [0]

    def ps_acquire():
        i = 3 + (ps_rr[0] % 5)
        ps_rr[0] += 1
        g = ps_guard[i]
        if g is not None:
            mk.pe.wait(g[0], g[1])
            ps_guard[i] = None
        return i

    def use_bank(i):
        g = ps_guard[i]
        if g is not None:
            mk.pe.wait(g[0], g[1])
            ps_guard[i] = None

    def ps_release(i, sem):
        ps_guard[i] = (sem, sem.n)

    ws = WStream(mk, S, "w", wbufs, [S["wb0"], S["wb1"]])

    mk.sp.dma(e16[:, :, :].rearrange("p a b -> p (a b)"), e16_d[:, :],
              inc=S["s_xdma"])
    mk.sp.dma(sel[:, :, :].rearrange("p a b -> p (a b)"), sel_d[:, :],
              inc=S["s_xdma"])
    mk.sp.dma(rt_t[:, :], rt_d[:, :], inc=S["s_xdma"])
    mk.sp.dma(eyet[:, :], eye_d[:, :], inc=S["s_xdma"])
    mk.sp.dma(qs_c[:, :], qs_d[:, :], inc=S["s_xdma"])
    mk.sp.dma(ones_col[:, :], onc_d[:, :], inc=S["s_xdma"])
    mk.sp.dma(ones_row[:, :], onr_d[:, :], inc=S["s_xdma"])
    consts_loaded = S["s_xdma"].n

    for c in range(NCH):
        off = 2 if c < 4 else 1
        mk.pool.dma(sx[:, c, :],
                    xh_d[c * PART:(c + 1) * PART, off:off + L0F_TE],
                    inc=S["s_xa"])
    sx_done = S["s_xa"].n

    def gemm_in(w_dram, n_slab, dst, width, tok_off, gates=(), evac=None):
        for slab in range(n_slab):
            mk.pool.wait(S["s_xa"], sx_done)
            seq = ws.fetch(w_dram[slab].rearrange("p c o -> p (c o)"),
                           gate=ws.gates.get(ws.count - 2))
            ws.wait_pe(seq)
            for g, v in gates:
                mk.pe.wait(g, v)
            buf = ws.buf(seq).rearrange("p (c o) -> p c o", c=8)
            for m in range(2):
                oc = slab * 2 + m
                for tt in range((width + 511) // 512):
                    w0 = tt * 512
                    wdt = min(512, width - w0)
                    b0 = ps_acquire()
                    for kc in range(NCH):
                        mk.pe.op("matmul", psum[b0][:, 0:wdt],
                                 buf[:, kc, m * 128:(m + 1) * 128].opt(),
                                 sx[:, kc, tok_off + w0:tok_off + w0 + wdt],
                                 start=(kc == 0), stop=(kc == NCH - 1),
                                 inc=(S["s_pe"] if kc == NCH - 1 else None))
                    mk.act.wait(S["s_pe"], S["s_pe"].n)
                    mk.act.op("activation", dst[:, oc, w0:w0 + wdt],
                              psum[b0][:, 0:wdt], AF.Copy, inc=S["s_act"])
                    ps_release(b0, S["s_act"])
            ws.gates[seq] = S["s_pe"].n

    gemm_in(wq_d, 4, q_t, T, 256)
    q_raw_done = S["s_act"].n
    gemm_in(wk_d, 4, k_t, L0F_TE, 0)
    k_raw_done = S["s_act"].n
    qk_pe_end = S["s_pe"].n

    mk.pool.wait(S["s_pe"], qk_pe_end)
    for kc in range(NCH):
        mk.pool.dma(wtokb[:, kc, :], wv_tok_d[kc], inc=S["wtk"])
    wv_loaded = S["wtk"].n
    mk.pool.op("memset", v_t4[:, :, :, 64:65], 1.0, inc=S["s_pool"])
    vones = S["s_pool"].n
    for tc in range(L0F_NTC):
        csl = slice(tc * 128, (tc + 1) * 128)
        b0 = ps_acquire()
        b1 = ps_acquire()
        mk.pe.wait(S["wtk"], wv_loaded)
        mk.pe.wait(S["s_xa"], sx_done)
        for kc in range(NCH):
            lhs = sx[:, kc, csl].opt()
            mk.pe.op("matmul", psum[b0][:, :], lhs,
                     wtokb[:, kc, 0:512].opt(), start=(kc == 0),
                     stop=(kc == NCH - 1))
            mk.pe.op("matmul", psum[b1][:, :], lhs,
                     wtokb[:, kc, 512:1024].opt(), start=(kc == 0),
                     stop=(kc == NCH - 1),
                     inc=(S["s_pe"] if kc == NCH - 1 else None))
        mmd = S["s_pe"].n
        mk.act.wait(S["s_pe"], mmd)
        mk.act.wait(S["s_pool"], vones)
        mk.act.op("activation", v_t4[:, tc, 0:8, 0:64],
                  psum[b0][:, :].rearrange("p (a b) -> p a b", a=8), AF.Copy)
        mk.act.op("activation", v_t4[:, tc, 8:16, 0:64],
                  psum[b1][:, :].rearrange("p (a b) -> p a b", a=8), AF.Copy,
                  inc=S["s_act"])
        ps_release(b0, S["s_act"])
        ps_release(b1, S["s_act"])
    v_done = S["s_act"].n
    v_pe_end = S["s_pe"].n

    t2s = [t2_st, view(69, 1, BF16)]
    scr_guard = {}

    def sumsq_rows(src_t, width, ready_val, rn_dst, idx0):
        idx = idx0
        for tt in range((width + 511) // 512):
            w0 = tt * 512
            wdt = min(512, width - w0)
            ba = ps_acquire()
            for c in range(NCH):
                slot = idx % 2
                mk.act.wait(S["s_act"], ready_val)
                g = scr_guard.get(slot)
                if g is not None:
                    mk.act.wait(g[0], g[1])
                mk.act.op("activation", t2s[slot][:, 0:wdt],
                          src_t[:, c, w0:w0 + wdt], AF.Square, inc=S["s_act"])
                mk.pe.wait(S["s_act"], S["s_act"].n)
                mk.pe.wait(S["s_xdma"], consts_loaded)
                mk.pe.op("matmul", psum[ba][0:16, 0:wdt], e16[:, c, :].opt(),
                         t2s[slot][:, 0:wdt], start=(c == 0),
                         stop=(c == NCH - 1), inc=S["s_pe"])
                scr_guard[slot] = (S["s_pe"], S["s_pe"].n)
                idx += 1
            mk.dve.wait(S["s_pe"], S["s_pe"].n)
            g = scr_guard.get("rows_war")
            if g is not None:
                mk.dve.wait(g[0], g[1])
            mk.dve.op("tensor_copy", rows16[0:16, w0:w0 + wdt],
                      psum[ba][0:16, 0:wdt], inc=S["s_dve"])
            ps_release(ba, S["s_dve"])
        mk.act.wait(S["s_dve"], S["s_dve"].n)
        mk.act.op("activation", rows16[0:16, 0:width],
                  rows16[0:16, 0:width], AF.Sqrt, inc=S["s_act"])
        mk.dve.wait(S["s_act"], S["s_act"].n)
        mk.dve.op("tensor_scalar", rows16[0:16, 0:width],
                  rows16[0:16, 0:width], 1e-12, None, ALU.max, chain=True)
        mk.dve.op("reciprocal", rows16[0:16, 0:width],
                  rows16[0:16, 0:width], chain=True)
        mk.dve.op("tensor_copy", rn_dst[0:16, 0:width],
                  rows16[0:16, 0:width], chain=True, inc=S["s_dve"])
        scr_guard["rows_war"] = (S["s_dve"], S["s_dve"].n)
        return S["s_dve"].n

    rnq_done = sumsq_rows(q_t, T, q_raw_done, rnq_bf, 0)
    rnk_done = sumsq_rows(k_t, L0F_TE, k_raw_done, rnk_bf, 0)

    mk.sp.wait(S["s_pe"], v_pe_end)
    mk.sp.wait(S["s_pe"], qk_pe_end)
    mk.sp.dma(csq_r[:, 0:T], csq_d[:, 256:], inc=S["s_tb"])
    mk.sp.dma(ssq_r[:, 0:T], ssq_d[:, 256:], inc=S["s_tb"])
    mk.sp.dma(csk_r[:, 0:L0F_TE], csk_d[:, :], inc=S["s_tb"])
    mk.sp.dma(ssk_r[:, 0:L0F_TE], ssk_d[:, :], inc=S["s_tb"])
    tbls_loaded = S["s_tb"].n

    tbl_used = {"n": 0}
    s1s = [t2_st, view(69, 1, BF16)]
    s2s = [view(70, 1, BF16), view(83, 1, BF16)]

    def rot_pass(src_t, width, rn_src, rn_ready, cs_t, ss_t, qs_col, qsp_col,
                 tag):
        nbt = (width + 511) // 512
        idx = 0
        rnb_free[(tag, 0)] = S["s_dve"].n
        for c in range(NCH):
            for tt in range(nbt):
                w0 = tt * 512
                wdt = min(512, width - w0)
                bb = ps_acquire()
                mk.pe.wait(S["s_dve"], rn_ready)
                mk.pe.wait(S["s_xdma"], consts_loaded)
                mk.pe.op("matmul", psum[bb][:, 0:wdt], sel[:, c, :].opt(),
                         rn_src[0:16, w0:w0 + wdt], start=True, stop=True,
                         inc=S["s_pe"])
                mk.act.wait(S["s_pe"], S["s_pe"].n)
                mk.act.wait(S["s_dve"], rnb_free.get((tag, c), 0))
                mk.act.op("activation", rnb_t[:, w0:w0 + wdt],
                          psum[bb][:, 0:wdt], AF.Copy, inc=S["s_act"])
                ps_release(bb, S["s_act"])
            rnb_done = S["s_act"].n
            for tt in range(nbt):
                w0 = tt * 512
                wdt = min(512, width - w0)
                zsl = src_t[:, c, w0:w0 + wdt]
                slot = idx % 2
                s1 = s1s[slot]
                s2 = s2s[slot]
                br = ps_acquire()
                mk.pe.op("matmul", psum[br][:, 0:wdt], rt_t[:, :], zsl,
                         start=True, stop=True, inc=S["s_pe"])
                rot_mm = S["s_pe"].n
                for gk in (("d1", slot), slot):
                    g = scr_guard.get(gk)
                    if g is not None:
                        mk.dve.wait(g[0], g[1])
                mk.dve.wait(S["s_tb"], tbls_loaded)
                mk.dve.op("scalar_tensor_tensor", s1[:, 0:wdt], zsl, qs_col,
                          cs_t[:, w0:w0 + wdt], ALU.mult, ALU.mult,
                          inc=S["s_dve"])
                mk.dve.wait(S["s_pe"], rot_mm)
                mk.dve.op("scalar_tensor_tensor", s2[:, 0:wdt],
                          psum[br][:, 0:wdt], qsp_col,
                          ss_t[:, w0:w0 + wdt], ALU.mult, ALU.mult,
                          chain=True, inc=S["s_dve"])
                ps_release(br, S["s_dve"])
                mk.dve.op("tensor_tensor", s1[:, 0:wdt], s1[:, 0:wdt],
                          s2[:, 0:wdt], ALU.add, chain=True)
                mk.dve.wait(S["s_act"], rnb_done)
                mk.dve.op("tensor_tensor", zsl, s1[:, 0:wdt],
                          rnb_t[:, w0:w0 + wdt], ALU.mult, chain=True,
                          inc=S["s_dve"])
                scr_guard[("d1", slot)] = (S["s_dve"], S["s_dve"].n)
                tbl_used["n"] = S["s_dve"].n
                idx += 1
            rnb_free[(tag, c + 1)] = S["s_dve"].n
        return S["s_dve"].n

    rnb_free = {}
    qrot_done = rot_pass(q_t, T, rnq_bf, rnq_done, csq_r, ssq_r,
                         qs_c[:, 0:1], qs_c[:, 1:2], "q")
    krot_done = rot_pass(k_t, L0F_TE, rnk_bf, rnk_done, csk_r, ssk_r,
                         qs_c[:, 2:3], qs_c[:, 3:4], "k")

    for i_ in range(8):
        mk.sp.dma(mpre_t[:, i_, :].opt(), mpre_d[i_], inc=S["s_xs"])
    masks_loaded = S["s_xs"].n

    units = [(h, qt) for h in range(H) for qt in range(L0F_NQT)]
    NU = len(units)
    for b_ in range(8):
        g_ = ps_guard[b_]
        if g_ is not None:
            mk.pe.wait(g_[0], g_[1])
            ps_guard[b_] = None
    sb_guard = {}
    av_guard = {}
    s_done = {}
    exp_written = {}
    av_done = {}
    avu_done = {}
    recip_done = {}
    bc_done = {}
    rb_evac = {}
    rb_read = {}
    exp_slot_guard = {}

    def emit_S(u, mis):
        h, qt = units[u]
        hp, hr, q0 = h // 2, (h % 2) * 64, qt * 512
        qfs = q_t[hr:hr + 64, hp, q0:q0 + 512].opt()
        if u == 0 and mis[0] == 0:
            mk.pe.wait(S["s_dve"], qrot_done)
            mk.pe.wait(S["s_dve"], krot_done)
            mk.pe.wait(S["s_xs"], masks_loaded)
            mk.pe.wait(S["s_xdma"], consts_loaded)
        for mi in mis:
            m = OFFS[mi]
            kk0 = ((q0 + m + 256) // 128) * 128
            kfs = k_t[hr:hr + 64, hp, kk0:kk0 + 128].opt()
            b = (u * 6 + mi) % 4
            g = sb_guard.get(b)
            if g is not None:
                mk.pe.wait(S["s_act"], g)
            tbl = (6 + mi) if (qt == 0 and mi < 2) else mi
            mk.pe.op("matmul", psum[b][0:128, 0:512], kfs, qfs,
                     start=True, stop=False)
            mk.pe.op("matmul", psum[b][0:128, 0:512], eyet[:, :],
                     mpre_t[:, tbl, :], start=False, stop=True,
                     inc=S["s_pe"])
            s_done[(u, mi)] = S["s_pe"].n

    def emit_E(u, mis):
        h, qt = units[u]
        for mi in mis:
            b = (u * 6 + mi) % 4
            mk.act.wait(S["s_pe"], s_done[(u, mi)])
            if u >= 1:
                mk.act.wait(S["s_pe"], av_done.get((u - 1, mi), 0))
            mk.act.op("activation", smB[:, mi, :], psum[b][0:128, 0:512],
                      AF.Exp, scale=8.0, inc=S["s_act"])
            sb_guard[b] = S["s_act"].n
            exp_written[(u, mi)] = S["s_act"].n

    def emit_AV(u):
        h, qt = units[u]
        q0 = qt * 512
        bAV = 5 + (u % 3)
        g = av_guard.get(bAV)
        if g is not None:
            mk.pe.wait(S["s_dve"], g)
        if u == 0:
            mk.pe.wait(S["s_act"], v_done)
        for mi, m in enumerate(OFFS):
            jc = (q0 + m + 256) // 128
            mk.pe.wait(S["s_act"], exp_written[(u, mi)])
            mk.pe.op("matmul", psum[bAV][0:65, 0:512],
                     v_t4[:, jc, h, :].opt(), smB[:, mi, :],
                     start=(mi == 0), stop=(mi == 5), inc=S["s_pe"])
            av_done[(u, mi)] = S["s_pe"].n
        avu_done[u] = S["s_pe"].n

    def emit_recip(u):
        bAV = 5 + (u % 3)
        mk.dve.wait(S["s_pe"], avu_done[u])
        mk.dve.wait(S["s_pe"], bc_done.get(u - 2, 0))
        mk.dve.op("reciprocal", r_f[:, :], psum[bAV][64:65, 0:512],
                  chain=True)
        mk.dve.op("tensor_copy", r_bfs[u % 2][:, :], r_f[:, :], chain=True,
                  inc=S["s_dve"])
        recip_done[u] = S["s_dve"].n

    def emit_bcast(u):
        mk.pe.wait(S["s_dve"], recip_done[u])
        mk.pe.wait(S["s_act"], rb_evac.get(u - 1, 0))
        mk.pe.op("matmul", psum[4][:, 0:512], ones_row[:, :],
                 r_bfs[u % 2][:, :], start=True, stop=True, inc=S["s_pe"])
        bc_done[u] = S["s_pe"].n

    def emit_divrest(u):
        h, qt = units[u]
        hp, hr, q0 = h // 2, (h % 2) * 64, qt * 512
        bAV = 5 + (u % 3)
        mk.act.wait(S["s_pe"], bc_done[u])
        mk.act.wait(S["s_dve"], rb_read.get(u - 2, 0))
        mk.act.op("activation", rb_dvs[u % 2][:, 0:512], psum[4][:, 0:512],
                  AF.Copy, inc=S["s_act"])
        rb_evac[u] = S["s_act"].n
        mk.dve.wait(S["s_act"], rb_evac[u])
        mk.dve.op("tensor_tensor", attn[hr:hr + 64, hp, q0:q0 + 512],
                  psum[bAV][0:64, 0:512], rb_dvs[u % 2][0:64, 0:512],
                  ALU.mult, inc=S["s_dve"])
        rb_read[u] = S["s_dve"].n
        av_guard[bAV] = S["s_dve"].n

    for u in range(NU):
        emit_S(u, [0, 1, 2, 3])
        if u >= 1:
            emit_AV(u - 1)
        emit_E(u, [0, 1, 2, 3])
        emit_S(u, [4, 5])
        if u >= 2:
            emit_bcast(u - 2)
        emit_E(u, [4, 5])
        if u >= 1:
            emit_recip(u - 1)
        if u >= 2:
            emit_divrest(u - 2)
    emit_AV(NU - 1)
    emit_bcast(NU - 2)
    emit_recip(NU - 1)
    emit_divrest(NU - 2)
    emit_bcast(NU - 1)
    emit_divrest(NU - 1)
    attn_done = S["s_dve"].n
    for i_ in range(8):
        ps_guard[i_] = (S["s_dve"], attn_done)

    x_sems = [S["g0"], S["g1"]]
    x_loaded = {}
    x_freed = {}

    slot_store = {}

    def x_reload(oc, gate=None):
        b = oc % 2
        mk.sp.wait(S["s_dve"], attn_done)
        mk.sp.wait(S["s_odma"], S["s_odma"].n)
        if gate is not None:
            mk.sp.wait(S["s_dve"], gate)
        mk.sp.dma(xrl[b][:, 0:T], xh_d[oc * PART:(oc + 1) * PART, 258:],
                  inc=x_sems[b])
        x_loaded[oc] = (x_sems[b], x_sems[b].n)

    x_reload(0)
    x_reload(1)
    for slab in range(4):
        mk.pool.wait(S["s_dve"], attn_done)
        seq = ws.fetch(wo_d[slab].rearrange("p c o -> p (c o)"),
                       gate=ws.gates.get(ws.count - 2))
        ws.wait_pe(seq)
        mk.pe.wait(S["s_dve"], attn_done)
        buf = ws.buf(seq).rearrange("p (c o) -> p c o", c=8)
        for m in range(2):
            oc = slab * 2 + m
            for tt in range(4):
                tsl = slice(tt * 512, (tt + 1) * 512)
                b0 = ps_acquire()
                for kc in range(NCH):
                    mk.pe.op("matmul", psum[b0][:, :],
                             buf[:, kc, m * 128:(m + 1) * 128].opt(),
                             attn[:, kc, tsl], start=(kc == 0),
                             stop=(kc == NCH - 1),
                             inc=(S["s_pe"] if kc == NCH - 1 else None))
                mmd = S["s_pe"].n
                mk.dve.wait(S["s_pe"], mmd)
                mk.dve.wait(*x_loaded[oc])
                mk.dve.op("tensor_tensor", xrl[oc % 2][:, tsl],
                          psum[b0][:, :], xrl[oc % 2][:, tsl], ALU.add,
                          inc=S["s_dve"])
                ps_release(b0, S["s_dve"])
            ydone = S["s_dve"].n
            x_freed[oc] = ydone
            mk.sp.wait(S["s_dve"], ydone)
            mk.sp.dma(y_d[oc * PART:(oc + 1) * PART, :], xrl[oc % 2][:, 0:T],
                      inc=S["s_odma"])
            if tail is not None:
                tap, tn = tail
                mk.sp.dma(tap[oc * PART:(oc + 1) * PART, 0:tn],
                          xrl[oc % 2][:, T - tn:T], inc=S["s_odma"])
            slot_store[oc % 2] = S["s_odma"].n
            if oc + 2 < NCH:
                x_reload(oc + 2, gate=x_freed[oc])
        ws.gates[seq] = S["s_pe"].n
    mk.sp.wait(S["s_odma"], S["s_odma"].n)
    mk.sp.flush()
    mk.pe.flush(); mk.act.flush(); mk.dve.flush(); mk.pool.flush()


def emit_l0f_ffn(mk, x_d, hc_d, y_d, tail=None):
    S = mk.std_sems(extra=["wb0", "wb1", "g0", "g1", "s_xs", "wtk"])
    HCH = 44
    KCH2 = 22

    w1_d = mk.din("w1d", [11, PART, 8, 512])
    w2_d = mk.din("w2d", [8, PART, KCH2, 128])
    b1_d = mk.din("b1d", [PART, HCH])
    b2_d = mk.din("b2d", [PART, 8])
    g_d = mk.din("gd", [PART, 8])
    bln_d = mk.din("blnd", [PART, 8])
    eps_d = mk.din("c_eps", [PART, 1])
    onc_d = mk.din("c_ones_col", [PART, 1], BF16)
    onr_d = mk.din("c_ones_row", [1, PART], BF16)

    arena = mk.sbuf("arena", [PART, 196 * KB], mybir.dt.uint8)

    def view(off_kb, kb, dtype, shape=None):
        ap = arena[:, int(off_kb * KB):int((off_kb + kb) * KB)].bitcast(dtype)
        if shape is None:
            return ap
        return ap.rearrange("p (a b) -> p a b", a=shape[0])

    sxf = view(0, 32, BF16, [NCH, T])
    xn = view(32, 32, BF16, [NCH, T])
    hg = view(64, 88, BF16, [KCH2, T])
    lnscr_b = view(64, 8, BF16, [NCH, 512])
    lnscr_sq = view(72, 8, BF16, [NCH, 512])
    lnscr2_b = view(80, 8, BF16, [NCH, 512])
    lnscr2_sq = view(88, 8, BF16, [NCH, 512])
    mubc = view(152, 4, BF16)
    rbc = view(156, 4, BF16)
    tmp2 = view(160, 8, F32, [2, 1024])
    xrl = [view(168, 8, F32), view(176, 8, F32)]
    a_tmps = [view(184, 4, BF16), view(188, 4, BF16)]
    wbufs = [view(0, 8, BF16), view(8, 8, BF16)]

    mu_f = mk.sbuf("mu_f", [1, 512], F32)
    tA = mk.sbuf("tA", [1, 512], F32)
    tB = mk.sbuf("tB", [1, 512], F32)
    r_f = mk.sbuf("r_f", [1, 512], F32)
    mu_bf = mk.sbuf("mu_bf", [1, 512], BF16)
    r_bf = mk.sbuf("r_bf", [1, 512], BF16)
    ones_col = mk.sbuf("ones_col", [PART, 1], BF16)
    ones_row = mk.sbuf("ones_row", [1, PART], BF16)
    epsc = mk.sbuf("epsc", [PART, 1], F32)
    b1c = mk.sbuf("b1c", [PART, HCH], F32)
    g_c = mk.sbuf("g_c", [PART, 8], F32)
    bln_c = mk.sbuf("bln_c", [PART, 8], F32)
    b2_c = mk.sbuf("b2_c", [PART, 8], F32)

    pp = PsumPool8(mk)
    psum = pp.banks
    lnt = LNTiles(mu_f, tA, tB, r_f, mu_bf, r_bf, mubc, rbc)
    ws = WStream(mk, S, "w", wbufs, [S["wb0"], S["wb1"]])

    for t_, d_ in [(ones_col, onc_d), (ones_row, onr_d), (epsc, eps_d),
                   (b1c, b1_d), (g_c, g_d), (bln_c, bln_d),
                   (b2_c, b2_d)]:
        mk.sp.dma(t_[:, :], d_[:, :], inc=S["s_xdma"])
    consts_loaded = S["s_xdma"].n
    b2_loaded = consts_loaded
    x_freed = {}
    st_gate = {}
    slot_store = {}

    for c in range(NCH):
        if c < 4:
            mk.pool.dma(sxf[:, c, :], x_d[c * PART:(c + 1) * PART, 0:T],
                        inc=S["wtk"])
        else:
            mk.pool.dma(sxf[:, c, 0:1], hc_d[c * PART:(c + 1) * PART, 0:1],
                        inc=S["wtk"])
            mk.pool.dma(sxf[:, c, 1:T], x_d[c * PART:(c + 1) * PART, 0:T - 1],
                        inc=S["wtk"])
    sx_done = S["wtk"].n

    bc = emit_ln_stats(mk, S, pp, lnt, sxf, [lnscr_b, lnscr2_b],
                       [lnscr_sq, lnscr2_sq], NCH, T,
                       ones_col, ones_row, epsc,
                       ready=[(S["wtk"], sx_done),
                              (S["s_xdma"], consts_loaded)])
    xn_done = emit_ln_apply(mk, S, lnt, sxf, tmp2, bc, NCH, dst_bf=xn,
                            gamma=g_c, beta=bln_c, W=T)

    at_guard = {}
    hg_parts = {}
    for slab in range(11):
        seq = ws.fetch(w1_d[slab].rearrange("p c o -> p (c o)"),
                       gate=ws.gates.get(ws.count - 2))
        ws.wait_pe(seq)
        mk.pe.wait(S["s_dve"], xn_done)
        buf = ws.buf(seq).rearrange("p (c o) -> p c o", c=8)
        for pr in range(2):
            k = slab * 2 + pr
            at = a_tmps[k % 2]
            a_done = {}
            for half, fn in ((0, None), (1, None)):
                for tt in range(4):
                    w0 = tt * 512
                    b0 = pp.acquire()
                    for kc in range(NCH):
                        mk.pe.op("matmul", psum[b0][:, :],
                                 buf[:, kc,
                                     (2 * pr + half) * 128:
                                     (2 * pr + half + 1) * 128].opt(),
                                 xn[:, kc, w0:w0 + 512],
                                 start=(kc == 0), stop=(kc == NCH - 1),
                                 inc=(S["s_pe"] if kc == NCH - 1 else None))
                    mk.act.wait(S["s_pe"], S["s_pe"].n)
                    if half == 0:
                        if k >= 2:
                            mk.act.wait(S["s_dve"], at_guard.get(k - 2, 0))
                        mk.act.op("activation", at[:, w0:w0 + 512],
                                  psum[b0][:, :], AF.Identity,
                                  bias=b1c[:, 2 * k:2 * k + 1],
                                  inc=S["s_act"])
                    else:
                        mk.act.op("activation", hg[:, k, w0:w0 + 512],
                                  psum[b0][:, :], AF.Gelu,
                                  bias=b1c[:, 2 * k + 1:2 * k + 2],
                                  inc=S["s_act"])
                    pp.release(b0, S["s_act"])
                    if half == 0:
                        a_done[tt] = S["s_act"].n
            mk.dve.wait(S["s_act"], S["s_act"].n)
            mk.dve.op("tensor_tensor", hg[:, k, :], hg[:, k, :],
                      at[:, 0:T], ALU.mult, inc=S["s_dve"])
            at_guard[k] = S["s_dve"].n
            hg_parts[k] = S["s_dve"].n
        ws.gates[seq] = S["s_pe"].n
    hg_done = S["s_dve"].n

    for slab in range(8):
        seq = ws.fetch(w2_d[slab].rearrange("p c o -> p (c o)"),
                       gate=ws.gates.get(ws.count - 2))
        ws.wait_pe(seq)
        mk.pe.wait(S["s_dve"], hg_done)
        oc = slab
        buf = ws.buf(seq)[:, 0:KCH2 * 128].rearrange(
            "p (c o) -> p c o", c=KCH2)
        b_ = oc % 2
        mk.sp.wait(S["s_dve"], x_freed.get(oc - 2, 0))
        mk.sp.wait(S["s_odma"], S["s_odma"].n)
        mk.sp.wait(S["s_dve"], hg_done)
        mk.sp.dma(xrl[b_][:, 0:T], x_d[oc * PART:(oc + 1) * PART, 0:T],
                  inc=S["g0" if b_ == 0 else "g1"])
        xl_sem = S["g0" if b_ == 0 else "g1"]
        xl_n = xl_sem.n
        for tt in range(4):
            w0 = tt * 512
            b0 = pp.acquire()
            for kc in range(KCH2):
                mk.pe.op("matmul", psum[b0][:, :],
                         buf[:, kc, :].opt(),
                         hg[:, kc, w0:w0 + 512], start=(kc == 0),
                         stop=(kc == KCH2 - 1),
                         inc=(S["s_pe"] if kc == KCH2 - 1 else None))
            mmd = S["s_pe"].n
            mk.dve.wait(S["s_pe"], mmd)
            mk.dve.wait(xl_sem, xl_n)
            mk.dve.wait(S["s_xdma"], b2_loaded)
            mk.dve.op("scalar_tensor_tensor",
                      xrl[b_][:, w0:w0 + 512],
                      psum[b0][:, :], b2_c[:, oc:oc + 1],
                      xrl[b_][:, w0:w0 + 512],
                      ALU.add, ALU.add, inc=S["s_dve"])
            pp.release(b0, S["s_dve"])
        ydone = S["s_dve"].n
        x_freed[oc] = ydone
        mk.sp.wait(S["s_dve"], ydone)
        mk.sp.dma(y_d[oc * PART:(oc + 1) * PART, 0:T], xrl[b_][:, 0:T],
                  inc=S["s_odma"])
        if tail is not None:
            tap, tn = tail
            mk.sp.dma(tap[oc * PART:(oc + 1) * PART, 0:tn],
                      xrl[b_][:, T - tn:T], inc=S["s_odma"])
        slot_store[b_] = S["s_odma"].n
        ws.gates[seq] = S["s_pe"].n

    mk.sp.wait(S["s_odma"], S["s_odma"].n)
    mk.sp.flush()
    mk.pe.flush(); mk.act.flush(); mk.dve.flush(); mk.pool.flush()



def emit_lnst(mk, residual, x_d, xr_d, y_d, tail=None):
    S = mk.std_sems(extra=["g0", "g1", "s_xs"])
    eps_d = mk.din("c_eps", [PART, 1])
    onc_d = mk.din("c_ones_col", [PART, 1], BF16)
    onr_d = mk.din("c_ones_row", [1, PART], BF16)

    arena = mk.sbuf("arena", [PART, 140 * KB], mybir.dt.uint8)

    def view(off_kb, kb, dtype, shape=None):
        ap = arena[:, int(off_kb * KB):int((off_kb + kb) * KB)].bitcast(dtype)
        if shape is None:
            return ap
        return ap.rearrange("p (a b) -> p a b", a=shape[0])

    xt = view(0, 64, F32, [NCH, T])
    lnscr_b = view(64, 8, BF16, [NCH, 512])
    lnscr_sq = view(72, 8, BF16, [NCH, 512])
    lnscr2_b = view(112, 8, BF16, [NCH, 512])
    lnscr2_sq = view(120, 8, BF16, [NCH, 512])
    mubc = view(80, 4, BF16)
    rbc = view(84, 4, BF16)
    tmp2 = view(88, 8, F32, [2, 1024])
    xr_rl = [view(96, 8, F32), view(104, 8, F32)]

    mu_f = mk.sbuf("mu_f", [1, 512], F32)
    tA = mk.sbuf("tA", [1, 512], F32)
    tB = mk.sbuf("tB", [1, 512], F32)
    r_f = mk.sbuf("r_f", [1, 512], F32)
    mu_bf = mk.sbuf("mu_bf", [1, 512], BF16)
    r_bf = mk.sbuf("r_bf", [1, 512], BF16)
    ones_col = mk.sbuf("ones_col", [PART, 1], BF16)
    ones_row = mk.sbuf("ones_row", [1, PART], BF16)
    epsc = mk.sbuf("epsc", [PART, 1], F32)

    pp = PsumPool8(mk)
    lnt = LNTiles(mu_f, tA, tB, r_f, mu_bf, r_bf, mubc, rbc)

    for t_, d_ in [(ones_col, onc_d), (ones_row, onr_d), (epsc, eps_d)]:
        mk.sp.dma(t_[:, :], d_[:, :], inc=S["s_xdma"])
    for c in range(NCH):
        mk.sp.dma(xt[:, c, :], x_d[c * PART:(c + 1) * PART, :], inc=S["s_xs"])
    x_loaded = S["s_xs"].n

    bc = emit_ln_stats(mk, S, pp, lnt, xt, [lnscr_b, lnscr2_b],
                       [lnscr_sq, lnscr2_sq], NCH, T,
                       ones_col, ones_row, epsc,
                       ready=[(S["s_xs"], x_loaded),
                              (S["s_xdma"], S["s_xdma"].n)])
    mk.pool.wait(S["s_act"], bc)
    x2_sems = [S["g0"], S["g1"]]
    x2_loaded = {}
    x2_freed = {}
    if residual:
        def xr_load(c, gate=None):
            b = c % 2
            mk.sp.wait(S["s_act"], bc)
            if gate is not None:
                mk.sp.wait(S["s_dve"], gate)
            mk.sp.dma(xr_rl[b][:, 0:T], xr_d[c * PART:(c + 1) * PART, :],
                      inc=x2_sems[b])
            x2_loaded[c] = (x2_sems[b], x2_sems[b].n)
        xr_load(0)
        xr_load(1)
    tmp_use = {0: 0, 1: 0}
    for c in range(NCH):
        b = c % 2
        for hf in range(2):
            hsl = slice(hf * 1024, (hf + 1) * 1024)
            mk.pool.wait(S["s_dve"], tmp_use[hf])
            mk.pool.op("tensor_tensor", tmp2[:, hf, :], xt[:, c, hsl],
                       mubc[:, hsl], ALU.subtract, inc=S["s_pool"])
            mk.dve.wait(S["s_pool"], S["s_pool"].n)
            if residual:
                mk.dve.wait(*x2_loaded[c])
            mk.dve.op("tensor_tensor", tmp2[:, hf, :], tmp2[:, hf, :],
                      rbc[:, hsl], ALU.mult)
            if residual:
                mk.dve.op("tensor_tensor", xt[:, c, hsl], xr_rl[b][:, hsl],
                          tmp2[:, hf, :], ALU.add, chain=True, inc=S["s_dve"])
            else:
                mk.dve.op("tensor_copy", xt[:, c, hsl], tmp2[:, hf, :],
                          chain=True, inc=S["s_dve"])
            tmp_use[hf] = S["s_dve"].n
        yc = S["s_dve"].n
        x2_freed[c] = yc
        mk.sp.wait(S["s_dve"], yc)
        mk.sp.dma(y_d[c * PART:(c + 1) * PART, :], xt[:, c, :],
                  inc=S["s_odma"])
        if tail is not None:
            tap, tn = tail
            mk.sp.dma(tap[c * PART:(c + 1) * PART, 0:tn],
                      xt[:, c, T - tn:T], inc=S["s_odma"])
        if residual and c + 2 < NCH:
            xr_load(c + 2, gate=x2_freed[c])
    mk.sp.wait(S["s_odma"], S["s_odma"].n)
    mk.sp.flush()
    mk.pe.flush(); mk.act.flush(); mk.dve.flush(); mk.pool.flush()



def make_slabs(w, kcs, ow):
    K, N = w.shape
    n_slab = N // ow
    return np.ascontiguousarray(
        w.reshape(kcs, 128, n_slab, ow).transpose(2, 1, 0, 3))


def rotary_tables_np(n=4096):
    half = np.arange(0, 64, 2, dtype=np.float64)
    inv_freq = 1.0 / 10000.0 ** (half / 64)
    t = np.arange(n, dtype=np.float64)
    freqs = t[:, None] * inv_freq[None, :]
    freqs = np.concatenate([freqs, freqs], -1)
    base_scale = (half + 0.4 * 64) / (1.4 * 64)
    power = (t - n // 2) / 128
    scale = base_scale[None, :] ** power[:, None]
    scale = np.concatenate([scale, scale], -1)
    return freqs, scale


def attn_host_prep(x_ext, Wqkv, qs, ks, Wo, h):
    import ml_dtypes
    d = {}
    d["xh"] = np.ascontiguousarray(x_ext.T)
    d["wqd"] = make_slabs(np.ascontiguousarray(Wqkv[:, 0:1024]), 8, 256)
    d["wkd"] = make_slabs(np.ascontiguousarray(Wqkv[:, 1024:2048]), 8, 256)
    d["wvtokd"] = np.ascontiguousarray(Wqkv[:, 2048:3072].reshape(8, 128, 1024))
    d["wod"] = make_slabs(np.asarray(Wo), 8, 256)
    freqs, scale = rotary_tables_np()
    g0 = h * 2048 - 256
    pos = np.arange(g0, g0 + L0F_TE)
    posc = np.clip(pos, 0, 4095)
    cs = np.cos(freqs[posc])
    sn = np.sin(freqs[posc])
    sc = scale[posc]
    csq = (cs * sc).T
    ssq = (sn * sc).T
    csk = (cs / sc).T
    ssk = (sn / sc).T
    tile2 = lambda a: np.ascontiguousarray(
        np.concatenate([a, a], 0)).astype(ml_dtypes.bfloat16)
    d["csqd"] = tile2(csq)
    d["ssqd"] = tile2(ssq)
    d["cskd"] = tile2(csk)
    d["sskd"] = tile2(ssk)
    qs_col = np.tile(qs, 2)
    ks_col = np.tile(ks, 2)
    qsp = np.tile(np.concatenate([qs[32:64], qs[0:32]]), 2)
    ksp = np.tile(np.concatenate([ks[32:64], ks[0:32]]), 2)
    d["qsd"] = np.stack([qs_col, qsp, ks_col, ksp], 1).astype(np.float32)
    R = np.zeros((128, 128), np.float64)
    for hh in range(2):
        o = hh * 64
        for dd in range(32):
            R[o + dd, o + dd + 32] = -1.0
            R[o + dd + 32, o + dd] = 1.0
    d["c_rt"] = np.ascontiguousarray(R.T).astype(ml_dtypes.bfloat16)
    e16 = np.zeros((128, 8, 16), np.float64)
    sel = np.zeros((16, 8, 128), np.float64)
    for c in range(8):
        for j in range(2):
            hh = c * 2 + j
            e16[j * 64:(j + 1) * 64, c, hh] = 1.0
            sel[hh, c, j * 64:(j + 1) * 64] = 1.0
    d["c_e16"] = e16.reshape(128, 128).astype(ml_dtypes.bfloat16)
    d["c_sel"] = sel.reshape(16, 1024).astype(ml_dtypes.bfloat16)
    ji = np.arange(128)[:, None]
    ti = np.arange(512)[None, :]
    mpre = np.zeros((8, 128, 512), np.float32)
    for i, m in enumerate(OFFS):
        dd = ti - m - ji
        bad = (dd < 0) | (dd > 256)
        mpre[i][bad] = -1e9 / 8
    for i in range(2):
        if h == 0:
            mpre[6 + i][:] = -1e9 / 8
        else:
            mpre[6 + i] = mpre[i]
    d["c_mpre"] = mpre.astype(ml_dtypes.bfloat16)
    d["c_eye"] = np.eye(128).astype(ml_dtypes.bfloat16)
    d["c_ones_col"] = np.ones((128, 1), ml_dtypes.bfloat16)
    d["c_ones_row"] = np.ones((1, 128), ml_dtypes.bfloat16)
    return d


def ffn_host_prep(x_ext1, g, bln, W1, b1, W2, b2):
    import ml_dtypes
    d = {}
    d["xh"] = np.ascontiguousarray(x_ext1.T)
    a_half = np.zeros((1024, 2816), np.float32)
    a_half[:, 0:2730] = W1[:, 0:2730]
    g_half = np.zeros((1024, 2816), np.float32)
    g_half[:, 0:2730] = W1[:, 2730:5460]
    ba = np.zeros(2816, np.float32)
    ba[0:2730] = b1[0:2730]
    bg = np.zeros(2816, np.float32)
    bg[0:2730] = b1[2730:5460]
    W1p = np.zeros((1024, 5632), np.float32)
    b1p = np.zeros(5632, np.float32)
    for k in range(22):
        W1p[:, 256 * k:256 * k + 128] = a_half[:, 128 * k:128 * k + 128]
        W1p[:, 256 * k + 128:256 * k + 256] = g_half[:, 128 * k:128 * k + 128]
        b1p[256 * k:256 * k + 128] = ba[128 * k:128 * k + 128]
        b1p[256 * k + 128:256 * k + 256] = bg[128 * k:128 * k + 128]
    W2p = np.zeros((2816, 1024), np.float32)
    W2p[0:2730] = W2
    d["w1d"] = make_slabs(W1p, 8, 512)
    d["w2d"] = make_slabs(W2p, 22, 128)
    d["b1d"] = np.ascontiguousarray(b1p.reshape(44, 128).T)
    d["b2d"] = np.ascontiguousarray(np.asarray(b2).reshape(8, 128).T)
    d["gd"] = np.ascontiguousarray(np.asarray(g).reshape(8, 128).T)
    d["blnd"] = np.ascontiguousarray(np.asarray(bln).reshape(8, 128).T)
    d["c_eps"] = np.full((128, 1), 1e-5, np.float32)
    d["c_ones_col"] = np.ones((128, 1), ml_dtypes.bfloat16)
    d["c_ones_row"] = np.ones((1, 128), ml_dtypes.bfloat16)
    return d


def ln_host_prep(x_core, xr_core=None):
    import ml_dtypes
    d = {"x": np.ascontiguousarray(x_core.T)}
    if xr_core is not None:
        d["xr"] = np.ascontiguousarray(xr_core.T)
    d["c_eps"] = np.full((128, 1), 1e-5, np.float32)
    d["c_ones_col"] = np.ones((128, 1), ml_dtypes.bfloat16)
    d["c_ones_row"] = np.ones((1, 128), ml_dtypes.bfloat16)
    return d





from concourse.bass_utils import run_bass_kernel_spmd


def emit_halo(mk, ncols, hsrc, hdst, dst_ap, flag_d, groups):
    S = mk.std_sems(extra=["rcv", "cc2"])
    arena = mk.sbuf("arena", [PART, 12 * KB], mybir.dt.uint8)
    hst = arena[:, 0:NCH * ncols * 4].bitcast(F32).rearrange(
        "p (a b) -> p a b", a=NCH)
    flagc = mk.sbuf("flagc", [PART, 1], F32)
    mk.sp.dma(flagc[:, :], flag_d[:, :], inc=S["s_xdma"])
    mk.pool.op("collective_compute", "AllGather", ALU.bypass,
               replica_groups=groups, ins=[hsrc[:, :].opt()],
               outs=[hdst[:, :].opt()], inc=S["cc2"], inc_by=1,
               inc_default=True)
    mk.pool.wait(S["cc2"], S["cc2"].n)
    for c in range(NCH):
        mk.pool.dma(hst[:, c, :], hdst[c * PART:(c + 1) * PART, :],
                    inc=S["rcv"])
    mk.pool.wait(S["rcv"], S["rcv"].n)
    mk.pool.wait(S["s_xdma"], S["s_xdma"].n)
    for c in range(NCH):
        mk.pool.op("tensor_scalar", hst[:, c, :], hst[:, c, :],
                   flagc[:, 0:1], None, ALU.mult, chain=True,
                   inc=(S["s_pool"] if c == NCH - 1 else None))
    mk.sp.wait(S["s_pool"], S["s_pool"].n)
    for c in range(NCH):
        mk.sp.dma(dst_ap[c * PART:(c + 1) * PART, :], hst[:, c, :],
                  inc=S["s_odma"])
    mk.sp.wait(S["s_odma"], S["s_odma"].n)


ALL_EXTRA = ["wb0", "wb1", "wtk", "wtk2", "wtk3", "wtk4", "cc1", "cc2",
             "cc3", "f0", "f1", "g0", "g1", "rcv", "s_spill", "s_iota",
             "s_xs", "s_tb", "s_xa"]


def build_fused(n_pairs=4):
    mk = MK("fused", fused=True)
    groups = [[2 * i, 2 * i + 1] for i in range(n_pairs)]

    x_in = mk.din("x", [D, T])
    y_out = mk.dout("y", [D, T])
    flag_d = mk.din("g_flag", [PART, 1])
    xbuf1 = mk.dtmp("xbuf1", [D, T])
    xpost = mk.dtmp("xpost", [D, T])
    x2b = mk.dtmp("x2b", [D, T])
    x3b = mk.dtmp("x3b", [D, T])
    xha = [mk.dtmp(f"xha{i}", [D, 258 + T]) for i in range(6)]
    xfm = [mk.dtmp(f"xfm{i}", [D, T]) for i in range(6)]
    hcol = [mk.dtmp(f"hcol{i}", [D, 1]) for i in range(6)]
    hsrcA = [mk.dtmp(f"hsrcA{i}", [D, 258]) for i in range(6)]
    hdstA = [mk.dtmp(f"hdstA{i}", [2 * D, 258]) for i in range(6)]
    hsrcF = [mk.dtmp(f"hsrcF{i}", [D, 1]) for i in range(6)]
    hdstF = [mk.dtmp(f"hdstF{i}", [2 * D, 1]) for i in range(6)]
    mk.std_sems(extra=ALL_EXTRA)

    mk.new_block("b0_")
    emit_l0attn(mk, x_in, xbuf1, n_pairs)
    mk.barrier()
    mk.new_block("b1_")
    emit_lnst(mk, False, xbuf1, None, xha[0][:, 258:], tail=(hsrcA[0], 258))
    mk.barrier()
    for li in range(6):
        mk.new_block(f"ha{li}_")
        emit_halo(mk, 258, hsrcA[li], hdstA[li], xha[li][:, 0:258], flag_d,
                  groups)
        mk.barrier()
        mk.new_block(f"a{li}_")
        emit_l0f_attn(mk, xha[li], xfm[li], tail=(hsrcF[li], 1))
        mk.barrier()
        mk.new_block(f"hf{li}_")
        emit_halo(mk, 1, hsrcF[li], hdstF[li], hcol[li], flag_d, groups)
        mk.barrier()
        mk.new_block(f"f{li}_")
        emit_l0f_ffn(mk, xfm[li], hcol[li],
                     xha[li + 1][:, 258:] if li < 5 else xpost,
                     tail=((hsrcA[li + 1], 258) if li < 5 else None))
        mk.barrier()
    mk.new_block("b2_")
    emit_lnst(mk, True, xpost, xbuf1, x2b)
    mk.barrier()
    mk.new_block("b3_")
    emit_l1attn(mk, x2b, x3b, n_pairs)
    mk.barrier()
    mk.new_block("b4_")
    emit_l1ffn(mk, x3b, y_out)
    return mk


def _fused_base(inp, h):
    dummy = np.zeros((1, 1), np.float32)

    def put(d, pfx, dd, drop=()):
        for k, v in dd.items():
            if k in drop:
                continue
            d[pfx + k] = v
    d = {"g_flag": np.full((128, 1), float(h), np.float32)}
    put(d, "b0_", host_prep_l0attn(dummy, inp, h), drop=("x",))
    put(d, "b1_", ln_host_prep(dummy), drop=("x",))
    for li in range(6):
        put(d, f"a{li}_", attn_host_prep(
            dummy, inp["L0_ffn_Wqkv"][li], inp["L0_ffn_q_scale"][li],
            inp["L0_ffn_k_scale"][li], inp["L0_ffn_Wo"][li], h),
            drop=("xh",))
        put(d, f"f{li}_", ffn_host_prep(
            dummy, inp["L0_ffn_ln_g"][li], inp["L0_ffn_ln_b"][li],
            inp["L0_ffn_W1"][li], inp["L0_ffn_b1"][li],
            inp["L0_ffn_W2"][li], inp["L0_ffn_b2"][li]), drop=("xh",))
    put(d, "b2_", ln_host_prep(dummy, dummy), drop=("x", "xr"))
    put(d, "b3_", host_prep_l1attn(dummy, inp, h), drop=("x",))
    put(d, "b4_", host_prep_l1ffn(dummy, inp["L1_ffn_w1"],
                                  inp["L1_ffn_b1"], inp["L1_ffn_w2"],
                                  inp["L1_ffn_b2"]), drop=("x",))
    return d


_prog_cache = {}


def _get(name, fn):
    if name not in _prog_cache:
        mkk = fn()
        _prog_cache[name] = (mkk, mkk.finalize())
    return _prog_cache[name][1]


def kernel(**inp):
    inp = {k: np.asarray(v) for k, v in inp.items()}
    nc = _get("fused", lambda: build_fused(4))
    x0 = inp["L0_attn_x"]
    base = {h: _fused_base(inp, h) for h in (0, 1)}
    datas = []
    for c in range(8):
        b, h = c // 2, c % 2
        d = dict(base[h])
        d["x"] = np.ascontiguousarray(
            x0[b, h * 2048:(h + 1) * 2048].astype(np.float32).T)
        datas.append(d)
    res = run_bass_kernel_spmd(nc, datas, core_ids=list(range(8)))
    out = np.zeros((4, 4096, 1024), np.float32)
    for c in range(8):
        b, h = c // 2, c % 2
        out[b, h * 2048:(h + 1) * 2048] = np.asarray(res.results[c]["y"]).T
    return out
